# version 09
# baseline (speedup 1.0000x reference)
import math
from functools import partial

import numpy as np
import concourse.bass as bass
import concourse.mybir as mybir
from concourse.bass_utils import run_bass_kernel_spmd

F32 = mybir.dt.float32
BF16 = mybir.dt.bfloat16
AF = mybir.ActivationFunctionType
ALU = mybir.AluOpType

NCORES = 8
TPC = 1024
D = 2048
KC = 16
ICH = 64
NST = 32
CH = 128
NCH_S = TPC // CH
EPS_LN = 1e-5
TB = 512


def _wait(e, sem, v):
    e.wait_ge(sem, v)


def _mm(e, out, l, r, st, sp, sem=None, n=1):
    i = e.matmul(out, l, r, start=st, stop=sp)
    if sem is not None:
        i.then_inc(sem, n)


def _dma(e, out, in_, sem=None, n=16, slow=False):
    kw = {}
    if slow:
        kw["allow_slow_non_contiguous"] = True
    i = e.dma_start(out=out, in_=in_, **kw)
    if sem is not None:
        i.then_inc(sem, n)


def _tt(e, out, a, b, op, sem=None):
    i = e.tensor_tensor(out, a, b, op)
    if sem is not None:
        i.then_inc(sem, 1)


def _ts(e, out, a, s1, s2, op0, op1, sem=None):
    if s2 is None:
        s2, op1 = 0.0, ALU.add
    i = e.tensor_scalar(out, a, s1, s2, op0, op1)
    if sem is not None:
        i.then_inc(sem, 1)


def _stt(e, out, in0, sc, in1, op0, op1, accum=None, sem=None):
    i = e.scalar_tensor_tensor(out, in0, sc, in1, op0, op1, accum_out=accum)
    if sem is not None:
        i.then_inc(sem, 1)


def _act(e, out, in_, func, bias=None, scale=None, sem=None):
    kw = {}
    if bias is not None:
        kw["bias"] = bias
    if scale is not None:
        kw["scale"] = scale
    i = e.activation(out, in_, func, **kw)
    if sem is not None:
        i.then_inc(sem, 1)


def _scan(e, out, d0, d1, init, op0, op1, sem=None):
    i = e.tensor_tensor_scan(out, d0, d1, init, op0, op1)
    if sem is not None:
        i.then_inc(sem, 1)


def _copy(e, out, in_, sem=None):
    i = e.tensor_copy(out, in_)
    if sem is not None:
        i.then_inc(sem, 1)


def _recip(e, out, in_, sem=None):
    i = e.reciprocal(out, in_)
    if sem is not None:
        i.then_inc(sem, 1)


def _memset(e, ap, v, sem=None):
    i = e.memset(ap, v)
    if sem is not None:
        i.then_inc(sem, 1)


def _cc(e, cin, cout, sem):
    e.collective_compute(
        "AllGather", ALU.bypass,
        replica_groups=[list(range(NCORES))],
        ins=[cin], outs=[cout],
    ).then_inc(sem, 1)


STAGE = 5
FFN_NQ = 4
FFN_NPAIR = 8
FFN_M2 = True
DEBUG_SSM = False
STREAM_STORE = True
PERF_NOCC = False
PREP_AFTER_LN1 = True
PREP_SKIP_VEC = False
PERF_NOEPI = False
PERF_NODMA = False


def build_nc():
    nc = bass.Bass("TRN2", target_bir_lowering=False)

    d_u = nc.dram_tensor("u", [D, TPC], F32, kind="ExternalInput")
    d_wgu = nc.dram_tensor("wgu", [D, 8 * D], F32, kind="ExternalInput")
    d_wd = nc.dram_tensor("wd", [4 * D, D], F32, kind="ExternalInput")
    d_bre = nc.dram_tensor("bre", [D, NST], F32, kind="ExternalInput")
    d_bim = nc.dram_tensor("bim", [D, NST], F32, kind="ExternalInput")
    d_cre = nc.dram_tensor("cre", [NST, D], F32, kind="ExternalInput")
    d_cim = nc.dram_tensor("cim", [NST, D], F32, kind="ExternalInput")
    d_dv = nc.dram_tensor("dvec", [KC, 128], F32, kind="ExternalInput")
    d_lre = nc.dram_tensor("lre", [NST, 1], F32, kind="ExternalInput")
    d_lim = nc.dram_tensor("lim", [NST, 1], F32, kind="ExternalInput")
    d_ldt = nc.dram_tensor("ldt", [NST, 1], F32, kind="ExternalInput")
    d_sel = nc.dram_tensor("selm", [NST, 24], F32, kind="ExternalInput")
    d_y = nc.dram_tensor("y", [D, TPC], F32, kind="ExternalOutput")
    d_ccin = nc.dram_tensor("cc_in", [2 * NST, 1], F32)
    d_ccout = nc.dram_tensor("cc_out", [NCORES * 2 * NST, 1], F32)
    d_wgub = nc.dram_tensor("wgub", [128, 64 * 4096], BF16)
    d_wdb = nc.dram_tensor("wdb", [128, 64 * 2048], BF16)

    u_r = d_u.ap().rearrange("(k p) t -> p k t", p=128)
    y_r = d_y.ap().rearrange("(k p) t -> p k t", p=128)
    wgu_r = d_wgu.ap().rearrange("(k p) o -> p k o", p=128)
    wd_r = d_wd.ap().rearrange("(i p) d -> p i d", p=128)
    bre_r = d_bre.ap().rearrange("(k p) n -> p k n", p=128)
    bim_r = d_bim.ap().rearrange("(k p) n -> p k n", p=128)
    dv_r = d_dv.ap().rearrange("k p -> p k")
    ccout_r = d_ccout.ap().rearrange("(j n) one -> n (j one)", n=2 * NST)

    prog = {e: [] for e in ("sync", "gpsimd", "tensor", "vector", "scalar")}
    cnt = {"in": 0, "t": 0, "v": 0, "s": 0, "cc": 0, "st": 0, "ws": 0, "g": 0,
           "wb0": 0, "wb1": 0, "wb2": 0, "wb3": 0, "wd0": 0, "wd1": 0, "cp": 0,
           "u0": 0, "u1": 0, "u2": 0, "u3": 0,
           "cast0": 0, "cast1": 0, "cast2": 0, "cast3": 0}
    hw = {e: dict(cnt) for e in prog}

    drain_mode = [False]

    def _drain(e):
        e.drain()

    def E(eng, fn):
        prog[eng].append(fn)
        if eng == "vector" and drain_mode[0]:
            prog[eng].append(_drain)

    def W(eng, sem, val):
        if val > hw[eng][sem]:
            hw[eng][sem] = val
            E(eng, partial(_wait, sem=sems[sem], v=val))

    import contextlib
    with contextlib.ExitStack() as _stk:
        def _sb(name, shape, dt):
            return _stk.enter_context(nc.sbuf_tensor(name, shape, dt))

        def _ps(name):
            return _stk.enter_context(nc.psum_tensor(name, [128, 1024], F32))

        def _sem(name):
            return _stk.enter_context(nc.semaphore(name))

        stream = _sb("stream", [128, KC * TPC], F32)
        lnx = _sb("lnx", [128, KC * TPC], BF16)
        acts = _sb("acts", [128, 16 * TPC], BF16)
        wb0 = _sb("wb0", [128, 4096], BF16)
        wb1 = _sb("wb1", [128, 4096], BF16)
        wb2 = _sb("wb2", [128, 4096], BF16)
        wd0 = _sb("wd0", [128, 2048], BF16)
        wd1 = _sb("wd1", [128, 2048], BF16)
        scr0 = _sb("scr0", [128, 1024], F32)
        scr1 = _sb("scr1", [128, 1024], F32)
        scr2 = _sb("scr2", [128, 1024], F32)
        bure = _sb("bure", [NST, TPC], F32)
        buim = _sb("buim", [NST, TPC], F32)
        sre = _sb("sre", [NST, TPC], F32)
        sim = _sb("sim", [NST, TPC], F32)
        sgb0 = _sb("sgb0", [128, 1024], BF16)
        sgb1 = _sb("sgb1", [128, 1024], BF16)
        wb3 = _sb("wb3", [128, 4096], BF16)
        btre = _sb("btre", [128, KC * NST], BF16)
        btim = _sb("btim", [128, KC * NST], BF16)
        pre = _sb("pre", [NST, CH], F32)
        pim = _sb("pim", [NST, CH], F32)
        qre = _sb("qre", [NST, CH], F32)
        qim = _sb("qim", [NST, CH], F32)
        cols = _sb("cols", [NST, 256], F32)
        tmpp = _sb("tmpp", [NST, 256], F32)
        sgre = _sb("sgre", [NST, NCORES], F32)
        sgim = _sb("sgim", [NST, NCORES], F32)
        selsb = _sb("selsb", [NST, 24], F32)
        dsb = _sb("dsb", [128, KC], F32)
        ones = _sb("ones", [128, 128], F32)
        ps0, ps1, ps2, ps3 = _ps("ps0"), _ps("ps1"), _ps("ps2"), _ps("ps3")
        q_in, q_t = _sem("s_in"), _sem("s_t")
        q_v, q_s, q_cc, q_st = _sem("s_v"), _sem("s_s"), _sem("s_cc"), _sem("s_st")
        q_ws = _sem("s_ws")
        q_wb = [_sem("s_wb%d" % i) for i in range(4)]
        q_wd = [_sem("s_wd%d" % i) for i in range(2)]
        q_g = _sem("s_g")
        q_cast4 = [_sem("s_cast%d" % i) for i in range(4)]
        q_cp = _sem("s_cp")
        q_u = [_sem("s_u%d" % i) for i in range(4)]
        block = _stk.enter_context(nc.Block())
        sems = {"in": q_in, "t": q_t, "v": q_v, "s": q_s,
                "cc": q_cc, "st": q_st, "ws": q_ws, "g": q_g,
                "cp": q_cp, "u0": q_u[0], "u1": q_u[1], "u2": q_u[2], "u3": q_u[3],
                "wb0": q_wb[0], "wb1": q_wb[1], "wb2": q_wb[2], "wb3": q_wb[3],
                "wd0": q_wd[0], "wd1": q_wd[1]}
        ps = [ps0, ps1, ps2, ps3]
        wb = [wb0, wb1, wb2, wb3]
        sgb = [sgb0, sgb1]
        xbre = acts[0:NST, 0:TPC]
        xbim = acts[0:NST, TPC:2 * TPC]
        ctre = acts[0:NST, 2 * TPC:2 * TPC + D]
        ctim = acts[0:NST, 2 * TPC + D:2 * TPC + 2 * D]
        wd = [wd0, wd1]
        scr = [scr0, scr1, scr2]

        colmap = {}

        def col(name):
            if name not in colmap:
                colmap[name] = len(colmap)
            i = colmap[name]
            return cols[:, i:i + 1]

        for g_ in range(4):
            E("sync", partial(_dma,
                              out=stream.ap()[:, g_ * 4096:(g_ + 1) * 4096]
                                  .rearrange("p (k t) -> p k t", t=TPC),
                              in_=u_r[:, 4 * g_:4 * g_ + 4, :], sem=q_u[g_]))
            cnt["u%d" % g_] += 16
        U_EVTS = [("u%d" % (k // 4), 16) for k in range(KC)]
        E("gpsimd", partial(_dma, out=selsb[:, :], in_=d_sel.ap(), sem=q_ws))
        cnt["ws"] += 16
        E("gpsimd", partial(_dma, out=col("lre"), in_=d_lre.ap(), sem=q_ws, slow=True))
        cnt["ws"] += 16
        E("gpsimd", partial(_dma, out=col("lim"), in_=d_lim.ap(), sem=q_ws, slow=True))
        cnt["ws"] += 16
        E("gpsimd", partial(_dma, out=col("ldt"), in_=d_ldt.ap(), sem=q_ws, slow=True))
        cnt["ws"] += 16
        E("gpsimd", partial(_dma, out=dsb[:, :], in_=dv_r, sem=q_ws, slow=True))
        cnt["ws"] += 16
        E("gpsimd", partial(_dma, out=btre.ap().rearrange("p (k n) -> p k n", n=NST),
                            in_=bre_r, sem=q_ws, slow=True))
        cnt["ws"] += 16
        E("gpsimd", partial(_dma, out=btim.ap().rearrange("p (k n) -> p k n", n=NST),
                            in_=bim_r, sem=q_ws, slow=True))
        cnt["ws"] += 16
        SMALLS = cnt["ws"]
        E("vector", partial(_memset, ap=ones[:, :], v=1.0, sem=q_v))
        cnt["v"] += 1
        ONES_DONE = cnt["v"]

        grank = {}
        drank = {}
        cast_list = []
        for q_ in range(4):
            for pr_ in range(8):
                cast_list.append(("g", q_, pr_, 0))
                cast_list.append(("g", q_, pr_, 1))
            for dp_ in range(8):
                for is_ in range(2):
                    cast_list.append(("d", q_, dp_, is_))
        pend = []
        for bi, ent in enumerate(cast_list):
            qj = q_cast4[bi % 4]
            if ent[0] == "g":
                _, q_, pr_, half_ = ent
                obase = (64 if half_ else 0) + 16 * q_ + 2 * pr_
                s_ = (32 if half_ else 0) + q_ * 8 + pr_
                grank[(q_, pr_, half_)] = bi + 1
                E("gpsimd", partial(
                    _dma,
                    out=d_wgub.ap()[:, s_ * 4096:(s_ + 1) * 4096]
                        .rearrange("p (k o) -> p k o", o=256),
                    in_=wgu_r[:, :, obase * 128:obase * 128 + 256], sem=qj))
            else:
                _, q_, dp_, is_ = ent
                ic0 = 16 * q_ + 8 * is_
                sd_ = q_ * 16 + dp_ * 2 + is_
                drank[(q_, dp_, is_)] = bi + 1
                E("gpsimd", partial(
                    _dma,
                    out=d_wdb.ap()[:, sd_ * 2048:(sd_ + 1) * 2048]
                        .rearrange("p (i d) -> p i d", d=256),
                    in_=wd_r[:, ic0:ic0 + 8, dp_ * 256:(dp_ + 1) * 256], sem=qj))
            cnt["cast%d" % (bi % 4)] += 16
            pend.append((qj, cnt["cast%d" % (bi % 4)]))
            if len(pend) >= 4:
                qk, vk = pend.pop(0)
                E("gpsimd", partial(_wait, sem=qk, v=vk))
                E("gpsimd", partial(lambda e, s: e.sem_inc(s, 1), s=q_cp))
                cnt["cp"] += 1
        while pend:
            qk, vk = pend.pop(0)
            E("gpsimd", partial(_wait, sem=qk, v=vk))
            E("gpsimd", partial(lambda e, s: e.sem_inc(s, 1), s=q_cp))
            cnt["cp"] += 1

        prepv = {}

        def emit_prep():
            W("scalar", "ws", SMALLS)
            E("scalar", partial(_act, out=col("t0"), in_=col("lre"), func=AF.Exp))
            E("scalar", partial(_act, out=col("sp"), in_=col("t0"), func=AF.Ln, bias=1.0))
            E("scalar", partial(_act, out=col("delta"), in_=col("ldt"), func=AF.Exp, sem=q_s))
            cnt["s"] += 1
            S_PREP1 = cnt["s"]
            if PREP_SKIP_VEC:
                E("vector", partial(_memset, ap=col("vdone"), v=0.0, sem=q_v))
                cnt["v"] += 1
                prepv["V_PREP"] = cnt["v"]
                return
            W("vector", "s", S_PREP1)
            drain_mode[0] = True
            E("vector", partial(_ts, out=col("lamre"), a=col("sp"), s1=-1.0, s2=-0.0101,
                                op0=ALU.mult, op1=ALU.add))
            E("vector", partial(_tt, out=col("rho"), a=col("lamre"), b=col("delta"), op=ALU.mult))
            E("vector", partial(_tt, out=col("omega"), a=col("lim"), b=col("delta"), op=ALU.mult))
            PI = math.pi
            for _ in range(2):
                E("vector", partial(_ts, out=col("kp"), a=col("omega"), s1=PI, s2=2 * PI,
                                    op0=ALU.is_gt, op1=ALU.mult))
                E("vector", partial(_ts, out=col("km"), a=col("omega"), s1=-PI, s2=2 * PI,
                                    op0=ALU.is_lt, op1=ALU.mult))
                E("vector", partial(_tt, out=col("omega"), a=col("omega"), b=col("kp"), op=ALU.subtract))
                E("vector", partial(_tt, out=col("omega"), a=col("omega"), b=col("km"), op=ALU.add, sem=q_v))
                cnt["v"] += 1
            V_OMEGA = cnt["v"]
            W("scalar", "v", V_OMEGA)
            E("scalar", partial(_act, out=col("emag"), in_=col("rho"), func=AF.Exp))
            E("scalar", partial(_act, out=col("einv"), in_=col("rho"), func=AF.Exp, scale=-1.0))
            E("scalar", partial(_act, out=col("sh"), in_=col("omega"), func=AF.Sin, scale=0.5))
            E("scalar", partial(_act, out=col("sh2"), in_=col("sh"), func=AF.Square))
            E("scalar", partial(_act, out=col("t1"), in_=col("sh2"), func=AF.Ln, scale=-1.0, bias=1.0))
            E("scalar", partial(_act, out=col("chalf"), in_=col("t1"), func=AF.Exp, scale=0.5, sem=q_s))
            cnt["s"] += 1
            S_PREP2 = cnt["s"]
            W("vector", "s", S_PREP2)
            E("vector", partial(_ts, out=col("cos"), a=col("sh2"), s1=-2.0, s2=1.0,
                                op0=ALU.mult, op1=ALU.add))
            E("vector", partial(_tt, out=col("t2"), a=col("sh"), b=col("chalf"), op=ALU.mult))
            E("vector", partial(_ts, out=col("sin"), a=col("t2"), s1=2.0, s2=None, op0=ALU.mult, op1=None))
            E("vector", partial(_tt, out=col("ar"), a=col("emag"), b=col("cos"), op=ALU.mult))
            E("vector", partial(_tt, out=col("ai"), a=col("emag"), b=col("sin"), op=ALU.mult))
            E("vector", partial(_tt, out=col("ivr"), a=col("einv"), b=col("cos"), op=ALU.mult))
            E("vector", partial(_stt, out=col("ivi"), in0=col("einv"), sc=-1.0, in1=col("sin"),
                                op0=ALU.mult, op1=ALU.mult))
            E("vector", partial(_ts, out=col("x"), a=col("ar"), s1=-1.0, s2=None,
                                op0=ALU.add, op1=None))
            E("vector", partial(_tt, out=col("d1"), a=col("lamre"), b=col("lamre"), op=ALU.mult))
            E("vector", partial(_tt, out=col("d2"), a=col("lim"), b=col("lim"), op=ALU.mult))
            E("vector", partial(_tt, out=col("den"), a=col("d1"), b=col("d2"), op=ALU.add))
            E("vector", partial(_recip, out=col("rden"), in_=col("den")))
            E("vector", partial(_ts, out=col("rden"), a=col("rden"), s1=2.0, s2=None, op0=ALU.mult, op1=None))
            E("vector", partial(_tt, out=col("t3"), a=col("x"), b=col("lamre"), op=ALU.mult))
            E("vector", partial(_tt, out=col("t4"), a=col("ai"), b=col("lim"), op=ALU.mult))
            E("vector", partial(_tt, out=col("t5"), a=col("t3"), b=col("t4"), op=ALU.add))
            E("vector", partial(_tt, out=col("fr"), a=col("t5"), b=col("rden"), op=ALU.mult))
            E("vector", partial(_tt, out=col("t3"), a=col("ai"), b=col("lamre"), op=ALU.mult))
            E("vector", partial(_tt, out=col("t4"), a=col("x"), b=col("lim"), op=ALU.mult))
            E("vector", partial(_tt, out=col("t5"), a=col("t3"), b=col("t4"), op=ALU.subtract))
            E("vector", partial(_tt, out=col("fi"), a=col("t5"), b=col("rden"), op=ALU.mult))

            def ladder(Pr, Pi, zr0, zi0, zpre):
                E("vector", partial(_memset, ap=Pr[:, 0:1], v=1.0))
                E("vector", partial(_memset, ap=Pi[:, 0:1], v=0.0))
                E("vector", partial(_copy, out=col(zpre + "r0"), in_=zr0))
                E("vector", partial(_copy, out=col(zpre + "i0"), in_=zi0))
                for k in range(7):
                    w_ = 1 << k
                    zr = col(zpre + "r%d" % k)
                    zi = col(zpre + "i%d" % k)
                    E("vector", partial(_ts, out=tmpp[:, 0:w_], a=Pr[:, 0:w_], s1=zr, s2=None,
                                        op0=ALU.mult, op1=None))
                    E("vector", partial(_ts, out=tmpp[:, 128:128 + w_], a=Pi[:, 0:w_], s1=zi, s2=None,
                                        op0=ALU.mult, op1=None))
                    E("vector", partial(_tt, out=Pr[:, w_:2 * w_], a=tmpp[:, 0:w_],
                                        b=tmpp[:, 128:128 + w_], op=ALU.subtract))
                    E("vector", partial(_ts, out=tmpp[:, 0:w_], a=Pr[:, 0:w_], s1=zi, s2=None,
                                        op0=ALU.mult, op1=None))
                    E("vector", partial(_ts, out=tmpp[:, 128:128 + w_], a=Pi[:, 0:w_], s1=zr, s2=None,
                                        op0=ALU.mult, op1=None))
                    E("vector", partial(_tt, out=Pi[:, w_:2 * w_], a=tmpp[:, 0:w_],
                                        b=tmpp[:, 128:128 + w_], op=ALU.add))
                    nzr = col(zpre + "r%d" % (k + 1))
                    nzi = col(zpre + "i%d" % (k + 1))
                    E("vector", partial(_tt, out=col("t3"), a=zr, b=zr, op=ALU.mult))
                    E("vector", partial(_tt, out=col("t4"), a=zi, b=zi, op=ALU.mult))
                    E("vector", partial(_tt, out=nzr, a=col("t3"), b=col("t4"), op=ALU.subtract))
                    E("vector", partial(_tt, out=col("t3"), a=zr, b=zi, op=ALU.mult))
                    E("vector", partial(_ts, out=nzi, a=col("t3"), s1=2.0, s2=None, op0=ALU.mult, op1=None))

            ladder(pre, pim, col("ar"), col("ai"), "za")
            ladder(qre, qim, col("ivr"), col("ivi"), "zb")

            def csq(srcr, srci, dstr, dsti):
                E("vector", partial(_tt, out=col("t3"), a=srcr, b=srcr, op=ALU.mult))
                E("vector", partial(_tt, out=col("t4"), a=srci, b=srci, op=ALU.mult))
                E("vector", partial(_tt, out=dstr, a=col("t3"), b=col("t4"), op=ALU.subtract))
                E("vector", partial(_tt, out=col("t3"), a=srcr, b=srci, op=ALU.mult))
                E("vector", partial(_ts, out=dsti, a=col("t3"), s1=2.0, s2=None, op0=ALU.mult, op1=None))

            csq(col("zar7"), col("zai7"), col("a256r"), col("a256i"))
            csq(col("a256r"), col("a256i"), col("a512r"), col("a512i"))
            csq(col("a512r"), col("a512i"), col("akr"), col("aki"))
            csq(col("akr"), col("aki"), col("ak2r"), col("ak2i"))
            E("vector", partial(_memset, ap=col("vdone"), v=0.0, sem=q_v))
            cnt["v"] += 1
            V_PREP = cnt["v"]
            drain_mode[0] = False

            prepv["V_PREP"] = V_PREP

        fills = []
        wd_fills = []

        lnstate = {}

        def emit_ln_stats_chunk(k, ready_evt, pe=False):
            if k == 0:
                lnstate.clear()
                lnstate["sxx_v"] = {}
                lnstate["pe_t"] = {}
                lnstate["pe"] = pe
            W("scalar", ready_evt[0], ready_evt[1])
            if k >= 1:
                if pe:
                    W("scalar", "t", lnstate["pe_t"][k - 1])
                else:
                    W("scalar", "v", lnstate["sxx_v"][k - 1])
            E("scalar", partial(_act, out=scr2[:, :],
                                in_=stream[:, k * 1024:(k + 1) * 1024],
                                func=AF.Square, sem=q_s))
            cnt["s"] += 1
            s_sq = cnt["s"]
            if pe:
                W("tensor", ready_evt[0], ready_evt[1])
                for tb in range(2):
                    E("tensor", partial(_mm, out=ps0[0:1, tb * TB:(tb + 1) * TB],
                                        l=ones[:, 0:1],
                                        r=stream[:, k * 1024 + tb * TB:k * 1024 + (tb + 1) * TB],
                                        st=(k == 0), sp=(k == KC - 1)))
                W("tensor", "s", s_sq)
                for tb in range(2):
                    E("tensor", partial(_mm, out=ps1[0:1, tb * TB:(tb + 1) * TB],
                                        l=ones[:, 0:1], r=scr2[:, tb * TB:(tb + 1) * TB],
                                        st=(k == 0), sp=(k == KC - 1),
                                        sem=(q_t if tb == 1 else None)))
                cnt["t"] += 1
                lnstate["pe_t"][k] = cnt["t"]
                return
            W("vector", ready_evt[0], ready_evt[1])
            if k == 0:
                E("vector", partial(_copy, out=scr0[:, :], in_=stream[:, 0:1024]))
            else:
                E("vector", partial(_tt, out=scr0[:, :], a=scr0[:, :],
                                    b=stream[:, k * 1024:(k + 1) * 1024], op=ALU.add))
            W("vector", "s", s_sq)
            if k == 0:
                E("vector", partial(_copy, out=scr1[:, :], in_=scr2[:, :], sem=q_v))
            else:
                E("vector", partial(_tt, out=scr1[:, :], a=scr1[:, :],
                                    b=scr2[:, :], op=ALU.add, sem=q_v))
            cnt["v"] += 1
            lnstate["sxx_v"][k] = cnt["v"]

        def emit_ln_tail():
            if lnstate.get("pe"):
                W("vector", "t", lnstate["pe_t"][KC - 1])
            else:
                v_sxx = cnt["v"]
                W("tensor", "v", v_sxx)
                for tb in range(2):
                    E("tensor", partial(_mm, out=ps0[0:1, tb * TB:(tb + 1) * TB],
                                        l=ones[:, 0:1], r=scr0[:, tb * TB:(tb + 1) * TB],
                                        st=True, sp=True))
                    E("tensor", partial(_mm, out=ps1[0:1, tb * TB:(tb + 1) * TB],
                                        l=ones[:, 0:1], r=scr1[:, tb * TB:(tb + 1) * TB],
                                        st=True, sp=True, sem=(q_t if tb == 1 else None)))
                cnt["t"] += 1
                t_stat = cnt["t"]
                W("vector", "t", t_stat)
            E("vector", partial(_ts, out=scr0[0:1, :], a=ps0[0:1, :], s1=1.0 / D, s2=None,
                                op0=ALU.mult, op1=None))
            E("vector", partial(_ts, out=scr1[0:1, :], a=ps1[0:1, :], s1=1.0 / D, s2=None,
                                op0=ALU.mult, op1=None))
            E("vector", partial(_tt, out=scr2[0:1, :], a=scr0[0:1, :], b=scr0[0:1, :], op=ALU.mult))
            E("vector", partial(_tt, out=scr1[0:1, :], a=scr1[0:1, :], b=scr2[0:1, :], op=ALU.subtract))
            E("vector", partial(_ts, out=scr1[0:1, :], a=scr1[0:1, :], s1=1.0, s2=EPS_LN,
                                op0=ALU.mult, op1=ALU.add, sem=q_v))
            cnt["v"] += 1
            v_fin = cnt["v"]
            W("scalar", "v", v_fin)
            E("scalar", partial(_act, out=scr1[0:1, :], in_=scr1[0:1, :], func=AF.Ln))
            E("scalar", partial(_act, out=scr1[0:1, :], in_=scr1[0:1, :], func=AF.Exp,
                                scale=-0.5, sem=q_s))
            cnt["s"] += 1
            s_rstd = cnt["s"]
            W("vector", "s", s_rstd)
            E("vector", partial(_tt, out=scr2[0:1, :], a=scr0[0:1, :], b=scr1[0:1, :],
                                op=ALU.mult, sem=q_v))
            cnt["v"] += 1
            v_mr = cnt["v"]
            W("tensor", "v", v_mr)
            W("tensor", "s", cnt["s"])
            for tb in range(2):
                E("tensor", partial(_mm, out=ps2[:, tb * TB:(tb + 1) * TB], l=ones[0:1, 0:128],
                                    r=scr1[0:1, tb * TB:(tb + 1) * TB], st=True, sp=True))
                E("tensor", partial(_mm, out=ps3[:, tb * TB:(tb + 1) * TB], l=ones[0:1, 0:128],
                                    r=scr2[0:1, tb * TB:(tb + 1) * TB], st=True, sp=True,
                                    sem=(q_t if tb == 1 else None)))
            cnt["t"] += 1
            t_bc = cnt["t"]
            W("vector", "t", t_bc)
            evts = []
            for k in range(KC):
                E("vector", partial(_tt, out=scr2[:, :], a=stream[:, k * 1024:(k + 1) * 1024],
                                    b=ps2[:, :], op=ALU.mult))
                E("vector", partial(_tt, out=lnx[:, k * 1024:(k + 1) * 1024], a=scr2[:, :],
                                    b=ps3[:, :], op=ALU.subtract, sem=q_v))
                cnt["v"] += 1
                evts.append(("v", cnt["v"]))
            return evts

        def emit_ffn(lnx_evts, store_out=False, fuse_stats=False):
            nonlocal fills, wd_fills
            fill_n = len(fills)
            silu_hist = []
            mult_hist = []
            for q in range(FFN_NQ):
                W("tensor", "v", cnt["v"])
                for pr in range(FFN_NPAIR):
                    i0 = 16 * q + 2 * pr
                    for half, obase in ((0, i0), (1, 64 + i0)):
                        slot = len(fills) % 4
                        if len(fills) >= 4:
                            W("sync", "t", fills[len(fills) - 4][1])
                        s_ = (32 if half else 0) + q * 8 + pr
                        W("sync", "cp", grank[(q, pr, half)])
                        sname = "wb%d" % slot
                        E("sync", partial(
                            _dma,
                            out=wb[slot].ap()[:, 0:8] if PERF_NODMA else wb[slot].ap(),
                            in_=d_wgub.ap()[:, s_ * 4096:s_ * 4096 + 8] if PERF_NODMA
                                else d_wgub.ap()[:, s_ * 4096:(s_ + 1) * 4096],
                            sem=sems[sname]))
                        cnt[sname] += 16
                        fills.append([slot, None, (sname, cnt[sname])])
                    gslot = fills[-2][0]
                    uslot = fills[-1][0]
                    for sub in range(2):
                        i = i0 + sub
                        for half in range(2):
                            slot = gslot if half == 0 else uslot
                            fi = len(fills) - 2 + half
                            pgrp = ps[2 * (i % 2) + half]
                            W("tensor", fills[fi][2][0], fills[fi][2][1])
                            perk = (q == 0 and pr == 0)
                            if not perk:
                                W("tensor", lnx_evts[KC - 1][0], lnx_evts[KC - 1][1])
                            if len(silu_hist) >= 2 and half == 0:
                                W("tensor", "s", silu_hist[-2])
                            if len(mult_hist) >= 2 and half == 1:
                                W("tensor", "v", mult_hist[-2])
                            for k in range(KC):
                                if perk:
                                    W("tensor", lnx_evts[k][0], lnx_evts[k][1])
                                for tb in range(2):
                                    E("tensor", partial(
                                        _mm,
                                        out=pgrp[:, tb * TB:(tb + 1) * TB],
                                        l=wb[slot][:, k * 256 + sub * 128:k * 256 + (sub + 1) * 128],
                                        r=lnx[:, k * 1024 + tb * TB:k * 1024 + (tb + 1) * TB],
                                        st=(k == 0), sp=(k == KC - 1),
                                        sem=(q_t if (k == KC - 1 and tb == 1) else None)))
                            cnt["t"] += 1
                            fills[fi][1] = cnt["t"]
                            t_grp = cnt["t"]
                            if half == 0:
                                W("scalar", "t", t_grp)
                                if len(mult_hist) >= 2:
                                    W("scalar", "v", mult_hist[-2])
                                E("scalar", partial(_act, out=sgb[i % 2][:, 0:8] if PERF_NOEPI else sgb[i % 2][:, :],
                                                    in_=pgrp[:, 0:8] if PERF_NOEPI else pgrp[:, :],
                                                    func=AF.Silu, sem=q_s))
                                cnt["s"] += 1
                                silu_hist.append(cnt["s"])
                            else:
                                W("vector", "s", silu_hist[-1])
                                W("vector", "t", t_grp)
                                ic_local = i - 16 * q
                                if PERF_NOEPI:
                                    E("vector", partial(_tt, out=acts[:, ic_local * 1024:ic_local * 1024 + 8],
                                                        a=sgb[i % 2][:, 0:8], b=pgrp[:, 0:8],
                                                        op=ALU.mult, sem=q_v))
                                else:
                                    E("vector", partial(_tt, out=acts[:, ic_local * 1024:(ic_local + 1) * 1024],
                                                        a=sgb[i % 2][:, :], b=pgrp[:, :],
                                                        op=ALU.mult, sem=q_v))
                                cnt["v"] += 1
                                mult_hist.append(cnt["v"])
                v_act_done = cnt["v"]
                s_silu_done = cnt["s"]
                if not FFN_M2:
                    continue
                prev_evac = {}
                for dp in range(8):
                    par = dp % 2
                    pAB = (ps[2 * par], ps[2 * par + 1])
                    for iset in range(2):
                        slot = len(wd_fills) % 2
                        if len(wd_fills) >= 2:
                            W("sync", "t", wd_fills[len(wd_fills) - 2][1])
                        sd_ = q * 16 + dp * 2 + iset
                        W("sync", "cp", drank[(q, dp, iset)])
                        sname = "wd%d" % slot
                        E("sync", partial(
                            _dma,
                            out=wd[slot].ap()[:, 0:8] if PERF_NODMA else wd[slot].ap(),
                            in_=d_wdb.ap()[:, sd_ * 2048:sd_ * 2048 + 8] if PERF_NODMA
                                else d_wdb.ap()[:, sd_ * 2048:(sd_ + 1) * 2048],
                            sem=sems[sname]))
                        cnt[sname] += 16
                        wd_fills.append([slot, None, cnt[sname]])
                        W("tensor", sname, cnt[sname])
                        W("tensor", "v", v_act_done)
                        W("tensor", "s", s_silu_done)
                        if par in prev_evac:
                            W("tensor", "v", prev_evac[par])
                        for i4 in range(8):
                            for tb in range(2):
                                for dc in range(2):
                                    first = (iset == 0 and i4 == 0)
                                    last = (iset == 1 and i4 == 7)
                                    ic_local = 8 * iset + i4
                                    E("tensor", partial(
                                        _mm,
                                        out=pAB[dc][:, tb * TB:(tb + 1) * TB],
                                        l=wd[slot][:, i4 * 256 + dc * 128:i4 * 256 + (dc + 1) * 128],
                                        r=acts[:, ic_local * 1024 + tb * TB:ic_local * 1024 + (tb + 1) * TB],
                                        st=first, sp=last,
                                        sem=(q_t if (i4 == 7 and tb == 1 and dc == 1) else None)))
                        cnt["t"] += 1
                        wd_fills[-1][1] = cnt["t"]
                    t_dp = cnt["t"]
                    W("vector", "t", t_dp)
                    for dc in range(2):
                        kk = 2 * dp + dc
                        E("vector", partial(_stt, out=stream[:, kk * 1024:(kk + 1) * 1024],
                                            in0=pAB[dc][:, :], sc=0.5,
                                            in1=stream[:, kk * 1024:(kk + 1) * 1024],
                                            op0=ALU.mult, op1=ALU.add,
                                            sem=(q_v if dc == 1 else None)))
                    cnt["v"] += 1
                    prev_evac[par] = cnt["v"]
                    if fuse_stats and q == FFN_NQ - 1:
                        for kk_ in (2 * dp, 2 * dp + 1):
                            emit_ln_stats_chunk(kk_, ("v", prev_evac[par]))
                    if store_out and STREAM_STORE and q == FFN_NQ - 1:
                        W("scalar", "v", cnt["v"])
                        E("scalar", partial(
                            _dma, out=y_r[:, 2 * dp:2 * dp + 2, :],
                            in_=stream.ap()[:, 2 * dp * 1024:(2 * dp + 2) * 1024]
                                .rearrange("p (k t) -> p k t", t=TPC),
                            sem=q_st))
                        cnt["st"] += 16
            return cnt["v"]

        def emit_ssm(lnx_evts, fuse_stats=False):
            t_entry = cnt["t"]
            W("gpsimd", "t", t_entry)
            E("gpsimd", partial(_dma, out=ctre, in_=d_cre.ap(), sem=q_ws))
            cnt["ws"] += 16
            E("gpsimd", partial(_dma, out=ctim, in_=d_cim.ap(), sem=q_ws))
            cnt["ws"] += 16
            CT_WS = cnt["ws"]
            W("scalar", "ws", CT_WS)
            E("scalar", partial(_act, out=ctim, in_=ctim, func=AF.Copy, scale=-1.0, sem=q_s))
            cnt["s"] += 1
            CT_NEG = cnt["s"]
            drain_mode[0] = True
            W("tensor", "ws", SMALLS)
            W("tensor", "s", cnt["s"])
            for comp, (bt, pg) in enumerate(((btre, ps0), (btim, ps1))):
                for tb in range(2):
                    for k in range(KC):
                        if comp == 0 and tb == 0:
                            W("tensor", lnx_evts[k][0], lnx_evts[k][1])
                        E("tensor", partial(
                            _mm, out=pg[0:NST, tb * TB:(tb + 1) * TB],
                            l=bt[:, k * NST:(k + 1) * NST],
                            r=lnx[:, k * 1024 + tb * TB:k * 1024 + (tb + 1) * TB],
                            st=(k == 0), sp=(k == KC - 1),
                            sem=(q_t if (comp == 1 and tb == 1 and k == KC - 1) else None)))
            cnt["t"] += 1
            t_g = cnt["t"]
            W("vector", "t", t_g)
            W("vector", "v", prepv["V_PREP"])
            E("vector", partial(_ts, out=bure[:, :], a=ps0[0:NST, :], s1=col("fr"), s2=None,
                                op0=ALU.mult, op1=None))
            E("vector", partial(_stt, out=scr0[0:NST, :], in0=ps1[0:NST, :], sc=col("fi"),
                                in1=bure[:, :], op0=ALU.mult, op1=ALU.subtract))
            E("vector", partial(_ts, out=scr0[0:NST, :], a=scr0[0:NST, :], s1=-1.0, s2=None,
                                op0=ALU.mult, op1=None))
            E("vector", partial(_ts, out=buim[:, :], a=ps1[0:NST, :], s1=col("fr"), s2=None,
                                op0=ALU.mult, op1=None))
            E("vector", partial(_stt, out=buim[:, :], in0=ps0[0:NST, :], sc=col("fi"),
                                in1=buim[:, :], op0=ALU.mult, op1=ALU.add))
            E("vector", partial(_copy, out=bure[:, :], in_=scr0[0:NST, :], sem=q_v))
            cnt["v"] += 1

            drain_mode[0] = False

            def DR():
                E("vector", _drain)

            SLS = [slice(j * CH, (j + 1) * CH) for j in range(NCH_S)]
            for sl in SLS:
                E("vector", partial(_tt, out=scr2[0:NST, sl], a=qim[:, :], b=buim[:, sl], op=ALU.mult))
            DR()
            for sl in SLS:
                E("vector", partial(_tt, out=scr0[0:NST, sl], a=qre[:, :], b=bure[:, sl], op=ALU.mult))
            DR()
            for sl in SLS:
                E("vector", partial(_tt, out=scr0[0:NST, sl], a=scr0[0:NST, sl], b=scr2[0:NST, sl], op=ALU.subtract))
            for sl in SLS:
                E("vector", partial(_tt, out=scr2[0:NST, sl], a=qim[:, :], b=bure[:, sl], op=ALU.mult))
            for sl in SLS:
                E("vector", partial(_tt, out=scr1[0:NST, sl], a=qre[:, :], b=buim[:, sl], op=ALU.mult))
            DR()
            for sl in SLS:
                E("vector", partial(_tt, out=scr1[0:NST, sl], a=scr1[0:NST, sl], b=scr2[0:NST, sl], op=ALU.add))
            DR()
            for j, sl in enumerate(SLS):
                E("vector", partial(_scan, out=sre[:, sl], d0=scr0[0:NST, sl], d1=scr0[0:NST, sl],
                                    init=0.0, op0=ALU.add, op1=ALU.bypass))
            for j, sl in enumerate(SLS):
                E("vector", partial(_scan, out=sim[:, sl], d0=scr1[0:NST, sl], d1=scr1[0:NST, sl],
                                    init=0.0, op0=ALU.add, op1=ALU.bypass))
            DR()
            for j in range(NCH_S):
                lc = slice(j * CH + CH - 1, j * CH + CH)
                E("vector", partial(_tt, out=tmpp[:, 4 * j:4 * j + 1], a=pre[:, CH - 1:CH], b=sre[:, lc], op=ALU.mult))
                E("vector", partial(_tt, out=tmpp[:, 4 * j + 1:4 * j + 2], a=pim[:, CH - 1:CH], b=sim[:, lc], op=ALU.mult))
                E("vector", partial(_tt, out=tmpp[:, 4 * j + 2:4 * j + 3], a=pre[:, CH - 1:CH], b=sim[:, lc], op=ALU.mult))
                E("vector", partial(_tt, out=tmpp[:, 4 * j + 3:4 * j + 4], a=pim[:, CH - 1:CH], b=sre[:, lc], op=ALU.mult))
            DR()
            for j in range(NCH_S):
                E("vector", partial(_tt, out=col("clr%d" % j), a=tmpp[:, 4 * j:4 * j + 1],
                                    b=tmpp[:, 4 * j + 1:4 * j + 2], op=ALU.subtract))
                E("vector", partial(_tt, out=col("cli%d" % j), a=tmpp[:, 4 * j + 2:4 * j + 3],
                                    b=tmpp[:, 4 * j + 3:4 * j + 4], op=ALU.add))
            E("vector", partial(_memset, ap=col("glr0"), v=0.0))
            E("vector", partial(_memset, ap=col("gli0"), v=0.0))
            DR()
            for j in range(1, NCH_S):
                pj = j - 1
                E("vector", partial(_tt, out=col("t3"), a=col("zar7"), b=col("clr%d" % pj), op=ALU.mult))
                E("vector", partial(_tt, out=col("t4"), a=col("zai7"), b=col("cli%d" % pj), op=ALU.mult))
                E("vector", partial(_tt, out=col("t5"), a=col("zar7"), b=col("cli%d" % pj), op=ALU.mult))
                E("vector", partial(_tt, out=col("t6"), a=col("zai7"), b=col("clr%d" % pj), op=ALU.mult))
                E("vector", partial(_tt, out=col("g3"), a=col("ar"), b=col("clr%d" % pj), op=ALU.mult))
                E("vector", partial(_tt, out=col("g4"), a=col("ai"), b=col("cli%d" % pj), op=ALU.mult))
                E("vector", partial(_tt, out=col("g5"), a=col("ar"), b=col("cli%d" % pj), op=ALU.mult))
                E("vector", partial(_tt, out=col("g6"), a=col("ai"), b=col("clr%d" % pj), op=ALU.mult))
                DR()
                E("vector", partial(_tt, out=col("t3"), a=col("t3"), b=col("t4"), op=ALU.subtract))
                E("vector", partial(_tt, out=col("t5"), a=col("t5"), b=col("t6"), op=ALU.add))
                E("vector", partial(_tt, out=col("glr%d" % j), a=col("g3"), b=col("g4"), op=ALU.subtract))
                E("vector", partial(_tt, out=col("gli%d" % j), a=col("g5"), b=col("g6"), op=ALU.add))
                DR()
                E("vector", partial(_tt, out=col("clr%d" % j), a=col("clr%d" % j), b=col("t3"), op=ALU.add))
                E("vector", partial(_tt, out=col("cli%d" % j), a=col("cli%d" % j), b=col("t5"), op=ALU.add))
                DR()
            E("vector", partial(_copy, out=col("scr_r"), in_=col("clr%d" % (NCH_S - 1))))
            E("vector", partial(_copy, out=col("scr_i"), in_=col("cli%d" % (NCH_S - 1)), sem=q_v))
            cnt["v"] += 1
            v_state = cnt["v"]
            E("vector", _drain)

            W("gpsimd", "v", v_state)
            E("gpsimd", partial(_dma, out=d_ccin[0:NST, 0:1], in_=col("scr_r"), sem=q_g, slow=True))
            cnt["g"] += 16
            E("gpsimd", partial(_dma, out=d_ccin[NST:2 * NST, 0:1], in_=col("scr_i"), sem=q_g, slow=True))
            cnt["g"] += 16
            W("gpsimd", "g", cnt["g"])
            if PERF_NOCC:
                E("gpsimd", partial(_dma, out=d_ccout.ap()[0:2 * NST, :], in_=d_ccin.ap(), sem=q_cc, n=16))
                cnt["cc"] += 16
            else:
                E("gpsimd", partial(_cc, cin=d_ccin.ap().opt(), cout=d_ccout.ap().opt(), sem=q_cc))
                cnt["cc"] += 1
            W("gpsimd", "cc", cnt["cc"])
            E("gpsimd", partial(_dma, out=sgre[:, :], in_=ccout_r[0:NST, :], sem=q_g, slow=True))
            cnt["g"] += 16
            E("gpsimd", partial(_dma, out=sgim[:, :], in_=ccout_r[NST:2 * NST, :], sem=q_g, slow=True))
            cnt["g"] += 16
            w_gather = cnt["g"]

            drain_mode[0] = True
            W("vector", "g", w_gather)
            W("vector", "in", cnt["in"])
            for m in range(3):
                E("vector", partial(_stt, out=tmpp[:, 64:64 + NCORES], in0=sgre[:, :], sc=1.0,
                                    in1=selsb[:, m * 8:(m + 1) * 8], op0=ALU.mult, op1=ALU.mult,
                                    accum=col("Tr%d" % m)))
                E("vector", partial(_stt, out=tmpp[:, 64:64 + NCORES], in0=sgim[:, :], sc=1.0,
                                    in1=selsb[:, m * 8:(m + 1) * 8], op0=ALU.mult, op1=ALU.mult,
                                    accum=col("Ti%d" % m)))
            E("vector", partial(_copy, out=col("Ir"), in_=col("Tr0")))
            E("vector", partial(_copy, out=col("Ii"), in_=col("Ti0")))
            for m, (wr, wi) in enumerate((("akr", "aki"), ("ak2r", "ak2i")), start=1):
                E("vector", partial(_tt, out=col("t3"), a=col(wr), b=col("Tr%d" % m), op=ALU.mult))
                E("vector", partial(_tt, out=col("t4"), a=col(wi), b=col("Ti%d" % m), op=ALU.mult))
                E("vector", partial(_tt, out=col("t3"), a=col("t3"), b=col("t4"), op=ALU.subtract))
                E("vector", partial(_tt, out=col("Ir"), a=col("Ir"), b=col("t3"), op=ALU.add))
                E("vector", partial(_tt, out=col("t3"), a=col(wr), b=col("Ti%d" % m), op=ALU.mult))
                E("vector", partial(_tt, out=col("t4"), a=col(wi), b=col("Tr%d" % m), op=ALU.mult))
                E("vector", partial(_tt, out=col("t3"), a=col("t3"), b=col("t4"), op=ALU.add))
                E("vector", partial(_tt, out=col("Ii"), a=col("Ii"), b=col("t3"), op=ALU.add))
            drain_mode[0] = False
            E("vector", partial(_tt, out=col("t3"), a=col("ar"), b=col("Ir"), op=ALU.mult))
            E("vector", partial(_tt, out=col("t4"), a=col("ai"), b=col("Ii"), op=ALU.mult))
            E("vector", partial(_tt, out=col("t5"), a=col("ar"), b=col("Ii"), op=ALU.mult))
            E("vector", partial(_tt, out=col("t6"), a=col("ai"), b=col("Ir"), op=ALU.mult))
            DR()
            E("vector", partial(_tt, out=col("hr0"), a=col("t3"), b=col("t4"), op=ALU.subtract))
            E("vector", partial(_tt, out=col("hi0"), a=col("t5"), b=col("t6"), op=ALU.add))
            DR()
            for j in range(1, NCH_S):
                pj = j - 1
                E("vector", partial(_tt, out=col("t3"), a=col("zar7"), b=col("hr%d" % pj), op=ALU.mult))
                E("vector", partial(_tt, out=col("t4"), a=col("zai7"), b=col("hi%d" % pj), op=ALU.mult))
                E("vector", partial(_tt, out=col("t5"), a=col("zar7"), b=col("hi%d" % pj), op=ALU.mult))
                E("vector", partial(_tt, out=col("t6"), a=col("zai7"), b=col("hr%d" % pj), op=ALU.mult))
                DR()
                E("vector", partial(_tt, out=col("hr%d" % j), a=col("t3"), b=col("t4"), op=ALU.subtract))
                E("vector", partial(_tt, out=col("hi%d" % j), a=col("t5"), b=col("t6"), op=ALU.add))
                DR()
            for j in range(NCH_S):
                E("vector", partial(_tt, out=col("wr%d" % j), a=col("glr%d" % j), b=col("hr%d" % j), op=ALU.add))
                E("vector", partial(_tt, out=col("wi%d" % j), a=col("gli%d" % j), b=col("hi%d" % j), op=ALU.add))
            DR()
            for j, sl in enumerate(SLS):
                E("vector", partial(_ts, out=scr0[0:NST, sl], a=sre[:, sl], s1=col("wr%d" % j), s2=None,
                                    op0=ALU.add, op1=None))
            for j, sl in enumerate(SLS):
                E("vector", partial(_ts, out=scr1[0:NST, sl], a=sim[:, sl], s1=col("wi%d" % j), s2=None,
                                    op0=ALU.add, op1=None))
            DR()
            for sl in SLS:
                E("vector", partial(_tt, out=bure[:, sl], a=pre[:, :], b=scr0[0:NST, sl], op=ALU.mult))
            for sl in SLS:
                E("vector", partial(_tt, out=buim[:, sl], a=pre[:, :], b=scr1[0:NST, sl], op=ALU.mult))
            DR()
            for sl in SLS:
                E("vector", partial(_tt, out=scr2[0:NST, sl], a=pim[:, :], b=scr1[0:NST, sl], op=ALU.mult))
            DR()
            for sl in SLS:
                E("vector", partial(_tt, out=bure[:, sl], a=bure[:, sl], b=scr2[0:NST, sl], op=ALU.subtract))
            for sl in SLS:
                E("vector", partial(_tt, out=scr2[0:NST, sl], a=pim[:, :], b=scr0[0:NST, sl], op=ALU.mult))
            DR()
            for sl in SLS:
                E("vector", partial(_tt, out=buim[:, sl], a=buim[:, sl], b=scr2[0:NST, sl], op=ALU.add))
            DR()
            E("vector", partial(_copy, out=xbre, in_=bure[:, :]))
            E("vector", partial(_copy, out=xbim, in_=buim[:, :], sem=q_v))
            cnt["v"] += 1
            v_xs = cnt["v"]

            W("tensor", "ws", CT_WS)
            W("tensor", "s", CT_NEG)
            prev_pe = {}
            for kk in range(KC):
                pg = ps[2 + (kk % 2)]
                W("tensor", "v", v_xs)
                if kk >= 2:
                    W("tensor", "v", prev_pe[kk % 2])
                for tb in range(2):
                    E("tensor", partial(_mm, out=pg[:, tb * TB:(tb + 1) * TB],
                                        l=ctre[:, kk * 128:(kk + 1) * 128],
                                        r=xbre[:, tb * TB:(tb + 1) * TB], st=True, sp=False))
                    E("tensor", partial(_mm, out=pg[:, tb * TB:(tb + 1) * TB],
                                        l=ctim[:, kk * 128:(kk + 1) * 128],
                                        r=xbim[:, tb * TB:(tb + 1) * TB], st=False, sp=True,
                                        sem=(q_t if tb == 1 else None)))
                cnt["t"] += 1
                W("vector", "t", cnt["t"])
                E("vector", partial(_stt, out=pg[:, :], in0=lnx[:, kk * 1024:(kk + 1) * 1024],
                                    sc=dsb[:, kk:kk + 1], in1=pg[:, :], op0=ALU.mult, op1=ALU.add))
                E("vector", partial(_tt, out=stream[:, kk * 1024:(kk + 1) * 1024],
                                    a=stream[:, kk * 1024:(kk + 1) * 1024], b=pg[:, :],
                                    op=ALU.add, sem=q_v))
                cnt["v"] += 1
                prev_pe[kk % 2] = cnt["v"]
                if fuse_stats:
                    drain_mode[0] = False
                    emit_ln_stats_chunk(kk, ("v", cnt["v"]))
            v_ssm = cnt["v"]
            drain_mode[0] = False
            return v_ssm


        if STAGE >= 2:
            for k_ in range(KC):
                emit_ln_stats_chunk(k_, U_EVTS[k_])
            lnx1_evts = emit_ln_tail()
            emit_prep()
        else:
            emit_prep()
        if STAGE >= 3:
            emit_ffn(lnx1_evts, fuse_stats=(STAGE >= 4))
        if STAGE >= 4:
            lnx2_evts = emit_ln_tail()
            emit_ssm(lnx2_evts, fuse_stats=(STAGE >= 5))
        stored = False
        if STAGE >= 5:
            lnx3_evts = emit_ln_tail()
            emit_ffn(lnx3_evts, store_out=(FFN_NQ == 4 and FFN_M2))
            stored = FFN_NQ == 4 and FFN_M2 and STREAM_STORE


        if not stored:
            W("scalar", "v", cnt["v"])
            E("scalar", partial(_dma, out=y_r,
                              in_=stream.ap().rearrange("p (k t) -> p k t", t=TPC),
                              sem=q_st))
            cnt["st"] += 16
        if DEBUG_SSM:
            d_dbg = nc.dram_tensor("dbg", [128, 2048], F32, kind="ExternalOutput")
            dumps = [
                (d_dbg.ap()[0:32, 0:128], cols[:, 0:128]),
                (d_dbg.ap()[0:32, 128:256], pre[:, :]),
                (d_dbg.ap()[0:32, 256:384], pim[:, :]),
                (d_dbg.ap()[0:32, 384:512], qre[:, :]),
                (d_dbg.ap()[0:32, 512:640], qim[:, :]),
                (d_dbg.ap()[0:32, 640:648], sgre[:, :]),
                (d_dbg.ap()[0:32, 648:656], sgim[:, :]),
                (d_dbg.ap()[32:64, 0:1024], sre[:, :]),
                (d_dbg.ap()[32:64, 1024:2048], sim[:, :]),
                (d_dbg.ap()[64:96, 0:1024], bure[:, :]),
                (d_dbg.ap()[64:96, 1024:2048], buim[:, :]),
            ]
            for o_, i_ in dumps:
                E("scalar", partial(_dma, out=o_, in_=i_, sem=q_st))
                cnt["st"] += 16
        W("scalar", "st", cnt["st"])

        def flush(ops):
            def body(e):
                for f in ops:
                    f(e)
            return body

        block.sync(flush(prog["sync"]))
        block.gpsimd(flush(prog["gpsimd"]))
        block.tensor(flush(prog["tensor"]))
        block.vector(flush(prog["vector"]))
        block.scalar(flush(prog["scalar"]))

    return nc




_NC_CACHE = [None]


def kernel(L0_attn_u, L0_attn_Lambda_unconstrained_re, L0_attn_Lambda_im,
           L0_attn_B_re, L0_attn_B_im, L0_attn_C_re, L0_attn_C_im, L0_attn_D,
           L0_attn_log_Delta, L0_ffn_w_gate_up, L0_ffn_w_down):
    u = np.asarray(L0_attn_u, dtype=np.float32)
    wguT = np.ascontiguousarray(np.asarray(L0_ffn_w_gate_up, dtype=np.float32).T)
    wdT = np.ascontiguousarray(np.asarray(L0_ffn_w_down, dtype=np.float32).T)
    breT = np.ascontiguousarray(np.asarray(L0_attn_B_re, dtype=np.float32).T)
    bimT = np.ascontiguousarray(np.asarray(L0_attn_B_im, dtype=np.float32).T)
    creT = np.ascontiguousarray(np.asarray(L0_attn_C_re, dtype=np.float32).T)
    cimT = np.ascontiguousarray(np.asarray(L0_attn_C_im, dtype=np.float32).T)
    dvec = np.ascontiguousarray(np.asarray(L0_attn_D, dtype=np.float32).reshape(KC, 128))
    lre = np.ascontiguousarray(np.asarray(
        L0_attn_Lambda_unconstrained_re, dtype=np.float32).reshape(NST, 1))
    lim = np.ascontiguousarray(np.asarray(L0_attn_Lambda_im, dtype=np.float32).reshape(NST, 1))
    ldt = np.ascontiguousarray(np.asarray(L0_attn_log_Delta, dtype=np.float32)[:NST].reshape(NST, 1))

    if _NC_CACHE[0] is None:
        _NC_CACHE[0] = build_nc()
    nc = _NC_CACHE[0]

    in_maps = []
    for c in range(NCORES):
        b, s = divmod(c, 4)
        u_core = np.ascontiguousarray(u[b, s * TPC:(s + 1) * TPC, :].T)
        selm = np.zeros((NST, 24), np.float32)
        for m in range(3):
            jloc = s - 1 - m
            if jloc >= 0:
                selm[:, m * 8 + (b * 4 + jloc)] = 1.0
        in_maps.append({
            "u": u_core, "wgu": wguT, "wd": wdT,
            "bre": breT, "bim": bimT, "cre": creT, "cim": cimT,
            "dvec": dvec, "lre": lre, "lim": lim, "ldt": ldt, "selm": selm,
        })
    res = run_bass_kernel_spmd(nc, in_maps, core_ids=list(range(NCORES)))
    _NC_CACHE.append(res)
    out = np.empty((2, 4096, D), np.float32)
    for c in range(NCORES):
        b, s = divmod(c, 4)
        out[b, s * TPC:(s + 1) * TPC, :] = np.asarray(res.results[c]["y"]).T
    return out


# version 10
# speedup vs baseline: 1.0012x; 1.0012x over previous
import math
from functools import partial

import numpy as np
import concourse.bass as bass
import concourse.mybir as mybir
from concourse.bass_utils import run_bass_kernel_spmd

F32 = mybir.dt.float32
BF16 = mybir.dt.bfloat16
AF = mybir.ActivationFunctionType
ALU = mybir.AluOpType

NCORES = 8
TPC = 1024
D = 2048
KC = 16
ICH = 64
NST = 32
CH = 128
NCH_S = TPC // CH
EPS_LN = 1e-5
TB = 512


def _wait(e, sem, v):
    e.wait_ge(sem, v)


def _mm(e, out, l, r, st, sp, sem=None, n=1):
    i = e.matmul(out, l, r, start=st, stop=sp)
    if sem is not None:
        i.then_inc(sem, n)


def _dma(e, out, in_, sem=None, n=16, slow=False):
    kw = {}
    if slow:
        kw["allow_slow_non_contiguous"] = True
    i = e.dma_start(out=out, in_=in_, **kw)
    if sem is not None:
        i.then_inc(sem, n)


def _tt(e, out, a, b, op, sem=None):
    i = e.tensor_tensor(out, a, b, op)
    if sem is not None:
        i.then_inc(sem, 1)


def _ts(e, out, a, s1, s2, op0, op1, sem=None):
    if s2 is None:
        s2, op1 = 0.0, ALU.add
    i = e.tensor_scalar(out, a, s1, s2, op0, op1)
    if sem is not None:
        i.then_inc(sem, 1)


def _stt(e, out, in0, sc, in1, op0, op1, accum=None, sem=None):
    i = e.scalar_tensor_tensor(out, in0, sc, in1, op0, op1, accum_out=accum)
    if sem is not None:
        i.then_inc(sem, 1)


def _act(e, out, in_, func, bias=None, scale=None, sem=None):
    kw = {}
    if bias is not None:
        kw["bias"] = bias
    if scale is not None:
        kw["scale"] = scale
    i = e.activation(out, in_, func, **kw)
    if sem is not None:
        i.then_inc(sem, 1)


def _scan(e, out, d0, d1, init, op0, op1, sem=None):
    i = e.tensor_tensor_scan(out, d0, d1, init, op0, op1)
    if sem is not None:
        i.then_inc(sem, 1)


def _copy(e, out, in_, sem=None):
    i = e.tensor_copy(out, in_)
    if sem is not None:
        i.then_inc(sem, 1)


def _recip(e, out, in_, sem=None):
    i = e.reciprocal(out, in_)
    if sem is not None:
        i.then_inc(sem, 1)


def _memset(e, ap, v, sem=None):
    i = e.memset(ap, v)
    if sem is not None:
        i.then_inc(sem, 1)


def _cc(e, cin, cout, sem):
    e.collective_compute(
        "AllGather", ALU.bypass,
        replica_groups=[list(range(NCORES))],
        ins=[cin], outs=[cout],
    ).then_inc(sem, 1)


STAGE = 5
FFN_NQ = 4
FFN_NPAIR = 8
FFN_M2 = True
DEBUG_SSM = False
STREAM_STORE = True
PERF_NOCC = False
PREP_AFTER_LN1 = True
PREP_SKIP_VEC = False
PERF_NOEPI = False
PERF_NODMA = False


def build_nc():
    nc = bass.Bass("TRN2", target_bir_lowering=False,
                   detect_race_conditions=False)

    d_u = nc.dram_tensor("u", [D, TPC], F32, kind="ExternalInput")
    d_wgu = nc.dram_tensor("wgu", [D, 8 * D], F32, kind="ExternalInput")
    d_wd = nc.dram_tensor("wd", [4 * D, D], F32, kind="ExternalInput")
    d_bre = nc.dram_tensor("bre", [D, NST], F32, kind="ExternalInput")
    d_bim = nc.dram_tensor("bim", [D, NST], F32, kind="ExternalInput")
    d_cre = nc.dram_tensor("cre", [NST, D], F32, kind="ExternalInput")
    d_cim = nc.dram_tensor("cim", [NST, D], F32, kind="ExternalInput")
    d_dv = nc.dram_tensor("dvec", [KC, 128], F32, kind="ExternalInput")
    d_lre = nc.dram_tensor("lre", [NST, 1], F32, kind="ExternalInput")
    d_lim = nc.dram_tensor("lim", [NST, 1], F32, kind="ExternalInput")
    d_ldt = nc.dram_tensor("ldt", [NST, 1], F32, kind="ExternalInput")
    d_sel = nc.dram_tensor("selm", [NST, 24], F32, kind="ExternalInput")
    d_y = nc.dram_tensor("y", [D, TPC], F32, kind="ExternalOutput")
    d_ccin = nc.dram_tensor("cc_in", [2 * NST, 1], F32)
    d_ccout = nc.dram_tensor("cc_out", [NCORES * 2 * NST, 1], F32)
    d_wgub = nc.dram_tensor("wgub", [128, 64 * 4096], BF16)
    d_wdb = nc.dram_tensor("wdb", [128, 64 * 2048], BF16)

    u_r = d_u.ap().rearrange("(k p) t -> p k t", p=128)
    y_r = d_y.ap().rearrange("(k p) t -> p k t", p=128)
    wgu_r = d_wgu.ap().rearrange("(k p) o -> p k o", p=128)
    wd_r = d_wd.ap().rearrange("(i p) d -> p i d", p=128)
    bre_r = d_bre.ap().rearrange("(k p) n -> p k n", p=128)
    bim_r = d_bim.ap().rearrange("(k p) n -> p k n", p=128)
    dv_r = d_dv.ap().rearrange("k p -> p k")
    ccout_r = d_ccout.ap().rearrange("(j n) one -> n (j one)", n=2 * NST)

    prog = {e: [] for e in ("sync", "gpsimd", "tensor", "vector", "scalar")}
    cnt = {"in": 0, "t": 0, "v": 0, "s": 0, "cc": 0, "st": 0, "ws": 0, "g": 0,
           "wb0": 0, "wb1": 0, "wb2": 0, "wb3": 0, "wd0": 0, "wd1": 0, "cp": 0,
           "u0": 0, "u1": 0, "u2": 0, "u3": 0,
           "cast0": 0, "cast1": 0, "cast2": 0, "cast3": 0}
    hw = {e: dict(cnt) for e in prog}

    drain_mode = [False]

    def _drain(e):
        e.drain()

    def E(eng, fn):
        prog[eng].append(fn)
        if eng == "vector" and drain_mode[0]:
            prog[eng].append(_drain)

    def W(eng, sem, val):
        if val > hw[eng][sem]:
            hw[eng][sem] = val
            E(eng, partial(_wait, sem=sems[sem], v=val))

    import contextlib
    with contextlib.ExitStack() as _stk:
        def _sb(name, shape, dt):
            return _stk.enter_context(nc.sbuf_tensor(name, shape, dt))

        def _ps(name):
            return _stk.enter_context(nc.psum_tensor(name, [128, 1024], F32))

        def _sem(name):
            return _stk.enter_context(nc.semaphore(name))

        stream = _sb("stream", [128, KC * TPC], F32)
        lnx = _sb("lnx", [128, KC * TPC], BF16)
        acts = _sb("acts", [128, 16 * TPC], BF16)
        wb0 = _sb("wb0", [128, 4096], BF16)
        wb1 = _sb("wb1", [128, 4096], BF16)
        wb2 = _sb("wb2", [128, 4096], BF16)
        wd0 = _sb("wd0", [128, 2048], BF16)
        wd1 = _sb("wd1", [128, 2048], BF16)
        scr0 = _sb("scr0", [128, 1024], F32)
        scr1 = _sb("scr1", [128, 1024], F32)
        scr2 = _sb("scr2", [128, 1024], F32)
        bure = _sb("bure", [NST, TPC], F32)
        buim = _sb("buim", [NST, TPC], F32)
        sre = _sb("sre", [NST, TPC], F32)
        sim = _sb("sim", [NST, TPC], F32)
        sgb0 = _sb("sgb0", [128, 1024], BF16)
        sgb1 = _sb("sgb1", [128, 1024], BF16)
        wb3 = _sb("wb3", [128, 4096], BF16)
        btre = _sb("btre", [128, KC * NST], BF16)
        btim = _sb("btim", [128, KC * NST], BF16)
        pre = _sb("pre", [NST, CH], F32)
        pim = _sb("pim", [NST, CH], F32)
        qre = _sb("qre", [NST, CH], F32)
        qim = _sb("qim", [NST, CH], F32)
        cols = _sb("cols", [NST, 256], F32)
        tmpp = _sb("tmpp", [NST, 256], F32)
        sgre = _sb("sgre", [NST, NCORES], F32)
        sgim = _sb("sgim", [NST, NCORES], F32)
        selsb = _sb("selsb", [NST, 24], F32)
        dsb = _sb("dsb", [128, KC], F32)
        ones = _sb("ones", [128, 128], F32)
        ps0, ps1, ps2, ps3 = _ps("ps0"), _ps("ps1"), _ps("ps2"), _ps("ps3")
        q_in, q_t = _sem("s_in"), _sem("s_t")
        q_v, q_s, q_cc, q_st = _sem("s_v"), _sem("s_s"), _sem("s_cc"), _sem("s_st")
        q_ws = _sem("s_ws")
        q_wb = [_sem("s_wb%d" % i) for i in range(4)]
        q_wd = [_sem("s_wd%d" % i) for i in range(2)]
        q_g = _sem("s_g")
        q_cast4 = [_sem("s_cast%d" % i) for i in range(4)]
        q_cp = _sem("s_cp")
        q_u = [_sem("s_u%d" % i) for i in range(4)]
        block = _stk.enter_context(nc.Block())
        sems = {"in": q_in, "t": q_t, "v": q_v, "s": q_s,
                "cc": q_cc, "st": q_st, "ws": q_ws, "g": q_g,
                "cp": q_cp, "u0": q_u[0], "u1": q_u[1], "u2": q_u[2], "u3": q_u[3],
                "wb0": q_wb[0], "wb1": q_wb[1], "wb2": q_wb[2], "wb3": q_wb[3],
                "wd0": q_wd[0], "wd1": q_wd[1]}
        ps = [ps0, ps1, ps2, ps3]
        wb = [wb0, wb1, wb2, wb3]
        sgb = [sgb0, sgb1]
        xbre = acts[0:NST, 0:TPC]
        xbim = acts[0:NST, TPC:2 * TPC]
        ctre = acts[0:NST, 2 * TPC:2 * TPC + D]
        ctim = acts[0:NST, 2 * TPC + D:2 * TPC + 2 * D]
        wd = [wd0, wd1]
        scr = [scr0, scr1, scr2]

        colmap = {}

        def col(name):
            if name not in colmap:
                colmap[name] = len(colmap)
            i = colmap[name]
            return cols[:, i:i + 1]

        for g_ in range(4):
            E("sync", partial(_dma,
                              out=stream.ap()[:, g_ * 4096:(g_ + 1) * 4096]
                                  .rearrange("p (k t) -> p k t", t=TPC),
                              in_=u_r[:, 4 * g_:4 * g_ + 4, :], sem=q_u[g_]))
            cnt["u%d" % g_] += 16
        U_EVTS = [("u%d" % (k // 4), 16) for k in range(KC)]
        E("gpsimd", partial(_dma, out=selsb[:, :], in_=d_sel.ap(), sem=q_ws))
        cnt["ws"] += 16
        E("gpsimd", partial(_dma, out=col("lre"), in_=d_lre.ap(), sem=q_ws, slow=True))
        cnt["ws"] += 16
        E("gpsimd", partial(_dma, out=col("lim"), in_=d_lim.ap(), sem=q_ws, slow=True))
        cnt["ws"] += 16
        E("gpsimd", partial(_dma, out=col("ldt"), in_=d_ldt.ap(), sem=q_ws, slow=True))
        cnt["ws"] += 16
        E("gpsimd", partial(_dma, out=dsb[:, :], in_=dv_r, sem=q_ws, slow=True))
        cnt["ws"] += 16
        E("gpsimd", partial(_dma, out=btre.ap().rearrange("p (k n) -> p k n", n=NST),
                            in_=bre_r, sem=q_ws, slow=True))
        cnt["ws"] += 16
        E("gpsimd", partial(_dma, out=btim.ap().rearrange("p (k n) -> p k n", n=NST),
                            in_=bim_r, sem=q_ws, slow=True))
        cnt["ws"] += 16
        SMALLS = cnt["ws"]
        E("vector", partial(_memset, ap=ones[:, :], v=1.0, sem=q_v))
        cnt["v"] += 1
        ONES_DONE = cnt["v"]

        grank = {}
        drank = {}
        cast_list = []
        for q_ in range(4):
            for pr_ in range(8):
                cast_list.append(("g", q_, pr_, 0))
                cast_list.append(("g", q_, pr_, 1))
            for dp_ in range(8):
                for is_ in range(2):
                    cast_list.append(("d", q_, dp_, is_))
        pend = []
        for bi, ent in enumerate(cast_list):
            qj = q_cast4[bi % 4]
            if ent[0] == "g":
                _, q_, pr_, half_ = ent
                obase = (64 if half_ else 0) + 16 * q_ + 2 * pr_
                s_ = (32 if half_ else 0) + q_ * 8 + pr_
                grank[(q_, pr_, half_)] = bi + 1
                E("gpsimd", partial(
                    _dma,
                    out=d_wgub.ap()[:, s_ * 4096:(s_ + 1) * 4096]
                        .rearrange("p (k o) -> p k o", o=256),
                    in_=wgu_r[:, :, obase * 128:obase * 128 + 256], sem=qj))
            else:
                _, q_, dp_, is_ = ent
                ic0 = 16 * q_ + 8 * is_
                sd_ = q_ * 16 + dp_ * 2 + is_
                drank[(q_, dp_, is_)] = bi + 1
                E("gpsimd", partial(
                    _dma,
                    out=d_wdb.ap()[:, sd_ * 2048:(sd_ + 1) * 2048]
                        .rearrange("p (i d) -> p i d", d=256),
                    in_=wd_r[:, ic0:ic0 + 8, dp_ * 256:(dp_ + 1) * 256], sem=qj))
            cnt["cast%d" % (bi % 4)] += 16
            pend.append((qj, cnt["cast%d" % (bi % 4)]))
            if len(pend) >= 4:
                qk, vk = pend.pop(0)
                E("gpsimd", partial(_wait, sem=qk, v=vk))
                E("gpsimd", partial(lambda e, s: e.sem_inc(s, 1), s=q_cp))
                cnt["cp"] += 1
        while pend:
            qk, vk = pend.pop(0)
            E("gpsimd", partial(_wait, sem=qk, v=vk))
            E("gpsimd", partial(lambda e, s: e.sem_inc(s, 1), s=q_cp))
            cnt["cp"] += 1

        prepv = {}

        def emit_prep():
            W("scalar", "ws", SMALLS)
            E("scalar", partial(_act, out=col("t0"), in_=col("lre"), func=AF.Exp))
            E("scalar", partial(_act, out=col("sp"), in_=col("t0"), func=AF.Ln, bias=1.0))
            E("scalar", partial(_act, out=col("delta"), in_=col("ldt"), func=AF.Exp, sem=q_s))
            cnt["s"] += 1
            S_PREP1 = cnt["s"]
            if PREP_SKIP_VEC:
                E("vector", partial(_memset, ap=col("vdone"), v=0.0, sem=q_v))
                cnt["v"] += 1
                prepv["V_PREP"] = cnt["v"]
                return
            W("vector", "s", S_PREP1)
            drain_mode[0] = True
            E("vector", partial(_ts, out=col("lamre"), a=col("sp"), s1=-1.0, s2=-0.0101,
                                op0=ALU.mult, op1=ALU.add))
            E("vector", partial(_tt, out=col("rho"), a=col("lamre"), b=col("delta"), op=ALU.mult))
            E("vector", partial(_tt, out=col("omega"), a=col("lim"), b=col("delta"), op=ALU.mult))
            PI = math.pi
            for _ in range(2):
                E("vector", partial(_ts, out=col("kp"), a=col("omega"), s1=PI, s2=2 * PI,
                                    op0=ALU.is_gt, op1=ALU.mult))
                E("vector", partial(_ts, out=col("km"), a=col("omega"), s1=-PI, s2=2 * PI,
                                    op0=ALU.is_lt, op1=ALU.mult))
                E("vector", partial(_tt, out=col("omega"), a=col("omega"), b=col("kp"), op=ALU.subtract))
                E("vector", partial(_tt, out=col("omega"), a=col("omega"), b=col("km"), op=ALU.add, sem=q_v))
                cnt["v"] += 1
            V_OMEGA = cnt["v"]
            W("scalar", "v", V_OMEGA)
            E("scalar", partial(_act, out=col("emag"), in_=col("rho"), func=AF.Exp))
            E("scalar", partial(_act, out=col("einv"), in_=col("rho"), func=AF.Exp, scale=-1.0))
            E("scalar", partial(_act, out=col("sh"), in_=col("omega"), func=AF.Sin, scale=0.5))
            E("scalar", partial(_act, out=col("sh2"), in_=col("sh"), func=AF.Square))
            E("scalar", partial(_act, out=col("t1"), in_=col("sh2"), func=AF.Ln, scale=-1.0, bias=1.0))
            E("scalar", partial(_act, out=col("chalf"), in_=col("t1"), func=AF.Exp, scale=0.5, sem=q_s))
            cnt["s"] += 1
            S_PREP2 = cnt["s"]
            W("vector", "s", S_PREP2)
            E("vector", partial(_ts, out=col("cos"), a=col("sh2"), s1=-2.0, s2=1.0,
                                op0=ALU.mult, op1=ALU.add))
            E("vector", partial(_tt, out=col("t2"), a=col("sh"), b=col("chalf"), op=ALU.mult))
            E("vector", partial(_ts, out=col("sin"), a=col("t2"), s1=2.0, s2=None, op0=ALU.mult, op1=None))
            E("vector", partial(_tt, out=col("ar"), a=col("emag"), b=col("cos"), op=ALU.mult))
            E("vector", partial(_tt, out=col("ai"), a=col("emag"), b=col("sin"), op=ALU.mult))
            E("vector", partial(_tt, out=col("ivr"), a=col("einv"), b=col("cos"), op=ALU.mult))
            E("vector", partial(_stt, out=col("ivi"), in0=col("einv"), sc=-1.0, in1=col("sin"),
                                op0=ALU.mult, op1=ALU.mult))
            E("vector", partial(_ts, out=col("x"), a=col("ar"), s1=-1.0, s2=None,
                                op0=ALU.add, op1=None))
            E("vector", partial(_tt, out=col("d1"), a=col("lamre"), b=col("lamre"), op=ALU.mult))
            E("vector", partial(_tt, out=col("d2"), a=col("lim"), b=col("lim"), op=ALU.mult))
            E("vector", partial(_tt, out=col("den"), a=col("d1"), b=col("d2"), op=ALU.add))
            E("vector", partial(_recip, out=col("rden"), in_=col("den")))
            E("vector", partial(_ts, out=col("rden"), a=col("rden"), s1=2.0, s2=None, op0=ALU.mult, op1=None))
            E("vector", partial(_tt, out=col("t3"), a=col("x"), b=col("lamre"), op=ALU.mult))
            E("vector", partial(_tt, out=col("t4"), a=col("ai"), b=col("lim"), op=ALU.mult))
            E("vector", partial(_tt, out=col("t5"), a=col("t3"), b=col("t4"), op=ALU.add))
            E("vector", partial(_tt, out=col("fr"), a=col("t5"), b=col("rden"), op=ALU.mult))
            E("vector", partial(_tt, out=col("t3"), a=col("ai"), b=col("lamre"), op=ALU.mult))
            E("vector", partial(_tt, out=col("t4"), a=col("x"), b=col("lim"), op=ALU.mult))
            E("vector", partial(_tt, out=col("t5"), a=col("t3"), b=col("t4"), op=ALU.subtract))
            E("vector", partial(_tt, out=col("fi"), a=col("t5"), b=col("rden"), op=ALU.mult))

            def ladder(Pr, Pi, zr0, zi0, zpre):
                E("vector", partial(_memset, ap=Pr[:, 0:1], v=1.0))
                E("vector", partial(_memset, ap=Pi[:, 0:1], v=0.0))
                E("vector", partial(_copy, out=col(zpre + "r0"), in_=zr0))
                E("vector", partial(_copy, out=col(zpre + "i0"), in_=zi0))
                for k in range(7):
                    w_ = 1 << k
                    zr = col(zpre + "r%d" % k)
                    zi = col(zpre + "i%d" % k)
                    E("vector", partial(_ts, out=tmpp[:, 0:w_], a=Pr[:, 0:w_], s1=zr, s2=None,
                                        op0=ALU.mult, op1=None))
                    E("vector", partial(_ts, out=tmpp[:, 128:128 + w_], a=Pi[:, 0:w_], s1=zi, s2=None,
                                        op0=ALU.mult, op1=None))
                    E("vector", partial(_tt, out=Pr[:, w_:2 * w_], a=tmpp[:, 0:w_],
                                        b=tmpp[:, 128:128 + w_], op=ALU.subtract))
                    E("vector", partial(_ts, out=tmpp[:, 0:w_], a=Pr[:, 0:w_], s1=zi, s2=None,
                                        op0=ALU.mult, op1=None))
                    E("vector", partial(_ts, out=tmpp[:, 128:128 + w_], a=Pi[:, 0:w_], s1=zr, s2=None,
                                        op0=ALU.mult, op1=None))
                    E("vector", partial(_tt, out=Pi[:, w_:2 * w_], a=tmpp[:, 0:w_],
                                        b=tmpp[:, 128:128 + w_], op=ALU.add))
                    nzr = col(zpre + "r%d" % (k + 1))
                    nzi = col(zpre + "i%d" % (k + 1))
                    E("vector", partial(_tt, out=col("t3"), a=zr, b=zr, op=ALU.mult))
                    E("vector", partial(_tt, out=col("t4"), a=zi, b=zi, op=ALU.mult))
                    E("vector", partial(_tt, out=nzr, a=col("t3"), b=col("t4"), op=ALU.subtract))
                    E("vector", partial(_tt, out=col("t3"), a=zr, b=zi, op=ALU.mult))
                    E("vector", partial(_ts, out=nzi, a=col("t3"), s1=2.0, s2=None, op0=ALU.mult, op1=None))

            ladder(pre, pim, col("ar"), col("ai"), "za")
            ladder(qre, qim, col("ivr"), col("ivi"), "zb")

            def csq(srcr, srci, dstr, dsti):
                E("vector", partial(_tt, out=col("t3"), a=srcr, b=srcr, op=ALU.mult))
                E("vector", partial(_tt, out=col("t4"), a=srci, b=srci, op=ALU.mult))
                E("vector", partial(_tt, out=dstr, a=col("t3"), b=col("t4"), op=ALU.subtract))
                E("vector", partial(_tt, out=col("t3"), a=srcr, b=srci, op=ALU.mult))
                E("vector", partial(_ts, out=dsti, a=col("t3"), s1=2.0, s2=None, op0=ALU.mult, op1=None))

            csq(col("zar7"), col("zai7"), col("a256r"), col("a256i"))
            csq(col("a256r"), col("a256i"), col("a512r"), col("a512i"))
            csq(col("a512r"), col("a512i"), col("akr"), col("aki"))
            csq(col("akr"), col("aki"), col("ak2r"), col("ak2i"))
            E("vector", partial(_memset, ap=col("vdone"), v=0.0, sem=q_v))
            cnt["v"] += 1
            V_PREP = cnt["v"]
            drain_mode[0] = False

            prepv["V_PREP"] = V_PREP

        fills = []
        wd_fills = []

        lnstate = {}

        def emit_ln_stats_chunk(k, ready_evt, pe=False):
            if k == 0:
                lnstate.clear()
                lnstate["sxx_v"] = {}
                lnstate["pe_t"] = {}
                lnstate["pe"] = pe
            W("scalar", ready_evt[0], ready_evt[1])
            if k >= 1:
                if pe:
                    W("scalar", "t", lnstate["pe_t"][k - 1])
                else:
                    W("scalar", "v", lnstate["sxx_v"][k - 1])
            E("scalar", partial(_act, out=scr2[:, :],
                                in_=stream[:, k * 1024:(k + 1) * 1024],
                                func=AF.Square, sem=q_s))
            cnt["s"] += 1
            s_sq = cnt["s"]
            if pe:
                W("tensor", ready_evt[0], ready_evt[1])
                for tb in range(2):
                    E("tensor", partial(_mm, out=ps0[0:1, tb * TB:(tb + 1) * TB],
                                        l=ones[:, 0:1],
                                        r=stream[:, k * 1024 + tb * TB:k * 1024 + (tb + 1) * TB],
                                        st=(k == 0), sp=(k == KC - 1)))
                W("tensor", "s", s_sq)
                for tb in range(2):
                    E("tensor", partial(_mm, out=ps1[0:1, tb * TB:(tb + 1) * TB],
                                        l=ones[:, 0:1], r=scr2[:, tb * TB:(tb + 1) * TB],
                                        st=(k == 0), sp=(k == KC - 1),
                                        sem=(q_t if tb == 1 else None)))
                cnt["t"] += 1
                lnstate["pe_t"][k] = cnt["t"]
                return
            W("vector", ready_evt[0], ready_evt[1])
            if k == 0:
                E("vector", partial(_copy, out=scr0[:, :], in_=stream[:, 0:1024]))
            else:
                E("vector", partial(_tt, out=scr0[:, :], a=scr0[:, :],
                                    b=stream[:, k * 1024:(k + 1) * 1024], op=ALU.add))
            W("vector", "s", s_sq)
            if k == 0:
                E("vector", partial(_copy, out=scr1[:, :], in_=scr2[:, :], sem=q_v))
            else:
                E("vector", partial(_tt, out=scr1[:, :], a=scr1[:, :],
                                    b=scr2[:, :], op=ALU.add, sem=q_v))
            cnt["v"] += 1
            lnstate["sxx_v"][k] = cnt["v"]

        def emit_ln_tail():
            if lnstate.get("pe"):
                W("vector", "t", lnstate["pe_t"][KC - 1])
            else:
                v_sxx = cnt["v"]
                W("tensor", "v", v_sxx)
                for tb in range(2):
                    E("tensor", partial(_mm, out=ps0[0:1, tb * TB:(tb + 1) * TB],
                                        l=ones[:, 0:1], r=scr0[:, tb * TB:(tb + 1) * TB],
                                        st=True, sp=True))
                    E("tensor", partial(_mm, out=ps1[0:1, tb * TB:(tb + 1) * TB],
                                        l=ones[:, 0:1], r=scr1[:, tb * TB:(tb + 1) * TB],
                                        st=True, sp=True, sem=(q_t if tb == 1 else None)))
                cnt["t"] += 1
                t_stat = cnt["t"]
                W("vector", "t", t_stat)
            E("vector", partial(_ts, out=scr0[0:1, :], a=ps0[0:1, :], s1=1.0 / D, s2=None,
                                op0=ALU.mult, op1=None))
            E("vector", partial(_ts, out=scr1[0:1, :], a=ps1[0:1, :], s1=1.0 / D, s2=None,
                                op0=ALU.mult, op1=None))
            E("vector", partial(_tt, out=scr2[0:1, :], a=scr0[0:1, :], b=scr0[0:1, :], op=ALU.mult))
            E("vector", partial(_tt, out=scr1[0:1, :], a=scr1[0:1, :], b=scr2[0:1, :], op=ALU.subtract))
            E("vector", partial(_ts, out=scr1[0:1, :], a=scr1[0:1, :], s1=1.0, s2=EPS_LN,
                                op0=ALU.mult, op1=ALU.add, sem=q_v))
            cnt["v"] += 1
            v_fin = cnt["v"]
            W("scalar", "v", v_fin)
            E("scalar", partial(_act, out=scr1[0:1, :], in_=scr1[0:1, :], func=AF.Ln))
            E("scalar", partial(_act, out=scr1[0:1, :], in_=scr1[0:1, :], func=AF.Exp,
                                scale=-0.5, sem=q_s))
            cnt["s"] += 1
            s_rstd = cnt["s"]
            W("vector", "s", s_rstd)
            E("vector", partial(_tt, out=scr2[0:1, :], a=scr0[0:1, :], b=scr1[0:1, :],
                                op=ALU.mult, sem=q_v))
            cnt["v"] += 1
            v_mr = cnt["v"]
            W("tensor", "v", v_mr)
            W("tensor", "s", cnt["s"])
            for tb in range(2):
                E("tensor", partial(_mm, out=ps2[:, tb * TB:(tb + 1) * TB], l=ones[0:1, 0:128],
                                    r=scr1[0:1, tb * TB:(tb + 1) * TB], st=True, sp=True))
                E("tensor", partial(_mm, out=ps3[:, tb * TB:(tb + 1) * TB], l=ones[0:1, 0:128],
                                    r=scr2[0:1, tb * TB:(tb + 1) * TB], st=True, sp=True,
                                    sem=(q_t if tb == 1 else None)))
            cnt["t"] += 1
            t_bc = cnt["t"]
            W("vector", "t", t_bc)
            evts = []
            for k in range(KC):
                E("vector", partial(_tt, out=scr2[:, :], a=stream[:, k * 1024:(k + 1) * 1024],
                                    b=ps2[:, :], op=ALU.mult))
                E("vector", partial(_tt, out=lnx[:, k * 1024:(k + 1) * 1024], a=scr2[:, :],
                                    b=ps3[:, :], op=ALU.subtract, sem=q_v))
                cnt["v"] += 1
                evts.append(("v", cnt["v"]))
            return evts

        def emit_ffn(lnx_evts, store_out=False, fuse_stats=False):
            nonlocal fills, wd_fills
            fill_n = len(fills)
            silu_hist = []
            mult_hist = []
            for q in range(FFN_NQ):
                W("tensor", "v", cnt["v"])
                for pr in range(FFN_NPAIR):
                    i0 = 16 * q + 2 * pr
                    for half, obase in ((0, i0), (1, 64 + i0)):
                        slot = len(fills) % 4
                        if len(fills) >= 4:
                            W("sync", "t", fills[len(fills) - 4][1])
                        s_ = (32 if half else 0) + q * 8 + pr
                        W("sync", "cp", grank[(q, pr, half)])
                        sname = "wb%d" % slot
                        E("sync", partial(
                            _dma,
                            out=wb[slot].ap()[:, 0:8] if PERF_NODMA else wb[slot].ap(),
                            in_=d_wgub.ap()[:, s_ * 4096:s_ * 4096 + 8] if PERF_NODMA
                                else d_wgub.ap()[:, s_ * 4096:(s_ + 1) * 4096],
                            sem=sems[sname]))
                        cnt[sname] += 16
                        fills.append([slot, None, (sname, cnt[sname])])
                    gslot = fills[-2][0]
                    uslot = fills[-1][0]
                    for sub in range(2):
                        i = i0 + sub
                        for half in range(2):
                            slot = gslot if half == 0 else uslot
                            fi = len(fills) - 2 + half
                            pgrp = ps[2 * (i % 2) + half]
                            W("tensor", fills[fi][2][0], fills[fi][2][1])
                            perk = (q == 0 and pr == 0)
                            if not perk:
                                W("tensor", lnx_evts[KC - 1][0], lnx_evts[KC - 1][1])
                            if len(silu_hist) >= 2 and half == 0:
                                W("tensor", "s", silu_hist[-2])
                            if len(mult_hist) >= 2 and half == 1:
                                W("tensor", "v", mult_hist[-2])
                            for k in range(KC):
                                if perk:
                                    W("tensor", lnx_evts[k][0], lnx_evts[k][1])
                                for tb in range(2):
                                    E("tensor", partial(
                                        _mm,
                                        out=pgrp[:, tb * TB:(tb + 1) * TB],
                                        l=wb[slot][:, k * 256 + sub * 128:k * 256 + (sub + 1) * 128],
                                        r=lnx[:, k * 1024 + tb * TB:k * 1024 + (tb + 1) * TB],
                                        st=(k == 0), sp=(k == KC - 1),
                                        sem=(q_t if (k == KC - 1 and tb == 1) else None)))
                            cnt["t"] += 1
                            fills[fi][1] = cnt["t"]
                            t_grp = cnt["t"]
                            if half == 0:
                                W("scalar", "t", t_grp)
                                if len(mult_hist) >= 2:
                                    W("scalar", "v", mult_hist[-2])
                                E("scalar", partial(_act, out=sgb[i % 2][:, 0:8] if PERF_NOEPI else sgb[i % 2][:, :],
                                                    in_=pgrp[:, 0:8] if PERF_NOEPI else pgrp[:, :],
                                                    func=AF.Silu, sem=q_s))
                                cnt["s"] += 1
                                silu_hist.append(cnt["s"])
                            else:
                                W("vector", "s", silu_hist[-1])
                                W("vector", "t", t_grp)
                                ic_local = i - 16 * q
                                if PERF_NOEPI:
                                    E("vector", partial(_tt, out=acts[:, ic_local * 1024:ic_local * 1024 + 8],
                                                        a=sgb[i % 2][:, 0:8], b=pgrp[:, 0:8],
                                                        op=ALU.mult, sem=q_v))
                                else:
                                    E("vector", partial(_tt, out=acts[:, ic_local * 1024:(ic_local + 1) * 1024],
                                                        a=sgb[i % 2][:, :], b=pgrp[:, :],
                                                        op=ALU.mult, sem=q_v))
                                cnt["v"] += 1
                                mult_hist.append(cnt["v"])
                v_act_done = cnt["v"]
                s_silu_done = cnt["s"]
                if not FFN_M2:
                    continue
                prev_evac = {}
                for dp in range(8):
                    par = dp % 2
                    pAB = (ps[2 * par], ps[2 * par + 1])
                    for iset in range(2):
                        slot = len(wd_fills) % 2
                        if len(wd_fills) >= 2:
                            W("sync", "t", wd_fills[len(wd_fills) - 2][1])
                        sd_ = q * 16 + dp * 2 + iset
                        W("sync", "cp", drank[(q, dp, iset)])
                        sname = "wd%d" % slot
                        E("sync", partial(
                            _dma,
                            out=wd[slot].ap()[:, 0:8] if PERF_NODMA else wd[slot].ap(),
                            in_=d_wdb.ap()[:, sd_ * 2048:sd_ * 2048 + 8] if PERF_NODMA
                                else d_wdb.ap()[:, sd_ * 2048:(sd_ + 1) * 2048],
                            sem=sems[sname]))
                        cnt[sname] += 16
                        wd_fills.append([slot, None, cnt[sname]])
                        W("tensor", sname, cnt[sname])
                        W("tensor", "v", v_act_done)
                        W("tensor", "s", s_silu_done)
                        if par in prev_evac:
                            W("tensor", "v", prev_evac[par])
                        for i4 in range(8):
                            for tb in range(2):
                                for dc in range(2):
                                    first = (iset == 0 and i4 == 0)
                                    last = (iset == 1 and i4 == 7)
                                    ic_local = 8 * iset + i4
                                    E("tensor", partial(
                                        _mm,
                                        out=pAB[dc][:, tb * TB:(tb + 1) * TB],
                                        l=wd[slot][:, i4 * 256 + dc * 128:i4 * 256 + (dc + 1) * 128],
                                        r=acts[:, ic_local * 1024 + tb * TB:ic_local * 1024 + (tb + 1) * TB],
                                        st=first, sp=last,
                                        sem=(q_t if (i4 == 7 and tb == 1 and dc == 1) else None)))
                        cnt["t"] += 1
                        wd_fills[-1][1] = cnt["t"]
                    t_dp = cnt["t"]
                    W("vector", "t", t_dp)
                    for dc in range(2):
                        kk = 2 * dp + dc
                        E("vector", partial(_stt, out=stream[:, kk * 1024:(kk + 1) * 1024],
                                            in0=pAB[dc][:, :], sc=0.5,
                                            in1=stream[:, kk * 1024:(kk + 1) * 1024],
                                            op0=ALU.mult, op1=ALU.add,
                                            sem=(q_v if dc == 1 else None)))
                    cnt["v"] += 1
                    prev_evac[par] = cnt["v"]
                    if fuse_stats and q == FFN_NQ - 1:
                        for kk_ in (2 * dp, 2 * dp + 1):
                            emit_ln_stats_chunk(kk_, ("v", prev_evac[par]))
                    if store_out and STREAM_STORE and q == FFN_NQ - 1:
                        W("scalar", "v", cnt["v"])
                        E("scalar", partial(
                            _dma, out=y_r[:, 2 * dp:2 * dp + 2, :],
                            in_=stream.ap()[:, 2 * dp * 1024:(2 * dp + 2) * 1024]
                                .rearrange("p (k t) -> p k t", t=TPC),
                            sem=q_st))
                        cnt["st"] += 16
            return cnt["v"]

        def emit_ssm(lnx_evts, fuse_stats=False):
            t_entry = cnt["t"]
            W("gpsimd", "t", t_entry)
            E("gpsimd", partial(_dma, out=ctre, in_=d_cre.ap(), sem=q_ws))
            cnt["ws"] += 16
            E("gpsimd", partial(_dma, out=ctim, in_=d_cim.ap(), sem=q_ws))
            cnt["ws"] += 16
            CT_WS = cnt["ws"]
            W("scalar", "ws", CT_WS)
            E("scalar", partial(_act, out=ctim, in_=ctim, func=AF.Copy, scale=-1.0, sem=q_s))
            cnt["s"] += 1
            CT_NEG = cnt["s"]
            drain_mode[0] = True
            W("tensor", "ws", SMALLS)
            W("tensor", "s", cnt["s"])
            for comp, (bt, pg) in enumerate(((btre, ps0), (btim, ps1))):
                for tb in range(2):
                    for k in range(KC):
                        if comp == 0 and tb == 0:
                            W("tensor", lnx_evts[k][0], lnx_evts[k][1])
                        E("tensor", partial(
                            _mm, out=pg[0:NST, tb * TB:(tb + 1) * TB],
                            l=bt[:, k * NST:(k + 1) * NST],
                            r=lnx[:, k * 1024 + tb * TB:k * 1024 + (tb + 1) * TB],
                            st=(k == 0), sp=(k == KC - 1),
                            sem=(q_t if (comp == 1 and tb == 1 and k == KC - 1) else None)))
            cnt["t"] += 1
            t_g = cnt["t"]
            W("vector", "t", t_g)
            W("vector", "v", prepv["V_PREP"])
            E("vector", partial(_ts, out=bure[:, :], a=ps0[0:NST, :], s1=col("fr"), s2=None,
                                op0=ALU.mult, op1=None))
            E("vector", partial(_stt, out=scr0[0:NST, :], in0=ps1[0:NST, :], sc=col("fi"),
                                in1=bure[:, :], op0=ALU.mult, op1=ALU.subtract))
            E("vector", partial(_ts, out=scr0[0:NST, :], a=scr0[0:NST, :], s1=-1.0, s2=None,
                                op0=ALU.mult, op1=None))
            E("vector", partial(_ts, out=buim[:, :], a=ps1[0:NST, :], s1=col("fr"), s2=None,
                                op0=ALU.mult, op1=None))
            E("vector", partial(_stt, out=buim[:, :], in0=ps0[0:NST, :], sc=col("fi"),
                                in1=buim[:, :], op0=ALU.mult, op1=ALU.add))
            E("vector", partial(_copy, out=bure[:, :], in_=scr0[0:NST, :], sem=q_v))
            cnt["v"] += 1

            drain_mode[0] = False

            def DR():
                E("vector", _drain)

            SLS = [slice(j * CH, (j + 1) * CH) for j in range(NCH_S)]
            for sl in SLS:
                E("vector", partial(_tt, out=scr2[0:NST, sl], a=qim[:, :], b=buim[:, sl], op=ALU.mult))
            DR()
            for sl in SLS:
                E("vector", partial(_tt, out=scr0[0:NST, sl], a=qre[:, :], b=bure[:, sl], op=ALU.mult))
            DR()
            for sl in SLS:
                E("vector", partial(_tt, out=scr0[0:NST, sl], a=scr0[0:NST, sl], b=scr2[0:NST, sl], op=ALU.subtract))
            for sl in SLS:
                E("vector", partial(_tt, out=scr2[0:NST, sl], a=qim[:, :], b=bure[:, sl], op=ALU.mult))
            for sl in SLS:
                E("vector", partial(_tt, out=scr1[0:NST, sl], a=qre[:, :], b=buim[:, sl], op=ALU.mult))
            DR()
            for sl in SLS:
                E("vector", partial(_tt, out=scr1[0:NST, sl], a=scr1[0:NST, sl], b=scr2[0:NST, sl], op=ALU.add))
            DR()
            for j, sl in enumerate(SLS):
                E("vector", partial(_scan, out=sre[:, sl], d0=scr0[0:NST, sl], d1=scr0[0:NST, sl],
                                    init=0.0, op0=ALU.add, op1=ALU.bypass))
            for j, sl in enumerate(SLS):
                E("vector", partial(_scan, out=sim[:, sl], d0=scr1[0:NST, sl], d1=scr1[0:NST, sl],
                                    init=0.0, op0=ALU.add, op1=ALU.bypass))
            DR()
            for j in range(NCH_S):
                lc = slice(j * CH + CH - 1, j * CH + CH)
                E("vector", partial(_tt, out=tmpp[:, 4 * j:4 * j + 1], a=pre[:, CH - 1:CH], b=sre[:, lc], op=ALU.mult))
                E("vector", partial(_tt, out=tmpp[:, 4 * j + 1:4 * j + 2], a=pim[:, CH - 1:CH], b=sim[:, lc], op=ALU.mult))
                E("vector", partial(_tt, out=tmpp[:, 4 * j + 2:4 * j + 3], a=pre[:, CH - 1:CH], b=sim[:, lc], op=ALU.mult))
                E("vector", partial(_tt, out=tmpp[:, 4 * j + 3:4 * j + 4], a=pim[:, CH - 1:CH], b=sre[:, lc], op=ALU.mult))
            DR()
            for j in range(NCH_S):
                E("vector", partial(_tt, out=col("clr%d" % j), a=tmpp[:, 4 * j:4 * j + 1],
                                    b=tmpp[:, 4 * j + 1:4 * j + 2], op=ALU.subtract))
                E("vector", partial(_tt, out=col("cli%d" % j), a=tmpp[:, 4 * j + 2:4 * j + 3],
                                    b=tmpp[:, 4 * j + 3:4 * j + 4], op=ALU.add))
            E("vector", partial(_memset, ap=col("glr0"), v=0.0))
            E("vector", partial(_memset, ap=col("gli0"), v=0.0))
            DR()
            for j in range(1, NCH_S):
                pj = j - 1
                E("vector", partial(_tt, out=col("t3"), a=col("zar7"), b=col("clr%d" % pj), op=ALU.mult))
                E("vector", partial(_tt, out=col("t4"), a=col("zai7"), b=col("cli%d" % pj), op=ALU.mult))
                E("vector", partial(_tt, out=col("t5"), a=col("zar7"), b=col("cli%d" % pj), op=ALU.mult))
                E("vector", partial(_tt, out=col("t6"), a=col("zai7"), b=col("clr%d" % pj), op=ALU.mult))
                E("vector", partial(_tt, out=col("g3"), a=col("ar"), b=col("clr%d" % pj), op=ALU.mult))
                E("vector", partial(_tt, out=col("g4"), a=col("ai"), b=col("cli%d" % pj), op=ALU.mult))
                E("vector", partial(_tt, out=col("g5"), a=col("ar"), b=col("cli%d" % pj), op=ALU.mult))
                E("vector", partial(_tt, out=col("g6"), a=col("ai"), b=col("clr%d" % pj), op=ALU.mult))
                DR()
                E("vector", partial(_tt, out=col("t3"), a=col("t3"), b=col("t4"), op=ALU.subtract))
                E("vector", partial(_tt, out=col("t5"), a=col("t5"), b=col("t6"), op=ALU.add))
                E("vector", partial(_tt, out=col("glr%d" % j), a=col("g3"), b=col("g4"), op=ALU.subtract))
                E("vector", partial(_tt, out=col("gli%d" % j), a=col("g5"), b=col("g6"), op=ALU.add))
                DR()
                E("vector", partial(_tt, out=col("clr%d" % j), a=col("clr%d" % j), b=col("t3"), op=ALU.add))
                E("vector", partial(_tt, out=col("cli%d" % j), a=col("cli%d" % j), b=col("t5"), op=ALU.add))
                DR()
            E("vector", partial(_copy, out=col("scr_r"), in_=col("clr%d" % (NCH_S - 1))))
            E("vector", partial(_copy, out=col("scr_i"), in_=col("cli%d" % (NCH_S - 1)), sem=q_v))
            cnt["v"] += 1
            v_state = cnt["v"]
            E("vector", _drain)

            W("gpsimd", "v", v_state)
            E("gpsimd", partial(_dma, out=d_ccin[0:NST, 0:1], in_=col("scr_r"), sem=q_g, slow=True))
            cnt["g"] += 16
            E("gpsimd", partial(_dma, out=d_ccin[NST:2 * NST, 0:1], in_=col("scr_i"), sem=q_g, slow=True))
            cnt["g"] += 16
            W("gpsimd", "g", cnt["g"])
            if PERF_NOCC:
                E("gpsimd", partial(_dma, out=d_ccout.ap()[0:2 * NST, :], in_=d_ccin.ap(), sem=q_cc, n=16))
                cnt["cc"] += 16
            else:
                E("gpsimd", partial(_cc, cin=d_ccin.ap().opt(), cout=d_ccout.ap().opt(), sem=q_cc))
                cnt["cc"] += 1
            W("gpsimd", "cc", cnt["cc"])
            E("gpsimd", partial(_dma, out=sgre[:, :], in_=ccout_r[0:NST, :], sem=q_g, slow=True))
            cnt["g"] += 16
            E("gpsimd", partial(_dma, out=sgim[:, :], in_=ccout_r[NST:2 * NST, :], sem=q_g, slow=True))
            cnt["g"] += 16
            w_gather = cnt["g"]

            drain_mode[0] = True
            W("vector", "g", w_gather)
            W("vector", "in", cnt["in"])
            for m in range(3):
                E("vector", partial(_stt, out=tmpp[:, 64:64 + NCORES], in0=sgre[:, :], sc=1.0,
                                    in1=selsb[:, m * 8:(m + 1) * 8], op0=ALU.mult, op1=ALU.mult,
                                    accum=col("Tr%d" % m)))
                E("vector", partial(_stt, out=tmpp[:, 64:64 + NCORES], in0=sgim[:, :], sc=1.0,
                                    in1=selsb[:, m * 8:(m + 1) * 8], op0=ALU.mult, op1=ALU.mult,
                                    accum=col("Ti%d" % m)))
            E("vector", partial(_copy, out=col("Ir"), in_=col("Tr0")))
            E("vector", partial(_copy, out=col("Ii"), in_=col("Ti0")))
            for m, (wr, wi) in enumerate((("akr", "aki"), ("ak2r", "ak2i")), start=1):
                E("vector", partial(_tt, out=col("t3"), a=col(wr), b=col("Tr%d" % m), op=ALU.mult))
                E("vector", partial(_tt, out=col("t4"), a=col(wi), b=col("Ti%d" % m), op=ALU.mult))
                E("vector", partial(_tt, out=col("t3"), a=col("t3"), b=col("t4"), op=ALU.subtract))
                E("vector", partial(_tt, out=col("Ir"), a=col("Ir"), b=col("t3"), op=ALU.add))
                E("vector", partial(_tt, out=col("t3"), a=col(wr), b=col("Ti%d" % m), op=ALU.mult))
                E("vector", partial(_tt, out=col("t4"), a=col(wi), b=col("Tr%d" % m), op=ALU.mult))
                E("vector", partial(_tt, out=col("t3"), a=col("t3"), b=col("t4"), op=ALU.add))
                E("vector", partial(_tt, out=col("Ii"), a=col("Ii"), b=col("t3"), op=ALU.add))
            drain_mode[0] = False
            E("vector", partial(_tt, out=col("t3"), a=col("ar"), b=col("Ir"), op=ALU.mult))
            E("vector", partial(_tt, out=col("t4"), a=col("ai"), b=col("Ii"), op=ALU.mult))
            E("vector", partial(_tt, out=col("t5"), a=col("ar"), b=col("Ii"), op=ALU.mult))
            E("vector", partial(_tt, out=col("t6"), a=col("ai"), b=col("Ir"), op=ALU.mult))
            DR()
            E("vector", partial(_tt, out=col("hr0"), a=col("t3"), b=col("t4"), op=ALU.subtract))
            E("vector", partial(_tt, out=col("hi0"), a=col("t5"), b=col("t6"), op=ALU.add))
            DR()
            for j in range(1, NCH_S):
                pj = j - 1
                E("vector", partial(_tt, out=col("t3"), a=col("zar7"), b=col("hr%d" % pj), op=ALU.mult))
                E("vector", partial(_tt, out=col("t4"), a=col("zai7"), b=col("hi%d" % pj), op=ALU.mult))
                E("vector", partial(_tt, out=col("t5"), a=col("zar7"), b=col("hi%d" % pj), op=ALU.mult))
                E("vector", partial(_tt, out=col("t6"), a=col("zai7"), b=col("hr%d" % pj), op=ALU.mult))
                DR()
                E("vector", partial(_tt, out=col("hr%d" % j), a=col("t3"), b=col("t4"), op=ALU.subtract))
                E("vector", partial(_tt, out=col("hi%d" % j), a=col("t5"), b=col("t6"), op=ALU.add))
                DR()
            for j in range(NCH_S):
                E("vector", partial(_tt, out=col("wr%d" % j), a=col("glr%d" % j), b=col("hr%d" % j), op=ALU.add))
                E("vector", partial(_tt, out=col("wi%d" % j), a=col("gli%d" % j), b=col("hi%d" % j), op=ALU.add))
            DR()
            for j, sl in enumerate(SLS):
                E("vector", partial(_ts, out=scr0[0:NST, sl], a=sre[:, sl], s1=col("wr%d" % j), s2=None,
                                    op0=ALU.add, op1=None))
            for j, sl in enumerate(SLS):
                E("vector", partial(_ts, out=scr1[0:NST, sl], a=sim[:, sl], s1=col("wi%d" % j), s2=None,
                                    op0=ALU.add, op1=None))
            DR()
            for sl in SLS:
                E("vector", partial(_tt, out=bure[:, sl], a=pre[:, :], b=scr0[0:NST, sl], op=ALU.mult))
            for sl in SLS:
                E("vector", partial(_tt, out=buim[:, sl], a=pre[:, :], b=scr1[0:NST, sl], op=ALU.mult))
            DR()
            for sl in SLS:
                E("vector", partial(_tt, out=scr2[0:NST, sl], a=pim[:, :], b=scr1[0:NST, sl], op=ALU.mult))
            DR()
            for sl in SLS:
                E("vector", partial(_tt, out=bure[:, sl], a=bure[:, sl], b=scr2[0:NST, sl], op=ALU.subtract))
            for sl in SLS:
                E("vector", partial(_tt, out=scr2[0:NST, sl], a=pim[:, :], b=scr0[0:NST, sl], op=ALU.mult))
            DR()
            for sl in SLS:
                E("vector", partial(_tt, out=buim[:, sl], a=buim[:, sl], b=scr2[0:NST, sl], op=ALU.add))
            DR()
            E("vector", partial(_copy, out=xbre, in_=bure[:, :]))
            E("vector", partial(_copy, out=xbim, in_=buim[:, :], sem=q_v))
            cnt["v"] += 1
            v_xs = cnt["v"]

            W("tensor", "ws", CT_WS)
            W("tensor", "s", CT_NEG)
            prev_pe = {}
            for kk in range(KC):
                pg = ps[2 + (kk % 2)]
                W("tensor", "v", v_xs)
                if kk >= 2:
                    W("tensor", "v", prev_pe[kk % 2])
                for tb in range(2):
                    E("tensor", partial(_mm, out=pg[:, tb * TB:(tb + 1) * TB],
                                        l=ctre[:, kk * 128:(kk + 1) * 128],
                                        r=xbre[:, tb * TB:(tb + 1) * TB], st=True, sp=False))
                    E("tensor", partial(_mm, out=pg[:, tb * TB:(tb + 1) * TB],
                                        l=ctim[:, kk * 128:(kk + 1) * 128],
                                        r=xbim[:, tb * TB:(tb + 1) * TB], st=False, sp=True,
                                        sem=(q_t if tb == 1 else None)))
                cnt["t"] += 1
                W("vector", "t", cnt["t"])
                E("vector", partial(_stt, out=pg[:, :], in0=lnx[:, kk * 1024:(kk + 1) * 1024],
                                    sc=dsb[:, kk:kk + 1], in1=pg[:, :], op0=ALU.mult, op1=ALU.add))
                E("vector", partial(_tt, out=stream[:, kk * 1024:(kk + 1) * 1024],
                                    a=stream[:, kk * 1024:(kk + 1) * 1024], b=pg[:, :],
                                    op=ALU.add, sem=q_v))
                cnt["v"] += 1
                prev_pe[kk % 2] = cnt["v"]
                if fuse_stats:
                    drain_mode[0] = False
                    emit_ln_stats_chunk(kk, ("v", cnt["v"]))
            v_ssm = cnt["v"]
            drain_mode[0] = False
            return v_ssm


        if STAGE >= 2:
            for k_ in range(KC):
                emit_ln_stats_chunk(k_, U_EVTS[k_])
            lnx1_evts = emit_ln_tail()
            emit_prep()
        else:
            emit_prep()
        if STAGE >= 3:
            emit_ffn(lnx1_evts, fuse_stats=(STAGE >= 4))
        if STAGE >= 4:
            lnx2_evts = emit_ln_tail()
            emit_ssm(lnx2_evts, fuse_stats=(STAGE >= 5))
        stored = False
        if STAGE >= 5:
            lnx3_evts = emit_ln_tail()
            emit_ffn(lnx3_evts, store_out=(FFN_NQ == 4 and FFN_M2))
            stored = FFN_NQ == 4 and FFN_M2 and STREAM_STORE


        if not stored:
            W("scalar", "v", cnt["v"])
            E("scalar", partial(_dma, out=y_r,
                              in_=stream.ap().rearrange("p (k t) -> p k t", t=TPC),
                              sem=q_st))
            cnt["st"] += 16
        if DEBUG_SSM:
            d_dbg = nc.dram_tensor("dbg", [128, 2048], F32, kind="ExternalOutput")
            dumps = [
                (d_dbg.ap()[0:32, 0:128], cols[:, 0:128]),
                (d_dbg.ap()[0:32, 128:256], pre[:, :]),
                (d_dbg.ap()[0:32, 256:384], pim[:, :]),
                (d_dbg.ap()[0:32, 384:512], qre[:, :]),
                (d_dbg.ap()[0:32, 512:640], qim[:, :]),
                (d_dbg.ap()[0:32, 640:648], sgre[:, :]),
                (d_dbg.ap()[0:32, 648:656], sgim[:, :]),
                (d_dbg.ap()[32:64, 0:1024], sre[:, :]),
                (d_dbg.ap()[32:64, 1024:2048], sim[:, :]),
                (d_dbg.ap()[64:96, 0:1024], bure[:, :]),
                (d_dbg.ap()[64:96, 1024:2048], buim[:, :]),
            ]
            for o_, i_ in dumps:
                E("scalar", partial(_dma, out=o_, in_=i_, sem=q_st))
                cnt["st"] += 16
        W("scalar", "st", cnt["st"])

        def flush(ops):
            def body(e):
                for f in ops:
                    f(e)
            return body

        block.sync(flush(prog["sync"]))
        block.gpsimd(flush(prog["gpsimd"]))
        block.tensor(flush(prog["tensor"]))
        block.vector(flush(prog["vector"]))
        block.scalar(flush(prog["scalar"]))

    return nc




_NC_CACHE = [None]


def kernel(L0_attn_u, L0_attn_Lambda_unconstrained_re, L0_attn_Lambda_im,
           L0_attn_B_re, L0_attn_B_im, L0_attn_C_re, L0_attn_C_im, L0_attn_D,
           L0_attn_log_Delta, L0_ffn_w_gate_up, L0_ffn_w_down):
    u = np.asarray(L0_attn_u, dtype=np.float32)
    wguT = np.ascontiguousarray(np.asarray(L0_ffn_w_gate_up, dtype=np.float32).T)
    wdT = np.ascontiguousarray(np.asarray(L0_ffn_w_down, dtype=np.float32).T)
    breT = np.ascontiguousarray(np.asarray(L0_attn_B_re, dtype=np.float32).T)
    bimT = np.ascontiguousarray(np.asarray(L0_attn_B_im, dtype=np.float32).T)
    creT = np.ascontiguousarray(np.asarray(L0_attn_C_re, dtype=np.float32).T)
    cimT = np.ascontiguousarray(np.asarray(L0_attn_C_im, dtype=np.float32).T)
    dvec = np.ascontiguousarray(np.asarray(L0_attn_D, dtype=np.float32).reshape(KC, 128))
    lre = np.ascontiguousarray(np.asarray(
        L0_attn_Lambda_unconstrained_re, dtype=np.float32).reshape(NST, 1))
    lim = np.ascontiguousarray(np.asarray(L0_attn_Lambda_im, dtype=np.float32).reshape(NST, 1))
    ldt = np.ascontiguousarray(np.asarray(L0_attn_log_Delta, dtype=np.float32)[:NST].reshape(NST, 1))

    if _NC_CACHE[0] is None:
        _NC_CACHE[0] = build_nc()
    nc = _NC_CACHE[0]

    in_maps = []
    for c in range(NCORES):
        b, s = divmod(c, 4)
        u_core = np.ascontiguousarray(u[b, s * TPC:(s + 1) * TPC, :].T)
        selm = np.zeros((NST, 24), np.float32)
        for m in range(3):
            jloc = s - 1 - m
            if jloc >= 0:
                selm[:, m * 8 + (b * 4 + jloc)] = 1.0
        in_maps.append({
            "u": u_core, "wgu": wguT, "wd": wdT,
            "bre": breT, "bim": bimT, "cre": creT, "cim": cimT,
            "dvec": dvec, "lre": lre, "lim": lim, "ldt": ldt, "selm": selm,
        })
    res = run_bass_kernel_spmd(nc, in_maps, core_ids=list(range(NCORES)))
    _NC_CACHE.append(res)
    out = np.empty((2, 4096, D), np.float32)
    for c in range(NCORES):
        b, s = divmod(c, 4)
        out[b, s * TPC:(s + 1) * TPC, :] = np.asarray(res.results[c]["y"]).T
    return out


# version 11
# speedup vs baseline: 1.0058x; 1.0046x over previous
import math
from functools import partial

import numpy as np
import concourse.bass as bass
import concourse.mybir as mybir
from concourse.bass_utils import run_bass_kernel_spmd

F32 = mybir.dt.float32
BF16 = mybir.dt.bfloat16
AF = mybir.ActivationFunctionType
ALU = mybir.AluOpType

NCORES = 8
TPC = 1024
D = 2048
KC = 16
ICH = 64
NST = 32
CH = 128
NCH_S = TPC // CH
EPS_LN = 1e-5
TB = 512


def _wait(e, sem, v):
    e.wait_ge(sem, v)


def _mm(e, out, l, r, st, sp, sem=None, n=1):
    i = e.matmul(out, l, r, start=st, stop=sp)
    if sem is not None:
        i.then_inc(sem, n)


def _dma(e, out, in_, sem=None, n=16, slow=False):
    kw = {}
    if slow:
        kw["allow_slow_non_contiguous"] = True
    i = e.dma_start(out=out, in_=in_, **kw)
    if sem is not None:
        i.then_inc(sem, n)


def _tt(e, out, a, b, op, sem=None):
    i = e.tensor_tensor(out, a, b, op)
    if sem is not None:
        i.then_inc(sem, 1)


def _ts(e, out, a, s1, s2, op0, op1, sem=None):
    if s2 is None:
        s2, op1 = 0.0, ALU.add
    i = e.tensor_scalar(out, a, s1, s2, op0, op1)
    if sem is not None:
        i.then_inc(sem, 1)


def _stt(e, out, in0, sc, in1, op0, op1, accum=None, sem=None):
    i = e.scalar_tensor_tensor(out, in0, sc, in1, op0, op1, accum_out=accum)
    if sem is not None:
        i.then_inc(sem, 1)


def _act(e, out, in_, func, bias=None, scale=None, sem=None):
    kw = {}
    if bias is not None:
        kw["bias"] = bias
    if scale is not None:
        kw["scale"] = scale
    i = e.activation(out, in_, func, **kw)
    if sem is not None:
        i.then_inc(sem, 1)


def _scan(e, out, d0, d1, init, op0, op1, sem=None):
    i = e.tensor_tensor_scan(out, d0, d1, init, op0, op1)
    if sem is not None:
        i.then_inc(sem, 1)


def _copy(e, out, in_, sem=None):
    i = e.tensor_copy(out, in_)
    if sem is not None:
        i.then_inc(sem, 1)


def _recip(e, out, in_, sem=None):
    i = e.reciprocal(out, in_)
    if sem is not None:
        i.then_inc(sem, 1)


def _memset(e, ap, v, sem=None):
    i = e.memset(ap, v)
    if sem is not None:
        i.then_inc(sem, 1)


def _cc(e, cin, cout, sem):
    e.collective_compute(
        "AllGather", ALU.bypass,
        replica_groups=[[0, 1, 2, 3], [4, 5, 6, 7]],
        ins=[cin], outs=[cout],
    ).then_inc(sem, 1)


STAGE = 5
FFN_NQ = 4
FFN_NPAIR = 8
FFN_M2 = True
DEBUG_SSM = False
STREAM_STORE = True
PERF_NOCC = False
PREP_AFTER_LN1 = True
PREP_SKIP_VEC = False
PERF_NOEPI = False
PERF_NODMA = False


def build_nc():
    nc = bass.Bass("TRN2", target_bir_lowering=False,
                   detect_race_conditions=False)

    d_u = nc.dram_tensor("u", [D, TPC], F32, kind="ExternalInput")
    d_wgu = nc.dram_tensor("wgu", [D, 8 * D], F32, kind="ExternalInput")
    d_wd = nc.dram_tensor("wd", [4 * D, D], F32, kind="ExternalInput")
    d_bre = nc.dram_tensor("bre", [D, NST], F32, kind="ExternalInput")
    d_bim = nc.dram_tensor("bim", [D, NST], F32, kind="ExternalInput")
    d_cre = nc.dram_tensor("cre", [NST, D], F32, kind="ExternalInput")
    d_cim = nc.dram_tensor("cim", [NST, D], F32, kind="ExternalInput")
    d_dv = nc.dram_tensor("dvec", [KC, 128], F32, kind="ExternalInput")
    d_lre = nc.dram_tensor("lre", [NST, 1], F32, kind="ExternalInput")
    d_lim = nc.dram_tensor("lim", [NST, 1], F32, kind="ExternalInput")
    d_ldt = nc.dram_tensor("ldt", [NST, 1], F32, kind="ExternalInput")
    d_sel = nc.dram_tensor("selm", [NST, 12], F32, kind="ExternalInput")
    d_y = nc.dram_tensor("y", [D, TPC], F32, kind="ExternalOutput")
    d_ccin = nc.dram_tensor("cc_in", [2 * NST, 1], F32)
    d_ccout = nc.dram_tensor("cc_out", [4 * 2 * NST, 1], F32)
    d_wgub = nc.dram_tensor("wgub", [128, 64 * 4096], BF16)
    d_wdb = nc.dram_tensor("wdb", [128, 64 * 2048], BF16)

    u_r = d_u.ap().rearrange("(k p) t -> p k t", p=128)
    y_r = d_y.ap().rearrange("(k p) t -> p k t", p=128)
    wgu_r = d_wgu.ap().rearrange("(k p) o -> p k o", p=128)
    wd_r = d_wd.ap().rearrange("(i p) d -> p i d", p=128)
    bre_r = d_bre.ap().rearrange("(k p) n -> p k n", p=128)
    bim_r = d_bim.ap().rearrange("(k p) n -> p k n", p=128)
    dv_r = d_dv.ap().rearrange("k p -> p k")
    ccout_r = d_ccout.ap().rearrange("(j n) one -> n (j one)", n=2 * NST)

    prog = {e: [] for e in ("sync", "gpsimd", "tensor", "vector", "scalar")}
    cnt = {"in": 0, "t": 0, "v": 0, "s": 0, "cc": 0, "st": 0, "ws": 0, "g": 0,
           "wb0": 0, "wb1": 0, "wb2": 0, "wb3": 0, "wd0": 0, "wd1": 0, "cp": 0,
           "u0": 0, "u1": 0, "u2": 0, "u3": 0,
           "cast0": 0, "cast1": 0, "cast2": 0, "cast3": 0}
    hw = {e: dict(cnt) for e in prog}

    drain_mode = [False]

    def _drain(e):
        e.drain()

    def E(eng, fn):
        prog[eng].append(fn)
        if eng == "vector" and drain_mode[0]:
            prog[eng].append(_drain)

    def W(eng, sem, val):
        if val > hw[eng][sem]:
            hw[eng][sem] = val
            E(eng, partial(_wait, sem=sems[sem], v=val))

    import contextlib
    with contextlib.ExitStack() as _stk:
        def _sb(name, shape, dt):
            return _stk.enter_context(nc.sbuf_tensor(name, shape, dt))

        def _ps(name):
            return _stk.enter_context(nc.psum_tensor(name, [128, 1024], F32))

        def _sem(name):
            return _stk.enter_context(nc.semaphore(name))

        stream = _sb("stream", [128, KC * TPC], F32)
        lnx = _sb("lnx", [128, KC * TPC], BF16)
        acts = _sb("acts", [128, 16 * TPC], BF16)
        wb0 = _sb("wb0", [128, 4096], BF16)
        wb1 = _sb("wb1", [128, 4096], BF16)
        wb2 = _sb("wb2", [128, 4096], BF16)
        wd0 = _sb("wd0", [128, 2048], BF16)
        wd1 = _sb("wd1", [128, 2048], BF16)
        scr0 = _sb("scr0", [128, 1024], F32)
        scr1 = _sb("scr1", [128, 1024], F32)
        scr2 = _sb("scr2", [128, 1024], F32)
        bure = _sb("bure", [NST, TPC], F32)
        buim = _sb("buim", [NST, TPC], F32)
        sre = _sb("sre", [NST, TPC], F32)
        sim = _sb("sim", [NST, TPC], F32)
        sgb0 = _sb("sgb0", [128, 1024], BF16)
        sgb1 = _sb("sgb1", [128, 1024], BF16)
        wb3 = _sb("wb3", [128, 4096], BF16)
        btre = _sb("btre", [128, KC * NST], BF16)
        btim = _sb("btim", [128, KC * NST], BF16)
        pre = _sb("pre", [NST, CH], F32)
        pim = _sb("pim", [NST, CH], F32)
        qre = _sb("qre", [NST, CH], F32)
        qim = _sb("qim", [NST, CH], F32)
        cols = _sb("cols", [NST, 256], F32)
        tmpp = _sb("tmpp", [NST, 256], F32)
        sgre = _sb("sgre", [NST, 4], F32)
        sgim = _sb("sgim", [NST, 4], F32)
        selsb = _sb("selsb", [NST, 12], F32)
        dsb = _sb("dsb", [128, KC], F32)
        ones = _sb("ones", [128, 128], F32)
        ps0, ps1, ps2, ps3 = _ps("ps0"), _ps("ps1"), _ps("ps2"), _ps("ps3")
        q_in, q_t = _sem("s_in"), _sem("s_t")
        q_v, q_s, q_cc, q_st = _sem("s_v"), _sem("s_s"), _sem("s_cc"), _sem("s_st")
        q_ws = _sem("s_ws")
        q_wb = [_sem("s_wb%d" % i) for i in range(4)]
        q_wd = [_sem("s_wd%d" % i) for i in range(2)]
        q_g = _sem("s_g")
        q_cast4 = [_sem("s_cast%d" % i) for i in range(4)]
        q_cp = _sem("s_cp")
        q_u = [_sem("s_u%d" % i) for i in range(4)]
        block = _stk.enter_context(nc.Block())
        sems = {"in": q_in, "t": q_t, "v": q_v, "s": q_s,
                "cc": q_cc, "st": q_st, "ws": q_ws, "g": q_g,
                "cp": q_cp, "u0": q_u[0], "u1": q_u[1], "u2": q_u[2], "u3": q_u[3],
                "wb0": q_wb[0], "wb1": q_wb[1], "wb2": q_wb[2], "wb3": q_wb[3],
                "wd0": q_wd[0], "wd1": q_wd[1]}
        ps = [ps0, ps1, ps2, ps3]
        wb = [wb0, wb1, wb2, wb3]
        sgb = [sgb0, sgb1]
        xbre = acts[0:NST, 0:TPC]
        xbim = acts[0:NST, TPC:2 * TPC]
        ctre = acts[0:NST, 2 * TPC:2 * TPC + D]
        ctim = acts[0:NST, 2 * TPC + D:2 * TPC + 2 * D]
        wd = [wd0, wd1]
        scr = [scr0, scr1, scr2]

        colmap = {}

        def col(name):
            if name not in colmap:
                colmap[name] = len(colmap)
            i = colmap[name]
            return cols[:, i:i + 1]

        for g_ in range(4):
            E("sync", partial(_dma,
                              out=stream.ap()[:, g_ * 4096:(g_ + 1) * 4096]
                                  .rearrange("p (k t) -> p k t", t=TPC),
                              in_=u_r[:, 4 * g_:4 * g_ + 4, :], sem=q_u[g_]))
            cnt["u%d" % g_] += 16
        U_EVTS = [("u%d" % (k // 4), 16) for k in range(KC)]
        E("gpsimd", partial(_dma, out=selsb[:, :], in_=d_sel.ap(), sem=q_ws))
        cnt["ws"] += 16
        E("gpsimd", partial(_dma, out=col("lre"), in_=d_lre.ap(), sem=q_ws, slow=True))
        cnt["ws"] += 16
        E("gpsimd", partial(_dma, out=col("lim"), in_=d_lim.ap(), sem=q_ws, slow=True))
        cnt["ws"] += 16
        E("gpsimd", partial(_dma, out=col("ldt"), in_=d_ldt.ap(), sem=q_ws, slow=True))
        cnt["ws"] += 16
        E("gpsimd", partial(_dma, out=dsb[:, :], in_=dv_r, sem=q_ws, slow=True))
        cnt["ws"] += 16
        E("gpsimd", partial(_dma, out=btre.ap().rearrange("p (k n) -> p k n", n=NST),
                            in_=bre_r, sem=q_ws, slow=True))
        cnt["ws"] += 16
        E("gpsimd", partial(_dma, out=btim.ap().rearrange("p (k n) -> p k n", n=NST),
                            in_=bim_r, sem=q_ws, slow=True))
        cnt["ws"] += 16
        SMALLS = cnt["ws"]
        E("vector", partial(_memset, ap=ones[:, :], v=1.0, sem=q_v))
        cnt["v"] += 1
        ONES_DONE = cnt["v"]

        grank = {}
        drank = {}
        cast_list = []
        for q_ in range(4):
            for pr_ in range(8):
                cast_list.append(("g", q_, pr_, 0))
                cast_list.append(("g", q_, pr_, 1))
            for dp_ in range(8):
                for is_ in range(2):
                    cast_list.append(("d", q_, dp_, is_))
        pend = []
        for bi, ent in enumerate(cast_list):
            qj = q_cast4[bi % 4]
            if ent[0] == "g":
                _, q_, pr_, half_ = ent
                obase = (64 if half_ else 0) + 16 * q_ + 2 * pr_
                s_ = (32 if half_ else 0) + q_ * 8 + pr_
                grank[(q_, pr_, half_)] = bi + 1
                E("gpsimd", partial(
                    _dma,
                    out=d_wgub.ap()[:, s_ * 4096:(s_ + 1) * 4096]
                        .rearrange("p (k o) -> p k o", o=256),
                    in_=wgu_r[:, :, obase * 128:obase * 128 + 256], sem=qj))
            else:
                _, q_, dp_, is_ = ent
                ic0 = 16 * q_ + 8 * is_
                sd_ = q_ * 16 + dp_ * 2 + is_
                drank[(q_, dp_, is_)] = bi + 1
                E("gpsimd", partial(
                    _dma,
                    out=d_wdb.ap()[:, sd_ * 2048:(sd_ + 1) * 2048]
                        .rearrange("p (i d) -> p i d", d=256),
                    in_=wd_r[:, ic0:ic0 + 8, dp_ * 256:(dp_ + 1) * 256], sem=qj))
            cnt["cast%d" % (bi % 4)] += 16
            pend.append((qj, cnt["cast%d" % (bi % 4)]))
            if len(pend) >= 4:
                qk, vk = pend.pop(0)
                E("gpsimd", partial(_wait, sem=qk, v=vk))
                E("gpsimd", partial(lambda e, s: e.sem_inc(s, 1), s=q_cp))
                cnt["cp"] += 1
        while pend:
            qk, vk = pend.pop(0)
            E("gpsimd", partial(_wait, sem=qk, v=vk))
            E("gpsimd", partial(lambda e, s: e.sem_inc(s, 1), s=q_cp))
            cnt["cp"] += 1

        prepv = {}

        def emit_prep():
            W("scalar", "ws", SMALLS)
            E("scalar", partial(_act, out=col("t0"), in_=col("lre"), func=AF.Exp))
            E("scalar", partial(_act, out=col("sp"), in_=col("t0"), func=AF.Ln, bias=1.0))
            E("scalar", partial(_act, out=col("delta"), in_=col("ldt"), func=AF.Exp, sem=q_s))
            cnt["s"] += 1
            S_PREP1 = cnt["s"]
            if PREP_SKIP_VEC:
                E("vector", partial(_memset, ap=col("vdone"), v=0.0, sem=q_v))
                cnt["v"] += 1
                prepv["V_PREP"] = cnt["v"]
                return
            W("vector", "s", S_PREP1)
            drain_mode[0] = True
            E("vector", partial(_ts, out=col("lamre"), a=col("sp"), s1=-1.0, s2=-0.0101,
                                op0=ALU.mult, op1=ALU.add))
            E("vector", partial(_tt, out=col("rho"), a=col("lamre"), b=col("delta"), op=ALU.mult))
            E("vector", partial(_tt, out=col("omega"), a=col("lim"), b=col("delta"), op=ALU.mult))
            PI = math.pi
            for _ in range(2):
                E("vector", partial(_ts, out=col("kp"), a=col("omega"), s1=PI, s2=2 * PI,
                                    op0=ALU.is_gt, op1=ALU.mult))
                E("vector", partial(_ts, out=col("km"), a=col("omega"), s1=-PI, s2=2 * PI,
                                    op0=ALU.is_lt, op1=ALU.mult))
                E("vector", partial(_tt, out=col("omega"), a=col("omega"), b=col("kp"), op=ALU.subtract))
                E("vector", partial(_tt, out=col("omega"), a=col("omega"), b=col("km"), op=ALU.add, sem=q_v))
                cnt["v"] += 1
            V_OMEGA = cnt["v"]
            W("scalar", "v", V_OMEGA)
            E("scalar", partial(_act, out=col("emag"), in_=col("rho"), func=AF.Exp))
            E("scalar", partial(_act, out=col("einv"), in_=col("rho"), func=AF.Exp, scale=-1.0))
            E("scalar", partial(_act, out=col("sh"), in_=col("omega"), func=AF.Sin, scale=0.5))
            E("scalar", partial(_act, out=col("sh2"), in_=col("sh"), func=AF.Square))
            E("scalar", partial(_act, out=col("t1"), in_=col("sh2"), func=AF.Ln, scale=-1.0, bias=1.0))
            E("scalar", partial(_act, out=col("chalf"), in_=col("t1"), func=AF.Exp, scale=0.5, sem=q_s))
            cnt["s"] += 1
            S_PREP2 = cnt["s"]
            W("vector", "s", S_PREP2)
            E("vector", partial(_ts, out=col("cos"), a=col("sh2"), s1=-2.0, s2=1.0,
                                op0=ALU.mult, op1=ALU.add))
            E("vector", partial(_tt, out=col("t2"), a=col("sh"), b=col("chalf"), op=ALU.mult))
            E("vector", partial(_ts, out=col("sin"), a=col("t2"), s1=2.0, s2=None, op0=ALU.mult, op1=None))
            E("vector", partial(_tt, out=col("ar"), a=col("emag"), b=col("cos"), op=ALU.mult))
            E("vector", partial(_tt, out=col("ai"), a=col("emag"), b=col("sin"), op=ALU.mult))
            E("vector", partial(_tt, out=col("ivr"), a=col("einv"), b=col("cos"), op=ALU.mult))
            E("vector", partial(_stt, out=col("ivi"), in0=col("einv"), sc=-1.0, in1=col("sin"),
                                op0=ALU.mult, op1=ALU.mult))
            E("vector", partial(_ts, out=col("x"), a=col("ar"), s1=-1.0, s2=None,
                                op0=ALU.add, op1=None))
            E("vector", partial(_tt, out=col("d1"), a=col("lamre"), b=col("lamre"), op=ALU.mult))
            E("vector", partial(_tt, out=col("d2"), a=col("lim"), b=col("lim"), op=ALU.mult))
            E("vector", partial(_tt, out=col("den"), a=col("d1"), b=col("d2"), op=ALU.add))
            E("vector", partial(_recip, out=col("rden"), in_=col("den")))
            E("vector", partial(_ts, out=col("rden"), a=col("rden"), s1=2.0, s2=None, op0=ALU.mult, op1=None))
            E("vector", partial(_tt, out=col("t3"), a=col("x"), b=col("lamre"), op=ALU.mult))
            E("vector", partial(_tt, out=col("t4"), a=col("ai"), b=col("lim"), op=ALU.mult))
            E("vector", partial(_tt, out=col("t5"), a=col("t3"), b=col("t4"), op=ALU.add))
            E("vector", partial(_tt, out=col("fr"), a=col("t5"), b=col("rden"), op=ALU.mult))
            E("vector", partial(_tt, out=col("t3"), a=col("ai"), b=col("lamre"), op=ALU.mult))
            E("vector", partial(_tt, out=col("t4"), a=col("x"), b=col("lim"), op=ALU.mult))
            E("vector", partial(_tt, out=col("t5"), a=col("t3"), b=col("t4"), op=ALU.subtract))
            E("vector", partial(_tt, out=col("fi"), a=col("t5"), b=col("rden"), op=ALU.mult))

            def ladder(Pr, Pi, zr0, zi0, zpre):
                E("vector", partial(_memset, ap=Pr[:, 0:1], v=1.0))
                E("vector", partial(_memset, ap=Pi[:, 0:1], v=0.0))
                E("vector", partial(_copy, out=col(zpre + "r0"), in_=zr0))
                E("vector", partial(_copy, out=col(zpre + "i0"), in_=zi0))
                for k in range(7):
                    w_ = 1 << k
                    zr = col(zpre + "r%d" % k)
                    zi = col(zpre + "i%d" % k)
                    E("vector", partial(_ts, out=tmpp[:, 0:w_], a=Pr[:, 0:w_], s1=zr, s2=None,
                                        op0=ALU.mult, op1=None))
                    E("vector", partial(_ts, out=tmpp[:, 128:128 + w_], a=Pi[:, 0:w_], s1=zi, s2=None,
                                        op0=ALU.mult, op1=None))
                    E("vector", partial(_tt, out=Pr[:, w_:2 * w_], a=tmpp[:, 0:w_],
                                        b=tmpp[:, 128:128 + w_], op=ALU.subtract))
                    E("vector", partial(_ts, out=tmpp[:, 0:w_], a=Pr[:, 0:w_], s1=zi, s2=None,
                                        op0=ALU.mult, op1=None))
                    E("vector", partial(_ts, out=tmpp[:, 128:128 + w_], a=Pi[:, 0:w_], s1=zr, s2=None,
                                        op0=ALU.mult, op1=None))
                    E("vector", partial(_tt, out=Pi[:, w_:2 * w_], a=tmpp[:, 0:w_],
                                        b=tmpp[:, 128:128 + w_], op=ALU.add))
                    nzr = col(zpre + "r%d" % (k + 1))
                    nzi = col(zpre + "i%d" % (k + 1))
                    E("vector", partial(_tt, out=col("t3"), a=zr, b=zr, op=ALU.mult))
                    E("vector", partial(_tt, out=col("t4"), a=zi, b=zi, op=ALU.mult))
                    E("vector", partial(_tt, out=nzr, a=col("t3"), b=col("t4"), op=ALU.subtract))
                    E("vector", partial(_tt, out=col("t3"), a=zr, b=zi, op=ALU.mult))
                    E("vector", partial(_ts, out=nzi, a=col("t3"), s1=2.0, s2=None, op0=ALU.mult, op1=None))

            ladder(pre, pim, col("ar"), col("ai"), "za")
            ladder(qre, qim, col("ivr"), col("ivi"), "zb")

            def csq(srcr, srci, dstr, dsti):
                E("vector", partial(_tt, out=col("t3"), a=srcr, b=srcr, op=ALU.mult))
                E("vector", partial(_tt, out=col("t4"), a=srci, b=srci, op=ALU.mult))
                E("vector", partial(_tt, out=dstr, a=col("t3"), b=col("t4"), op=ALU.subtract))
                E("vector", partial(_tt, out=col("t3"), a=srcr, b=srci, op=ALU.mult))
                E("vector", partial(_ts, out=dsti, a=col("t3"), s1=2.0, s2=None, op0=ALU.mult, op1=None))

            csq(col("zar7"), col("zai7"), col("a256r"), col("a256i"))
            csq(col("a256r"), col("a256i"), col("a512r"), col("a512i"))
            csq(col("a512r"), col("a512i"), col("akr"), col("aki"))
            csq(col("akr"), col("aki"), col("ak2r"), col("ak2i"))
            E("vector", partial(_memset, ap=col("vdone"), v=0.0, sem=q_v))
            cnt["v"] += 1
            V_PREP = cnt["v"]
            drain_mode[0] = False

            prepv["V_PREP"] = V_PREP

        fills = []
        wd_fills = []

        lnstate = {}

        def emit_ln_stats_chunk(k, ready_evt, pe=False):
            if k == 0:
                lnstate.clear()
                lnstate["sxx_v"] = {}
                lnstate["pe_t"] = {}
                lnstate["pe"] = pe
            W("scalar", ready_evt[0], ready_evt[1])
            if k >= 1:
                if pe:
                    W("scalar", "t", lnstate["pe_t"][k - 1])
                else:
                    W("scalar", "v", lnstate["sxx_v"][k - 1])
            E("scalar", partial(_act, out=scr2[:, :],
                                in_=stream[:, k * 1024:(k + 1) * 1024],
                                func=AF.Square, sem=q_s))
            cnt["s"] += 1
            s_sq = cnt["s"]
            if pe:
                W("tensor", ready_evt[0], ready_evt[1])
                for tb in range(2):
                    E("tensor", partial(_mm, out=ps0[0:1, tb * TB:(tb + 1) * TB],
                                        l=ones[:, 0:1],
                                        r=stream[:, k * 1024 + tb * TB:k * 1024 + (tb + 1) * TB],
                                        st=(k == 0), sp=(k == KC - 1)))
                W("tensor", "s", s_sq)
                for tb in range(2):
                    E("tensor", partial(_mm, out=ps1[0:1, tb * TB:(tb + 1) * TB],
                                        l=ones[:, 0:1], r=scr2[:, tb * TB:(tb + 1) * TB],
                                        st=(k == 0), sp=(k == KC - 1),
                                        sem=(q_t if tb == 1 else None)))
                cnt["t"] += 1
                lnstate["pe_t"][k] = cnt["t"]
                return
            W("vector", ready_evt[0], ready_evt[1])
            if k == 0:
                E("vector", partial(_copy, out=scr0[:, :], in_=stream[:, 0:1024]))
            else:
                E("vector", partial(_tt, out=scr0[:, :], a=scr0[:, :],
                                    b=stream[:, k * 1024:(k + 1) * 1024], op=ALU.add))
            W("vector", "s", s_sq)
            if k == 0:
                E("vector", partial(_copy, out=scr1[:, :], in_=scr2[:, :], sem=q_v))
            else:
                E("vector", partial(_tt, out=scr1[:, :], a=scr1[:, :],
                                    b=scr2[:, :], op=ALU.add, sem=q_v))
            cnt["v"] += 1
            lnstate["sxx_v"][k] = cnt["v"]

        def emit_ln_tail():
            if lnstate.get("pe"):
                W("vector", "t", lnstate["pe_t"][KC - 1])
            else:
                v_sxx = cnt["v"]
                W("tensor", "v", v_sxx)
                for tb in range(2):
                    E("tensor", partial(_mm, out=ps0[0:1, tb * TB:(tb + 1) * TB],
                                        l=ones[:, 0:1], r=scr0[:, tb * TB:(tb + 1) * TB],
                                        st=True, sp=True))
                    E("tensor", partial(_mm, out=ps1[0:1, tb * TB:(tb + 1) * TB],
                                        l=ones[:, 0:1], r=scr1[:, tb * TB:(tb + 1) * TB],
                                        st=True, sp=True, sem=(q_t if tb == 1 else None)))
                cnt["t"] += 1
                t_stat = cnt["t"]
                W("vector", "t", t_stat)
            E("vector", partial(_ts, out=scr0[0:1, :], a=ps0[0:1, :], s1=1.0 / D, s2=None,
                                op0=ALU.mult, op1=None))
            E("vector", partial(_ts, out=scr1[0:1, :], a=ps1[0:1, :], s1=1.0 / D, s2=None,
                                op0=ALU.mult, op1=None))
            E("vector", partial(_tt, out=scr2[0:1, :], a=scr0[0:1, :], b=scr0[0:1, :], op=ALU.mult))
            E("vector", partial(_tt, out=scr1[0:1, :], a=scr1[0:1, :], b=scr2[0:1, :], op=ALU.subtract))
            E("vector", partial(_ts, out=scr1[0:1, :], a=scr1[0:1, :], s1=1.0, s2=EPS_LN,
                                op0=ALU.mult, op1=ALU.add, sem=q_v))
            cnt["v"] += 1
            v_fin = cnt["v"]
            W("scalar", "v", v_fin)
            E("scalar", partial(_act, out=scr1[0:1, :], in_=scr1[0:1, :], func=AF.Ln))
            E("scalar", partial(_act, out=scr1[0:1, :], in_=scr1[0:1, :], func=AF.Exp,
                                scale=-0.5, sem=q_s))
            cnt["s"] += 1
            s_rstd = cnt["s"]
            W("vector", "s", s_rstd)
            E("vector", partial(_tt, out=scr2[0:1, :], a=scr0[0:1, :], b=scr1[0:1, :],
                                op=ALU.mult, sem=q_v))
            cnt["v"] += 1
            v_mr = cnt["v"]
            W("tensor", "v", v_mr)
            W("tensor", "s", cnt["s"])
            for tb in range(2):
                E("tensor", partial(_mm, out=ps2[:, tb * TB:(tb + 1) * TB], l=ones[0:1, 0:128],
                                    r=scr1[0:1, tb * TB:(tb + 1) * TB], st=True, sp=True))
                E("tensor", partial(_mm, out=ps3[:, tb * TB:(tb + 1) * TB], l=ones[0:1, 0:128],
                                    r=scr2[0:1, tb * TB:(tb + 1) * TB], st=True, sp=True,
                                    sem=(q_t if tb == 1 else None)))
            cnt["t"] += 1
            t_bc = cnt["t"]
            W("vector", "t", t_bc)
            evts = []
            for k in range(KC):
                E("vector", partial(_tt, out=scr2[:, :], a=stream[:, k * 1024:(k + 1) * 1024],
                                    b=ps2[:, :], op=ALU.mult))
                E("vector", partial(_tt, out=lnx[:, k * 1024:(k + 1) * 1024], a=scr2[:, :],
                                    b=ps3[:, :], op=ALU.subtract, sem=q_v))
                cnt["v"] += 1
                evts.append(("v", cnt["v"]))
            return evts

        def emit_ffn(lnx_evts, store_out=False, fuse_stats=False):
            nonlocal fills, wd_fills
            fill_n = len(fills)
            silu_hist = []
            mult_hist = []
            for q in range(FFN_NQ):
                W("tensor", "v", cnt["v"])
                for pr in range(FFN_NPAIR):
                    i0 = 16 * q + 2 * pr
                    for half, obase in ((0, i0), (1, 64 + i0)):
                        slot = len(fills) % 4
                        if len(fills) >= 4:
                            W("sync", "t", fills[len(fills) - 4][1])
                        s_ = (32 if half else 0) + q * 8 + pr
                        W("sync", "cp", grank[(q, pr, half)])
                        sname = "wb%d" % slot
                        E("sync", partial(
                            _dma,
                            out=wb[slot].ap()[:, 0:8] if PERF_NODMA else wb[slot].ap(),
                            in_=d_wgub.ap()[:, s_ * 4096:s_ * 4096 + 8] if PERF_NODMA
                                else d_wgub.ap()[:, s_ * 4096:(s_ + 1) * 4096],
                            sem=sems[sname]))
                        cnt[sname] += 16
                        fills.append([slot, None, (sname, cnt[sname])])
                    gslot = fills[-2][0]
                    uslot = fills[-1][0]
                    for sub in range(2):
                        i = i0 + sub
                        for half in range(2):
                            slot = gslot if half == 0 else uslot
                            fi = len(fills) - 2 + half
                            pgrp = ps[2 * (i % 2) + half]
                            W("tensor", fills[fi][2][0], fills[fi][2][1])
                            perk = (q == 0 and pr == 0)
                            if not perk:
                                W("tensor", lnx_evts[KC - 1][0], lnx_evts[KC - 1][1])
                            if len(silu_hist) >= 2 and half == 0:
                                W("tensor", "s", silu_hist[-2])
                            if len(mult_hist) >= 2 and half == 1:
                                W("tensor", "v", mult_hist[-2])
                            for k in range(KC):
                                if perk:
                                    W("tensor", lnx_evts[k][0], lnx_evts[k][1])
                                for tb in range(2):
                                    E("tensor", partial(
                                        _mm,
                                        out=pgrp[:, tb * TB:(tb + 1) * TB],
                                        l=wb[slot][:, k * 256 + sub * 128:k * 256 + (sub + 1) * 128],
                                        r=lnx[:, k * 1024 + tb * TB:k * 1024 + (tb + 1) * TB],
                                        st=(k == 0), sp=(k == KC - 1),
                                        sem=(q_t if (k == KC - 1 and tb == 1) else None)))
                            cnt["t"] += 1
                            fills[fi][1] = cnt["t"]
                            t_grp = cnt["t"]
                            if half == 0:
                                W("scalar", "t", t_grp)
                                if len(mult_hist) >= 2:
                                    W("scalar", "v", mult_hist[-2])
                                E("scalar", partial(_act, out=sgb[i % 2][:, 0:8] if PERF_NOEPI else sgb[i % 2][:, :],
                                                    in_=pgrp[:, 0:8] if PERF_NOEPI else pgrp[:, :],
                                                    func=AF.Silu, sem=q_s))
                                cnt["s"] += 1
                                silu_hist.append(cnt["s"])
                            else:
                                W("vector", "s", silu_hist[-1])
                                W("vector", "t", t_grp)
                                ic_local = i - 16 * q
                                if PERF_NOEPI:
                                    E("vector", partial(_tt, out=acts[:, ic_local * 1024:ic_local * 1024 + 8],
                                                        a=sgb[i % 2][:, 0:8], b=pgrp[:, 0:8],
                                                        op=ALU.mult, sem=q_v))
                                else:
                                    E("vector", partial(_tt, out=acts[:, ic_local * 1024:(ic_local + 1) * 1024],
                                                        a=sgb[i % 2][:, :], b=pgrp[:, :],
                                                        op=ALU.mult, sem=q_v))
                                cnt["v"] += 1
                                mult_hist.append(cnt["v"])
                v_act_done = cnt["v"]
                s_silu_done = cnt["s"]
                if not FFN_M2:
                    continue
                prev_evac = {}
                for dp in range(8):
                    par = dp % 2
                    pAB = (ps[2 * par], ps[2 * par + 1])
                    for iset in range(2):
                        slot = len(wd_fills) % 2
                        if len(wd_fills) >= 2:
                            W("sync", "t", wd_fills[len(wd_fills) - 2][1])
                        sd_ = q * 16 + dp * 2 + iset
                        W("sync", "cp", drank[(q, dp, iset)])
                        sname = "wd%d" % slot
                        E("sync", partial(
                            _dma,
                            out=wd[slot].ap()[:, 0:8] if PERF_NODMA else wd[slot].ap(),
                            in_=d_wdb.ap()[:, sd_ * 2048:sd_ * 2048 + 8] if PERF_NODMA
                                else d_wdb.ap()[:, sd_ * 2048:(sd_ + 1) * 2048],
                            sem=sems[sname]))
                        cnt[sname] += 16
                        wd_fills.append([slot, None, cnt[sname]])
                        W("tensor", sname, cnt[sname])
                        W("tensor", "v", v_act_done)
                        W("tensor", "s", s_silu_done)
                        if par in prev_evac:
                            W("tensor", "v", prev_evac[par])
                        for i4 in range(8):
                            for tb in range(2):
                                for dc in range(2):
                                    first = (iset == 0 and i4 == 0)
                                    last = (iset == 1 and i4 == 7)
                                    ic_local = 8 * iset + i4
                                    E("tensor", partial(
                                        _mm,
                                        out=pAB[dc][:, tb * TB:(tb + 1) * TB],
                                        l=wd[slot][:, i4 * 256 + dc * 128:i4 * 256 + (dc + 1) * 128],
                                        r=acts[:, ic_local * 1024 + tb * TB:ic_local * 1024 + (tb + 1) * TB],
                                        st=first, sp=last,
                                        sem=(q_t if (i4 == 7 and tb == 1 and dc == 1) else None)))
                        cnt["t"] += 1
                        wd_fills[-1][1] = cnt["t"]
                    t_dp = cnt["t"]
                    W("vector", "t", t_dp)
                    for dc in range(2):
                        kk = 2 * dp + dc
                        E("vector", partial(_stt, out=stream[:, kk * 1024:(kk + 1) * 1024],
                                            in0=pAB[dc][:, :], sc=0.5,
                                            in1=stream[:, kk * 1024:(kk + 1) * 1024],
                                            op0=ALU.mult, op1=ALU.add,
                                            sem=(q_v if dc == 1 else None)))
                    cnt["v"] += 1
                    prev_evac[par] = cnt["v"]
                    if fuse_stats and q == FFN_NQ - 1:
                        for kk_ in (2 * dp, 2 * dp + 1):
                            emit_ln_stats_chunk(kk_, ("v", prev_evac[par]))
                    if store_out and STREAM_STORE and q == FFN_NQ - 1:
                        W("scalar", "v", cnt["v"])
                        E("scalar", partial(
                            _dma, out=y_r[:, 2 * dp:2 * dp + 2, :],
                            in_=stream.ap()[:, 2 * dp * 1024:(2 * dp + 2) * 1024]
                                .rearrange("p (k t) -> p k t", t=TPC),
                            sem=q_st))
                        cnt["st"] += 16
            return cnt["v"]

        def emit_ssm(lnx_evts, fuse_stats=False):
            t_entry = cnt["t"]
            W("gpsimd", "t", t_entry)
            E("gpsimd", partial(_dma, out=ctre, in_=d_cre.ap(), sem=q_ws))
            cnt["ws"] += 16
            E("gpsimd", partial(_dma, out=ctim, in_=d_cim.ap(), sem=q_ws))
            cnt["ws"] += 16
            CT_WS = cnt["ws"]
            W("scalar", "ws", CT_WS)
            E("scalar", partial(_act, out=ctim, in_=ctim, func=AF.Copy, scale=-1.0, sem=q_s))
            cnt["s"] += 1
            CT_NEG = cnt["s"]
            drain_mode[0] = True
            W("tensor", "ws", SMALLS)
            W("tensor", "s", cnt["s"])
            for comp, (bt, pg) in enumerate(((btre, ps0), (btim, ps1))):
                for tb in range(2):
                    for k in range(KC):
                        if comp == 0 and tb == 0:
                            W("tensor", lnx_evts[k][0], lnx_evts[k][1])
                        E("tensor", partial(
                            _mm, out=pg[0:NST, tb * TB:(tb + 1) * TB],
                            l=bt[:, k * NST:(k + 1) * NST],
                            r=lnx[:, k * 1024 + tb * TB:k * 1024 + (tb + 1) * TB],
                            st=(k == 0), sp=(k == KC - 1),
                            sem=(q_t if (comp == 1 and tb == 1 and k == KC - 1) else None)))
            cnt["t"] += 1
            t_g = cnt["t"]
            W("vector", "t", t_g)
            W("vector", "v", prepv["V_PREP"])
            E("vector", partial(_ts, out=bure[:, :], a=ps0[0:NST, :], s1=col("fr"), s2=None,
                                op0=ALU.mult, op1=None))
            E("vector", partial(_stt, out=scr0[0:NST, :], in0=ps1[0:NST, :], sc=col("fi"),
                                in1=bure[:, :], op0=ALU.mult, op1=ALU.subtract))
            E("vector", partial(_ts, out=scr0[0:NST, :], a=scr0[0:NST, :], s1=-1.0, s2=None,
                                op0=ALU.mult, op1=None))
            E("vector", partial(_ts, out=buim[:, :], a=ps1[0:NST, :], s1=col("fr"), s2=None,
                                op0=ALU.mult, op1=None))
            E("vector", partial(_stt, out=buim[:, :], in0=ps0[0:NST, :], sc=col("fi"),
                                in1=buim[:, :], op0=ALU.mult, op1=ALU.add))
            E("vector", partial(_copy, out=bure[:, :], in_=scr0[0:NST, :], sem=q_v))
            cnt["v"] += 1

            drain_mode[0] = False

            def DR():
                E("vector", _drain)

            SLS = [slice(j * CH, (j + 1) * CH) for j in range(NCH_S)]
            for sl in SLS:
                E("vector", partial(_tt, out=scr2[0:NST, sl], a=qim[:, :], b=buim[:, sl], op=ALU.mult))
            DR()
            for sl in SLS:
                E("vector", partial(_tt, out=scr0[0:NST, sl], a=qre[:, :], b=bure[:, sl], op=ALU.mult))
            DR()
            for sl in SLS:
                E("vector", partial(_tt, out=scr0[0:NST, sl], a=scr0[0:NST, sl], b=scr2[0:NST, sl], op=ALU.subtract))
            for sl in SLS:
                E("vector", partial(_tt, out=scr2[0:NST, sl], a=qim[:, :], b=bure[:, sl], op=ALU.mult))
            for sl in SLS:
                E("vector", partial(_tt, out=scr1[0:NST, sl], a=qre[:, :], b=buim[:, sl], op=ALU.mult))
            DR()
            for sl in SLS:
                E("vector", partial(_tt, out=scr1[0:NST, sl], a=scr1[0:NST, sl], b=scr2[0:NST, sl], op=ALU.add))
            DR()
            for j, sl in enumerate(SLS):
                E("vector", partial(_scan, out=sre[:, sl], d0=scr0[0:NST, sl], d1=scr0[0:NST, sl],
                                    init=0.0, op0=ALU.add, op1=ALU.bypass))
            for j, sl in enumerate(SLS):
                E("vector", partial(_scan, out=sim[:, sl], d0=scr1[0:NST, sl], d1=scr1[0:NST, sl],
                                    init=0.0, op0=ALU.add, op1=ALU.bypass))
            DR()
            for j in range(NCH_S):
                lc = slice(j * CH + CH - 1, j * CH + CH)
                E("vector", partial(_tt, out=tmpp[:, 4 * j:4 * j + 1], a=pre[:, CH - 1:CH], b=sre[:, lc], op=ALU.mult))
                E("vector", partial(_tt, out=tmpp[:, 4 * j + 1:4 * j + 2], a=pim[:, CH - 1:CH], b=sim[:, lc], op=ALU.mult))
                E("vector", partial(_tt, out=tmpp[:, 4 * j + 2:4 * j + 3], a=pre[:, CH - 1:CH], b=sim[:, lc], op=ALU.mult))
                E("vector", partial(_tt, out=tmpp[:, 4 * j + 3:4 * j + 4], a=pim[:, CH - 1:CH], b=sre[:, lc], op=ALU.mult))
            DR()
            for j in range(NCH_S):
                E("vector", partial(_tt, out=col("clr%d" % j), a=tmpp[:, 4 * j:4 * j + 1],
                                    b=tmpp[:, 4 * j + 1:4 * j + 2], op=ALU.subtract))
                E("vector", partial(_tt, out=col("cli%d" % j), a=tmpp[:, 4 * j + 2:4 * j + 3],
                                    b=tmpp[:, 4 * j + 3:4 * j + 4], op=ALU.add))
            E("vector", partial(_memset, ap=col("glr0"), v=0.0))
            E("vector", partial(_memset, ap=col("gli0"), v=0.0))
            DR()
            for j in range(1, NCH_S):
                pj = j - 1
                E("vector", partial(_tt, out=col("t3"), a=col("zar7"), b=col("clr%d" % pj), op=ALU.mult))
                E("vector", partial(_tt, out=col("t4"), a=col("zai7"), b=col("cli%d" % pj), op=ALU.mult))
                E("vector", partial(_tt, out=col("t5"), a=col("zar7"), b=col("cli%d" % pj), op=ALU.mult))
                E("vector", partial(_tt, out=col("t6"), a=col("zai7"), b=col("clr%d" % pj), op=ALU.mult))
                E("vector", partial(_tt, out=col("g3"), a=col("ar"), b=col("clr%d" % pj), op=ALU.mult))
                E("vector", partial(_tt, out=col("g4"), a=col("ai"), b=col("cli%d" % pj), op=ALU.mult))
                E("vector", partial(_tt, out=col("g5"), a=col("ar"), b=col("cli%d" % pj), op=ALU.mult))
                E("vector", partial(_tt, out=col("g6"), a=col("ai"), b=col("clr%d" % pj), op=ALU.mult))
                DR()
                E("vector", partial(_tt, out=col("t3"), a=col("t3"), b=col("t4"), op=ALU.subtract))
                E("vector", partial(_tt, out=col("t5"), a=col("t5"), b=col("t6"), op=ALU.add))
                E("vector", partial(_tt, out=col("glr%d" % j), a=col("g3"), b=col("g4"), op=ALU.subtract))
                E("vector", partial(_tt, out=col("gli%d" % j), a=col("g5"), b=col("g6"), op=ALU.add))
                DR()
                E("vector", partial(_tt, out=col("clr%d" % j), a=col("clr%d" % j), b=col("t3"), op=ALU.add))
                E("vector", partial(_tt, out=col("cli%d" % j), a=col("cli%d" % j), b=col("t5"), op=ALU.add))
                DR()
            E("vector", partial(_copy, out=col("scr_r"), in_=col("clr%d" % (NCH_S - 1))))
            E("vector", partial(_copy, out=col("scr_i"), in_=col("cli%d" % (NCH_S - 1)), sem=q_v))
            cnt["v"] += 1
            v_state = cnt["v"]
            E("vector", _drain)

            W("gpsimd", "v", v_state)
            E("gpsimd", partial(_dma, out=d_ccin[0:NST, 0:1], in_=col("scr_r"), sem=q_g, slow=True))
            cnt["g"] += 16
            E("gpsimd", partial(_dma, out=d_ccin[NST:2 * NST, 0:1], in_=col("scr_i"), sem=q_g, slow=True))
            cnt["g"] += 16
            W("gpsimd", "g", cnt["g"])
            if PERF_NOCC:
                E("gpsimd", partial(_dma, out=d_ccout.ap()[0:2 * NST, :], in_=d_ccin.ap(), sem=q_cc, n=16))
                cnt["cc"] += 16
            else:
                E("gpsimd", partial(_cc, cin=d_ccin.ap().opt(), cout=d_ccout.ap().opt(), sem=q_cc))
                cnt["cc"] += 1
            W("gpsimd", "cc", cnt["cc"])
            E("gpsimd", partial(_dma, out=sgre[:, :], in_=ccout_r[0:NST, :], sem=q_g, slow=True))
            cnt["g"] += 16
            E("gpsimd", partial(_dma, out=sgim[:, :], in_=ccout_r[NST:2 * NST, :], sem=q_g, slow=True))
            cnt["g"] += 16
            w_gather = cnt["g"]

            drain_mode[0] = True
            W("vector", "g", w_gather)
            W("vector", "in", cnt["in"])
            for m in range(3):
                E("vector", partial(_stt, out=tmpp[:, 64:68], in0=sgre[:, :], sc=1.0,
                                    in1=selsb[:, m * 4:(m + 1) * 4], op0=ALU.mult, op1=ALU.mult,
                                    accum=col("Tr%d" % m)))
                E("vector", partial(_stt, out=tmpp[:, 64:68], in0=sgim[:, :], sc=1.0,
                                    in1=selsb[:, m * 4:(m + 1) * 4], op0=ALU.mult, op1=ALU.mult,
                                    accum=col("Ti%d" % m)))
            E("vector", partial(_copy, out=col("Ir"), in_=col("Tr0")))
            E("vector", partial(_copy, out=col("Ii"), in_=col("Ti0")))
            for m, (wr, wi) in enumerate((("akr", "aki"), ("ak2r", "ak2i")), start=1):
                E("vector", partial(_tt, out=col("t3"), a=col(wr), b=col("Tr%d" % m), op=ALU.mult))
                E("vector", partial(_tt, out=col("t4"), a=col(wi), b=col("Ti%d" % m), op=ALU.mult))
                E("vector", partial(_tt, out=col("t3"), a=col("t3"), b=col("t4"), op=ALU.subtract))
                E("vector", partial(_tt, out=col("Ir"), a=col("Ir"), b=col("t3"), op=ALU.add))
                E("vector", partial(_tt, out=col("t3"), a=col(wr), b=col("Ti%d" % m), op=ALU.mult))
                E("vector", partial(_tt, out=col("t4"), a=col(wi), b=col("Tr%d" % m), op=ALU.mult))
                E("vector", partial(_tt, out=col("t3"), a=col("t3"), b=col("t4"), op=ALU.add))
                E("vector", partial(_tt, out=col("Ii"), a=col("Ii"), b=col("t3"), op=ALU.add))
            drain_mode[0] = False
            E("vector", partial(_tt, out=col("t3"), a=col("ar"), b=col("Ir"), op=ALU.mult))
            E("vector", partial(_tt, out=col("t4"), a=col("ai"), b=col("Ii"), op=ALU.mult))
            E("vector", partial(_tt, out=col("t5"), a=col("ar"), b=col("Ii"), op=ALU.mult))
            E("vector", partial(_tt, out=col("t6"), a=col("ai"), b=col("Ir"), op=ALU.mult))
            DR()
            E("vector", partial(_tt, out=col("hr0"), a=col("t3"), b=col("t4"), op=ALU.subtract))
            E("vector", partial(_tt, out=col("hi0"), a=col("t5"), b=col("t6"), op=ALU.add))
            DR()
            for j in range(1, NCH_S):
                pj = j - 1
                E("vector", partial(_tt, out=col("t3"), a=col("zar7"), b=col("hr%d" % pj), op=ALU.mult))
                E("vector", partial(_tt, out=col("t4"), a=col("zai7"), b=col("hi%d" % pj), op=ALU.mult))
                E("vector", partial(_tt, out=col("t5"), a=col("zar7"), b=col("hi%d" % pj), op=ALU.mult))
                E("vector", partial(_tt, out=col("t6"), a=col("zai7"), b=col("hr%d" % pj), op=ALU.mult))
                DR()
                E("vector", partial(_tt, out=col("hr%d" % j), a=col("t3"), b=col("t4"), op=ALU.subtract))
                E("vector", partial(_tt, out=col("hi%d" % j), a=col("t5"), b=col("t6"), op=ALU.add))
                DR()
            for j in range(NCH_S):
                E("vector", partial(_tt, out=col("wr%d" % j), a=col("glr%d" % j), b=col("hr%d" % j), op=ALU.add))
                E("vector", partial(_tt, out=col("wi%d" % j), a=col("gli%d" % j), b=col("hi%d" % j), op=ALU.add))
            DR()
            for j, sl in enumerate(SLS):
                E("vector", partial(_ts, out=scr0[0:NST, sl], a=sre[:, sl], s1=col("wr%d" % j), s2=None,
                                    op0=ALU.add, op1=None))
            for j, sl in enumerate(SLS):
                E("vector", partial(_ts, out=scr1[0:NST, sl], a=sim[:, sl], s1=col("wi%d" % j), s2=None,
                                    op0=ALU.add, op1=None))
            DR()
            for sl in SLS:
                E("vector", partial(_tt, out=bure[:, sl], a=pre[:, :], b=scr0[0:NST, sl], op=ALU.mult))
            for sl in SLS:
                E("vector", partial(_tt, out=buim[:, sl], a=pre[:, :], b=scr1[0:NST, sl], op=ALU.mult))
            DR()
            for sl in SLS:
                E("vector", partial(_tt, out=scr2[0:NST, sl], a=pim[:, :], b=scr1[0:NST, sl], op=ALU.mult))
            DR()
            for sl in SLS:
                E("vector", partial(_tt, out=bure[:, sl], a=bure[:, sl], b=scr2[0:NST, sl], op=ALU.subtract))
            for sl in SLS:
                E("vector", partial(_tt, out=scr2[0:NST, sl], a=pim[:, :], b=scr0[0:NST, sl], op=ALU.mult))
            DR()
            for sl in SLS:
                E("vector", partial(_tt, out=buim[:, sl], a=buim[:, sl], b=scr2[0:NST, sl], op=ALU.add))
            DR()
            E("vector", partial(_copy, out=xbre, in_=bure[:, :]))
            E("vector", partial(_copy, out=xbim, in_=buim[:, :], sem=q_v))
            cnt["v"] += 1
            v_xs = cnt["v"]

            W("tensor", "ws", CT_WS)
            W("tensor", "s", CT_NEG)
            prev_pe = {}
            for kk in range(KC):
                pg = ps[2 + (kk % 2)]
                W("tensor", "v", v_xs)
                if kk >= 2:
                    W("tensor", "v", prev_pe[kk % 2])
                for tb in range(2):
                    E("tensor", partial(_mm, out=pg[:, tb * TB:(tb + 1) * TB],
                                        l=ctre[:, kk * 128:(kk + 1) * 128],
                                        r=xbre[:, tb * TB:(tb + 1) * TB], st=True, sp=False))
                    E("tensor", partial(_mm, out=pg[:, tb * TB:(tb + 1) * TB],
                                        l=ctim[:, kk * 128:(kk + 1) * 128],
                                        r=xbim[:, tb * TB:(tb + 1) * TB], st=False, sp=True,
                                        sem=(q_t if tb == 1 else None)))
                cnt["t"] += 1
                W("vector", "t", cnt["t"])
                E("vector", partial(_stt, out=pg[:, :], in0=lnx[:, kk * 1024:(kk + 1) * 1024],
                                    sc=dsb[:, kk:kk + 1], in1=pg[:, :], op0=ALU.mult, op1=ALU.add))
                E("vector", partial(_tt, out=stream[:, kk * 1024:(kk + 1) * 1024],
                                    a=stream[:, kk * 1024:(kk + 1) * 1024], b=pg[:, :],
                                    op=ALU.add, sem=q_v))
                cnt["v"] += 1
                prev_pe[kk % 2] = cnt["v"]
                if fuse_stats:
                    drain_mode[0] = False
                    emit_ln_stats_chunk(kk, ("v", cnt["v"]))
            v_ssm = cnt["v"]
            drain_mode[0] = False
            return v_ssm


        if STAGE >= 2:
            for k_ in range(KC):
                emit_ln_stats_chunk(k_, U_EVTS[k_])
            lnx1_evts = emit_ln_tail()
            emit_prep()
        else:
            emit_prep()
        if STAGE >= 3:
            emit_ffn(lnx1_evts, fuse_stats=(STAGE >= 4))
        if STAGE >= 4:
            lnx2_evts = emit_ln_tail()
            emit_ssm(lnx2_evts, fuse_stats=(STAGE >= 5))
        stored = False
        if STAGE >= 5:
            lnx3_evts = emit_ln_tail()
            emit_ffn(lnx3_evts, store_out=(FFN_NQ == 4 and FFN_M2))
            stored = FFN_NQ == 4 and FFN_M2 and STREAM_STORE


        if not stored:
            W("scalar", "v", cnt["v"])
            E("scalar", partial(_dma, out=y_r,
                              in_=stream.ap().rearrange("p (k t) -> p k t", t=TPC),
                              sem=q_st))
            cnt["st"] += 16
        if DEBUG_SSM:
            d_dbg = nc.dram_tensor("dbg", [128, 2048], F32, kind="ExternalOutput")
            dumps = [
                (d_dbg.ap()[0:32, 0:128], cols[:, 0:128]),
                (d_dbg.ap()[0:32, 128:256], pre[:, :]),
                (d_dbg.ap()[0:32, 256:384], pim[:, :]),
                (d_dbg.ap()[0:32, 384:512], qre[:, :]),
                (d_dbg.ap()[0:32, 512:640], qim[:, :]),
                (d_dbg.ap()[0:32, 640:648], sgre[:, :]),
                (d_dbg.ap()[0:32, 648:656], sgim[:, :]),
                (d_dbg.ap()[32:64, 0:1024], sre[:, :]),
                (d_dbg.ap()[32:64, 1024:2048], sim[:, :]),
                (d_dbg.ap()[64:96, 0:1024], bure[:, :]),
                (d_dbg.ap()[64:96, 1024:2048], buim[:, :]),
            ]
            for o_, i_ in dumps:
                E("scalar", partial(_dma, out=o_, in_=i_, sem=q_st))
                cnt["st"] += 16
        W("scalar", "st", cnt["st"])

        def flush(ops):
            def body(e):
                for f in ops:
                    f(e)
            return body

        block.sync(flush(prog["sync"]))
        block.gpsimd(flush(prog["gpsimd"]))
        block.tensor(flush(prog["tensor"]))
        block.vector(flush(prog["vector"]))
        block.scalar(flush(prog["scalar"]))

    return nc




_NC_CACHE = [None]


def kernel(L0_attn_u, L0_attn_Lambda_unconstrained_re, L0_attn_Lambda_im,
           L0_attn_B_re, L0_attn_B_im, L0_attn_C_re, L0_attn_C_im, L0_attn_D,
           L0_attn_log_Delta, L0_ffn_w_gate_up, L0_ffn_w_down):
    u = np.asarray(L0_attn_u, dtype=np.float32)
    wguT = np.ascontiguousarray(np.asarray(L0_ffn_w_gate_up, dtype=np.float32).T)
    wdT = np.ascontiguousarray(np.asarray(L0_ffn_w_down, dtype=np.float32).T)
    breT = np.ascontiguousarray(np.asarray(L0_attn_B_re, dtype=np.float32).T)
    bimT = np.ascontiguousarray(np.asarray(L0_attn_B_im, dtype=np.float32).T)
    creT = np.ascontiguousarray(np.asarray(L0_attn_C_re, dtype=np.float32).T)
    cimT = np.ascontiguousarray(np.asarray(L0_attn_C_im, dtype=np.float32).T)
    dvec = np.ascontiguousarray(np.asarray(L0_attn_D, dtype=np.float32).reshape(KC, 128))
    lre = np.ascontiguousarray(np.asarray(
        L0_attn_Lambda_unconstrained_re, dtype=np.float32).reshape(NST, 1))
    lim = np.ascontiguousarray(np.asarray(L0_attn_Lambda_im, dtype=np.float32).reshape(NST, 1))
    ldt = np.ascontiguousarray(np.asarray(L0_attn_log_Delta, dtype=np.float32)[:NST].reshape(NST, 1))

    if _NC_CACHE[0] is None:
        _NC_CACHE[0] = build_nc()
    nc = _NC_CACHE[0]

    in_maps = []
    for c in range(NCORES):
        b, s = divmod(c, 4)
        u_core = np.ascontiguousarray(u[b, s * TPC:(s + 1) * TPC, :].T)
        selm = np.zeros((NST, 12), np.float32)
        for m in range(3):
            jloc = s - 1 - m
            if jloc >= 0:
                selm[:, m * 4 + jloc] = 1.0
        in_maps.append({
            "u": u_core, "wgu": wguT, "wd": wdT,
            "bre": breT, "bim": bimT, "cre": creT, "cim": cimT,
            "dvec": dvec, "lre": lre, "lim": lim, "ldt": ldt, "selm": selm,
        })
    res = run_bass_kernel_spmd(nc, in_maps, core_ids=list(range(NCORES)))
    _NC_CACHE.append(res)
    out = np.empty((2, 4096, D), np.float32)
    for c in range(NCORES):
        b, s = divmod(c, 4)
        out[b, s * TPC:(s + 1) * TPC, :] = np.asarray(res.results[c]["y"]).T
    return out


# version 12 (final)
# speedup vs baseline: 1.0295x; 1.0236x over previous
import math
from functools import partial

import numpy as np
import concourse.bass as bass
import concourse.mybir as mybir
from concourse.bass_utils import run_bass_kernel_spmd

F32 = mybir.dt.float32
BF16 = mybir.dt.bfloat16
AF = mybir.ActivationFunctionType
ALU = mybir.AluOpType

NCORES = 8
TPC = 1024
D = 2048
KC = 16
ICH = 64
NST = 32
CH = 128
NCH_S = TPC // CH
EPS_LN = 1e-5
TB = 512


def _wait(e, sem, v):
    e.wait_ge(sem, v)


def _mm(e, out, l, r, st, sp, sem=None, n=1):
    i = e.matmul(out, l, r, start=st, stop=sp)
    if sem is not None:
        i.then_inc(sem, n)


def _dma(e, out, in_, sem=None, n=16, slow=False):
    kw = {}
    if slow:
        kw["allow_slow_non_contiguous"] = True
    i = e.dma_start(out=out, in_=in_, **kw)
    if sem is not None:
        i.then_inc(sem, n)


def _tt(e, out, a, b, op, sem=None):
    i = e.tensor_tensor(out, a, b, op)
    if sem is not None:
        i.then_inc(sem, 1)


def _ts(e, out, a, s1, s2, op0, op1, sem=None):
    if s2 is None:
        s2, op1 = 0.0, ALU.add
    i = e.tensor_scalar(out, a, s1, s2, op0, op1)
    if sem is not None:
        i.then_inc(sem, 1)


def _stt(e, out, in0, sc, in1, op0, op1, accum=None, sem=None):
    i = e.scalar_tensor_tensor(out, in0, sc, in1, op0, op1, accum_out=accum)
    if sem is not None:
        i.then_inc(sem, 1)


def _act(e, out, in_, func, bias=None, scale=None, sem=None):
    kw = {}
    if bias is not None:
        kw["bias"] = bias
    if scale is not None:
        kw["scale"] = scale
    i = e.activation(out, in_, func, **kw)
    if sem is not None:
        i.then_inc(sem, 1)


def _scan(e, out, d0, d1, init, op0, op1, sem=None):
    i = e.tensor_tensor_scan(out, d0, d1, init, op0, op1)
    if sem is not None:
        i.then_inc(sem, 1)


def _copy(e, out, in_, sem=None):
    i = e.tensor_copy(out, in_)
    if sem is not None:
        i.then_inc(sem, 1)


def _recip(e, out, in_, sem=None):
    i = e.reciprocal(out, in_)
    if sem is not None:
        i.then_inc(sem, 1)


def _memset(e, ap, v, sem=None):
    i = e.memset(ap, v)
    if sem is not None:
        i.then_inc(sem, 1)


def _cc(e, cin, cout, sem):
    e.collective_compute(
        "AllGather", ALU.bypass,
        replica_groups=[[0, 1, 2, 3], [4, 5, 6, 7]],
        ins=[cin], outs=[cout],
    ).then_inc(sem, 1)


STAGE = 5
FFN_NQ = 4
FFN_NPAIR = 8
FFN_M2 = True
DEBUG_SSM = False
STREAM_STORE = True
PERF_NOCC = False
PREP_AFTER_LN1 = True
PREP_SKIP_VEC = False
PERF_NOEPI = False
PERF_NODMA = False


def build_nc():
    nc = bass.Bass("TRN2", target_bir_lowering=False,
                   detect_race_conditions=False)

    d_u = nc.dram_tensor("u", [D, TPC], F32, kind="ExternalInput")
    d_wgu = nc.dram_tensor("wgu", [D, 8 * D], F32, kind="ExternalInput")
    d_wd = nc.dram_tensor("wd", [4 * D, D], F32, kind="ExternalInput")
    d_bre = nc.dram_tensor("bre", [D, NST], F32, kind="ExternalInput")
    d_bim = nc.dram_tensor("bim", [D, NST], F32, kind="ExternalInput")
    d_cre = nc.dram_tensor("cre", [NST, D], F32, kind="ExternalInput")
    d_cim = nc.dram_tensor("cim", [NST, D], F32, kind="ExternalInput")
    d_dv = nc.dram_tensor("dvec", [KC, 128], F32, kind="ExternalInput")
    d_lre = nc.dram_tensor("lre", [NST, 1], F32, kind="ExternalInput")
    d_lim = nc.dram_tensor("lim", [NST, 1], F32, kind="ExternalInput")
    d_ldt = nc.dram_tensor("ldt", [NST, 1], F32, kind="ExternalInput")
    d_sel = nc.dram_tensor("selm", [NST, 12], F32, kind="ExternalInput")
    d_y = nc.dram_tensor("y", [D, TPC], F32, kind="ExternalOutput")
    d_ccin = nc.dram_tensor("cc_in", [2 * NST, 1], F32)
    d_ccout = nc.dram_tensor("cc_out", [4 * 2 * NST, 1], F32)
    d_wgub = nc.dram_tensor("wgub", [128, 64 * 4096], BF16)
    d_wdb = nc.dram_tensor("wdb", [128, 64 * 2048], BF16)

    u_r = d_u.ap().rearrange("(k p) t -> p k t", p=128)
    y_r = d_y.ap().rearrange("(k p) t -> p k t", p=128)
    wgu_r = d_wgu.ap().rearrange("(k p) o -> p k o", p=128)
    wd_r = d_wd.ap().rearrange("(i p) d -> p i d", p=128)
    bre_r = d_bre.ap().rearrange("(k p) n -> p k n", p=128)
    bim_r = d_bim.ap().rearrange("(k p) n -> p k n", p=128)
    dv_r = d_dv.ap().rearrange("k p -> p k")
    ccout_r = d_ccout.ap().rearrange("(j n) one -> n (j one)", n=2 * NST)

    prog = {e: [] for e in ("sync", "gpsimd", "tensor", "vector", "scalar")}
    cnt = {"in": 0, "t": 0, "v": 0, "s": 0, "cc": 0, "st": 0, "ws": 0, "g": 0,
           "wb0": 0, "wb1": 0, "wb2": 0, "wb3": 0, "wd0": 0, "wd1": 0, "wd2": 0, "cp": 0,
           "u0": 0, "u1": 0, "u2": 0, "u3": 0,
           "cast0": 0, "cast1": 0, "cast2": 0, "cast3": 0}
    hw = {e: dict(cnt) for e in prog}

    drain_mode = [False]

    def _drain(e):
        e.drain()

    def E(eng, fn):
        prog[eng].append(fn)
        if eng == "vector" and drain_mode[0]:
            prog[eng].append(_drain)

    def W(eng, sem, val):
        if val > hw[eng][sem]:
            hw[eng][sem] = val
            E(eng, partial(_wait, sem=sems[sem], v=val))

    import contextlib
    with contextlib.ExitStack() as _stk:
        def _sb(name, shape, dt):
            return _stk.enter_context(nc.sbuf_tensor(name, shape, dt))

        def _ps(name):
            return _stk.enter_context(nc.psum_tensor(name, [128, 1024], F32))

        def _sem(name):
            return _stk.enter_context(nc.semaphore(name))

        stream = _sb("stream", [128, KC * TPC], F32)
        lnx = _sb("lnx", [128, KC * TPC], BF16)
        acts = _sb("acts", [128, 16 * TPC], BF16)
        wb0 = _sb("wb0", [128, 4096], BF16)
        wb1 = _sb("wb1", [128, 4096], BF16)
        wb2 = _sb("wb2", [128, 4096], BF16)
        wd0 = _sb("wd0", [128, 2048], BF16)
        wd1 = _sb("wd1", [128, 2048], BF16)
        scr0 = _sb("scr0", [128, 1024], F32)
        scr1 = _sb("scr1", [128, 1024], F32)
        scr2 = _sb("scr2", [128, 1024], F32)
        wd2 = _sb("wd2", [128, 2048], BF16)
        sre = _sb("sre", [NST, TPC], F32)
        sim = _sb("sim", [NST, TPC], F32)
        sgb0 = _sb("sgb0", [128, 1024], BF16)
        sgb1 = _sb("sgb1", [128, 1024], BF16)
        wb3 = _sb("wb3", [128, 4096], BF16)
        btre = _sb("btre", [128, KC * NST], BF16)
        btim = _sb("btim", [128, KC * NST], BF16)
        pre = _sb("pre", [NST, CH], F32)
        pim = _sb("pim", [NST, CH], F32)
        qre = _sb("qre", [NST, CH], F32)
        qim = _sb("qim", [NST, CH], F32)
        cols = _sb("cols", [NST, 256], F32)
        tmpp = _sb("tmpp", [NST, 256], F32)
        sgre = _sb("sgre", [NST, 4], F32)
        sgim = _sb("sgim", [NST, 4], F32)
        selsb = _sb("selsb", [NST, 12], F32)
        dsb = _sb("dsb", [128, KC], F32)
        ones = _sb("ones", [128, 128], F32)
        ps0, ps1, ps2, ps3 = _ps("ps0"), _ps("ps1"), _ps("ps2"), _ps("ps3")
        q_in, q_t = _sem("s_in"), _sem("s_t")
        q_v, q_s, q_cc, q_st = _sem("s_v"), _sem("s_s"), _sem("s_cc"), _sem("s_st")
        q_ws = _sem("s_ws")
        q_wb = [_sem("s_wb%d" % i) for i in range(4)]
        q_wd = [_sem("s_wd%d" % i) for i in range(3)]
        q_g = _sem("s_g")
        q_cast4 = [_sem("s_cast%d" % i) for i in range(4)]
        q_cp = _sem("s_cp")
        q_u = [_sem("s_u%d" % i) for i in range(4)]
        block = _stk.enter_context(nc.Block())
        sems = {"in": q_in, "t": q_t, "v": q_v, "s": q_s,
                "cc": q_cc, "st": q_st, "ws": q_ws, "g": q_g,
                "cp": q_cp, "u0": q_u[0], "u1": q_u[1], "u2": q_u[2], "u3": q_u[3],
                "wb0": q_wb[0], "wb1": q_wb[1], "wb2": q_wb[2], "wb3": q_wb[3],
                "wd0": q_wd[0], "wd1": q_wd[1], "wd2": q_wd[2]}
        ps = [ps0, ps1, ps2, ps3]
        wb = [wb0, wb1, wb2, wb3]
        sgb = [sgb0, sgb1]
        xbre = acts[0:NST, 0:TPC]
        xbim = acts[0:NST, TPC:2 * TPC]
        ctre = acts[0:NST, 2 * TPC:2 * TPC + D]
        ctim = acts[0:NST, 2 * TPC + D:2 * TPC + 2 * D]
        bure = acts[0:NST, 6 * TPC:7 * TPC]
        buim = acts[0:NST, 7 * TPC:8 * TPC]
        wd = [wd0, wd1, wd2]
        scr = [scr0, scr1, scr2]

        colmap = {}

        def col(name):
            if name not in colmap:
                colmap[name] = len(colmap)
            i = colmap[name]
            return cols[:, i:i + 1]

        for g_ in range(4):
            E("sync", partial(_dma,
                              out=stream.ap()[:, g_ * 4096:(g_ + 1) * 4096]
                                  .rearrange("p (k t) -> p k t", t=TPC),
                              in_=u_r[:, 4 * g_:4 * g_ + 4, :], sem=q_u[g_]))
            cnt["u%d" % g_] += 16
        U_EVTS = [("u%d" % (k // 4), 16) for k in range(KC)]
        E("gpsimd", partial(_dma, out=selsb[:, :], in_=d_sel.ap(), sem=q_ws))
        cnt["ws"] += 16
        E("gpsimd", partial(_dma, out=col("lre"), in_=d_lre.ap(), sem=q_ws, slow=True))
        cnt["ws"] += 16
        E("gpsimd", partial(_dma, out=col("lim"), in_=d_lim.ap(), sem=q_ws, slow=True))
        cnt["ws"] += 16
        E("gpsimd", partial(_dma, out=col("ldt"), in_=d_ldt.ap(), sem=q_ws, slow=True))
        cnt["ws"] += 16
        E("gpsimd", partial(_dma, out=dsb[:, :], in_=dv_r, sem=q_ws, slow=True))
        cnt["ws"] += 16
        E("gpsimd", partial(_dma, out=btre.ap().rearrange("p (k n) -> p k n", n=NST),
                            in_=bre_r, sem=q_ws, slow=True))
        cnt["ws"] += 16
        E("gpsimd", partial(_dma, out=btim.ap().rearrange("p (k n) -> p k n", n=NST),
                            in_=bim_r, sem=q_ws, slow=True))
        cnt["ws"] += 16
        SMALLS = cnt["ws"]
        E("vector", partial(_memset, ap=ones[:, :], v=1.0, sem=q_v))
        cnt["v"] += 1
        ONES_DONE = cnt["v"]

        grank = {}
        drank = {}
        cast_list = []
        for q_ in range(4):
            for pr_ in range(8):
                cast_list.append(("g", q_, pr_, 0))
                cast_list.append(("g", q_, pr_, 1))
            for dp_ in range(8):
                for is_ in range(2):
                    cast_list.append(("d", q_, dp_, is_))
        pend = []
        for bi, ent in enumerate(cast_list):
            qj = q_cast4[bi % 4]
            if ent[0] == "g":
                _, q_, pr_, half_ = ent
                obase = (64 if half_ else 0) + 16 * q_ + 2 * pr_
                s_ = (32 if half_ else 0) + q_ * 8 + pr_
                grank[(q_, pr_, half_)] = bi + 1
                E("gpsimd", partial(
                    _dma,
                    out=d_wgub.ap()[:, s_ * 4096:(s_ + 1) * 4096]
                        .rearrange("p (k o) -> p k o", o=256),
                    in_=wgu_r[:, :, obase * 128:obase * 128 + 256], sem=qj))
            else:
                _, q_, dp_, is_ = ent
                ic0 = 16 * q_ + 8 * is_
                sd_ = q_ * 16 + dp_ * 2 + is_
                drank[(q_, dp_, is_)] = bi + 1
                E("gpsimd", partial(
                    _dma,
                    out=d_wdb.ap()[:, sd_ * 2048:(sd_ + 1) * 2048]
                        .rearrange("p (i d) -> p i d", d=256),
                    in_=wd_r[:, ic0:ic0 + 8, dp_ * 256:(dp_ + 1) * 256], sem=qj))
            cnt["cast%d" % (bi % 4)] += 16
            pend.append((qj, cnt["cast%d" % (bi % 4)]))
            if len(pend) >= 4:
                qk, vk = pend.pop(0)
                E("gpsimd", partial(_wait, sem=qk, v=vk))
                E("gpsimd", partial(lambda e, s: e.sem_inc(s, 1), s=q_cp))
                cnt["cp"] += 1
        while pend:
            qk, vk = pend.pop(0)
            E("gpsimd", partial(_wait, sem=qk, v=vk))
            E("gpsimd", partial(lambda e, s: e.sem_inc(s, 1), s=q_cp))
            cnt["cp"] += 1

        prepv = {}

        def emit_prep():
            W("scalar", "ws", SMALLS)
            E("scalar", partial(_act, out=col("t0"), in_=col("lre"), func=AF.Exp))
            E("scalar", partial(_act, out=col("sp"), in_=col("t0"), func=AF.Ln, bias=1.0))
            E("scalar", partial(_act, out=col("delta"), in_=col("ldt"), func=AF.Exp, sem=q_s))
            cnt["s"] += 1
            S_PREP1 = cnt["s"]
            if PREP_SKIP_VEC:
                E("vector", partial(_memset, ap=col("vdone"), v=0.0, sem=q_v))
                cnt["v"] += 1
                prepv["V_PREP"] = cnt["v"]
                return
            W("vector", "s", S_PREP1)
            drain_mode[0] = True
            E("vector", partial(_ts, out=col("lamre"), a=col("sp"), s1=-1.0, s2=-0.0101,
                                op0=ALU.mult, op1=ALU.add))
            E("vector", partial(_tt, out=col("rho"), a=col("lamre"), b=col("delta"), op=ALU.mult))
            E("vector", partial(_tt, out=col("omega"), a=col("lim"), b=col("delta"), op=ALU.mult))
            PI = math.pi
            for _ in range(2):
                E("vector", partial(_ts, out=col("kp"), a=col("omega"), s1=PI, s2=2 * PI,
                                    op0=ALU.is_gt, op1=ALU.mult))
                E("vector", partial(_ts, out=col("km"), a=col("omega"), s1=-PI, s2=2 * PI,
                                    op0=ALU.is_lt, op1=ALU.mult))
                E("vector", partial(_tt, out=col("omega"), a=col("omega"), b=col("kp"), op=ALU.subtract))
                E("vector", partial(_tt, out=col("omega"), a=col("omega"), b=col("km"), op=ALU.add, sem=q_v))
                cnt["v"] += 1
            V_OMEGA = cnt["v"]
            W("scalar", "v", V_OMEGA)
            E("scalar", partial(_act, out=col("emag"), in_=col("rho"), func=AF.Exp))
            E("scalar", partial(_act, out=col("einv"), in_=col("rho"), func=AF.Exp, scale=-1.0))
            E("scalar", partial(_act, out=col("sh"), in_=col("omega"), func=AF.Sin, scale=0.5))
            E("scalar", partial(_act, out=col("sh2"), in_=col("sh"), func=AF.Square))
            E("scalar", partial(_act, out=col("t1"), in_=col("sh2"), func=AF.Ln, scale=-1.0, bias=1.0))
            E("scalar", partial(_act, out=col("chalf"), in_=col("t1"), func=AF.Exp, scale=0.5, sem=q_s))
            cnt["s"] += 1
            S_PREP2 = cnt["s"]
            W("vector", "s", S_PREP2)
            E("vector", partial(_ts, out=col("cos"), a=col("sh2"), s1=-2.0, s2=1.0,
                                op0=ALU.mult, op1=ALU.add))
            E("vector", partial(_tt, out=col("t2"), a=col("sh"), b=col("chalf"), op=ALU.mult))
            E("vector", partial(_ts, out=col("sin"), a=col("t2"), s1=2.0, s2=None, op0=ALU.mult, op1=None))
            E("vector", partial(_tt, out=col("ar"), a=col("emag"), b=col("cos"), op=ALU.mult))
            E("vector", partial(_tt, out=col("ai"), a=col("emag"), b=col("sin"), op=ALU.mult))
            E("vector", partial(_tt, out=col("ivr"), a=col("einv"), b=col("cos"), op=ALU.mult))
            E("vector", partial(_stt, out=col("ivi"), in0=col("einv"), sc=-1.0, in1=col("sin"),
                                op0=ALU.mult, op1=ALU.mult))
            E("vector", partial(_ts, out=col("x"), a=col("ar"), s1=-1.0, s2=None,
                                op0=ALU.add, op1=None))
            E("vector", partial(_tt, out=col("d1"), a=col("lamre"), b=col("lamre"), op=ALU.mult))
            E("vector", partial(_tt, out=col("d2"), a=col("lim"), b=col("lim"), op=ALU.mult))
            E("vector", partial(_tt, out=col("den"), a=col("d1"), b=col("d2"), op=ALU.add))
            E("vector", partial(_recip, out=col("rden"), in_=col("den")))
            E("vector", partial(_ts, out=col("rden"), a=col("rden"), s1=2.0, s2=None, op0=ALU.mult, op1=None))
            E("vector", partial(_tt, out=col("t3"), a=col("x"), b=col("lamre"), op=ALU.mult))
            E("vector", partial(_tt, out=col("t4"), a=col("ai"), b=col("lim"), op=ALU.mult))
            E("vector", partial(_tt, out=col("t5"), a=col("t3"), b=col("t4"), op=ALU.add))
            E("vector", partial(_tt, out=col("fr"), a=col("t5"), b=col("rden"), op=ALU.mult))
            E("vector", partial(_tt, out=col("t3"), a=col("ai"), b=col("lamre"), op=ALU.mult))
            E("vector", partial(_tt, out=col("t4"), a=col("x"), b=col("lim"), op=ALU.mult))
            E("vector", partial(_tt, out=col("t5"), a=col("t3"), b=col("t4"), op=ALU.subtract))
            E("vector", partial(_tt, out=col("fi"), a=col("t5"), b=col("rden"), op=ALU.mult))

            def ladder(Pr, Pi, zr0, zi0, zpre):
                E("vector", partial(_memset, ap=Pr[:, 0:1], v=1.0))
                E("vector", partial(_memset, ap=Pi[:, 0:1], v=0.0))
                E("vector", partial(_copy, out=col(zpre + "r0"), in_=zr0))
                E("vector", partial(_copy, out=col(zpre + "i0"), in_=zi0))
                for k in range(7):
                    w_ = 1 << k
                    zr = col(zpre + "r%d" % k)
                    zi = col(zpre + "i%d" % k)
                    E("vector", partial(_ts, out=tmpp[:, 0:w_], a=Pr[:, 0:w_], s1=zr, s2=None,
                                        op0=ALU.mult, op1=None))
                    E("vector", partial(_ts, out=tmpp[:, 128:128 + w_], a=Pi[:, 0:w_], s1=zi, s2=None,
                                        op0=ALU.mult, op1=None))
                    E("vector", partial(_tt, out=Pr[:, w_:2 * w_], a=tmpp[:, 0:w_],
                                        b=tmpp[:, 128:128 + w_], op=ALU.subtract))
                    E("vector", partial(_ts, out=tmpp[:, 0:w_], a=Pr[:, 0:w_], s1=zi, s2=None,
                                        op0=ALU.mult, op1=None))
                    E("vector", partial(_ts, out=tmpp[:, 128:128 + w_], a=Pi[:, 0:w_], s1=zr, s2=None,
                                        op0=ALU.mult, op1=None))
                    E("vector", partial(_tt, out=Pi[:, w_:2 * w_], a=tmpp[:, 0:w_],
                                        b=tmpp[:, 128:128 + w_], op=ALU.add))
                    nzr = col(zpre + "r%d" % (k + 1))
                    nzi = col(zpre + "i%d" % (k + 1))
                    E("vector", partial(_tt, out=col("t3"), a=zr, b=zr, op=ALU.mult))
                    E("vector", partial(_tt, out=col("t4"), a=zi, b=zi, op=ALU.mult))
                    E("vector", partial(_tt, out=nzr, a=col("t3"), b=col("t4"), op=ALU.subtract))
                    E("vector", partial(_tt, out=col("t3"), a=zr, b=zi, op=ALU.mult))
                    E("vector", partial(_ts, out=nzi, a=col("t3"), s1=2.0, s2=None, op0=ALU.mult, op1=None))

            ladder(pre, pim, col("ar"), col("ai"), "za")
            ladder(qre, qim, col("ivr"), col("ivi"), "zb")

            def csq(srcr, srci, dstr, dsti):
                E("vector", partial(_tt, out=col("t3"), a=srcr, b=srcr, op=ALU.mult))
                E("vector", partial(_tt, out=col("t4"), a=srci, b=srci, op=ALU.mult))
                E("vector", partial(_tt, out=dstr, a=col("t3"), b=col("t4"), op=ALU.subtract))
                E("vector", partial(_tt, out=col("t3"), a=srcr, b=srci, op=ALU.mult))
                E("vector", partial(_ts, out=dsti, a=col("t3"), s1=2.0, s2=None, op0=ALU.mult, op1=None))

            csq(col("zar7"), col("zai7"), col("a256r"), col("a256i"))
            csq(col("a256r"), col("a256i"), col("a512r"), col("a512i"))
            csq(col("a512r"), col("a512i"), col("akr"), col("aki"))
            csq(col("akr"), col("aki"), col("ak2r"), col("ak2i"))
            E("vector", partial(_memset, ap=col("vdone"), v=0.0, sem=q_v))
            cnt["v"] += 1
            V_PREP = cnt["v"]
            drain_mode[0] = False

            prepv["V_PREP"] = V_PREP

        fills = []
        wd_fills = []

        lnstate = {}

        def emit_ln_stats_chunk(k, ready_evt, pe=False):
            if k == 0:
                lnstate.clear()
                lnstate["sxx_v"] = {}
                lnstate["pe_t"] = {}
                lnstate["pe"] = pe
            W("scalar", ready_evt[0], ready_evt[1])
            if k >= 1:
                if pe:
                    W("scalar", "t", lnstate["pe_t"][k - 1])
                else:
                    W("scalar", "v", lnstate["sxx_v"][k - 1])
            E("scalar", partial(_act, out=scr2[:, :],
                                in_=stream[:, k * 1024:(k + 1) * 1024],
                                func=AF.Square, sem=q_s))
            cnt["s"] += 1
            s_sq = cnt["s"]
            if pe:
                W("tensor", ready_evt[0], ready_evt[1])
                for tb in range(2):
                    E("tensor", partial(_mm, out=ps0[0:1, tb * TB:(tb + 1) * TB],
                                        l=ones[:, 0:1],
                                        r=stream[:, k * 1024 + tb * TB:k * 1024 + (tb + 1) * TB],
                                        st=(k == 0), sp=(k == KC - 1)))
                W("tensor", "s", s_sq)
                for tb in range(2):
                    E("tensor", partial(_mm, out=ps1[0:1, tb * TB:(tb + 1) * TB],
                                        l=ones[:, 0:1], r=scr2[:, tb * TB:(tb + 1) * TB],
                                        st=(k == 0), sp=(k == KC - 1),
                                        sem=(q_t if tb == 1 else None)))
                cnt["t"] += 1
                lnstate["pe_t"][k] = cnt["t"]
                return
            W("vector", ready_evt[0], ready_evt[1])
            if k == 0:
                E("vector", partial(_copy, out=scr0[:, :], in_=stream[:, 0:1024]))
            else:
                E("vector", partial(_tt, out=scr0[:, :], a=scr0[:, :],
                                    b=stream[:, k * 1024:(k + 1) * 1024], op=ALU.add))
            W("vector", "s", s_sq)
            if k == 0:
                E("vector", partial(_copy, out=scr1[:, :], in_=scr2[:, :], sem=q_v))
            else:
                E("vector", partial(_tt, out=scr1[:, :], a=scr1[:, :],
                                    b=scr2[:, :], op=ALU.add, sem=q_v))
            cnt["v"] += 1
            lnstate["sxx_v"][k] = cnt["v"]

        def emit_ln_tail():
            if lnstate.get("pe"):
                W("vector", "t", lnstate["pe_t"][KC - 1])
            else:
                v_sxx = cnt["v"]
                W("tensor", "v", v_sxx)
                for tb in range(2):
                    E("tensor", partial(_mm, out=ps0[0:1, tb * TB:(tb + 1) * TB],
                                        l=ones[:, 0:1], r=scr0[:, tb * TB:(tb + 1) * TB],
                                        st=True, sp=True))
                    E("tensor", partial(_mm, out=ps1[0:1, tb * TB:(tb + 1) * TB],
                                        l=ones[:, 0:1], r=scr1[:, tb * TB:(tb + 1) * TB],
                                        st=True, sp=True, sem=(q_t if tb == 1 else None)))
                cnt["t"] += 1
                t_stat = cnt["t"]
                W("vector", "t", t_stat)
            E("vector", partial(_ts, out=scr0[0:1, :], a=ps0[0:1, :], s1=1.0 / D, s2=None,
                                op0=ALU.mult, op1=None))
            E("vector", partial(_ts, out=scr1[0:1, :], a=ps1[0:1, :], s1=1.0 / D, s2=None,
                                op0=ALU.mult, op1=None))
            E("vector", partial(_tt, out=scr2[0:1, :], a=scr0[0:1, :], b=scr0[0:1, :], op=ALU.mult))
            E("vector", partial(_tt, out=scr1[0:1, :], a=scr1[0:1, :], b=scr2[0:1, :], op=ALU.subtract))
            E("vector", partial(_ts, out=scr1[0:1, :], a=scr1[0:1, :], s1=1.0, s2=EPS_LN,
                                op0=ALU.mult, op1=ALU.add, sem=q_v))
            cnt["v"] += 1
            v_fin = cnt["v"]
            W("scalar", "v", v_fin)
            E("scalar", partial(_act, out=scr1[0:1, :], in_=scr1[0:1, :], func=AF.Ln))
            E("scalar", partial(_act, out=scr1[0:1, :], in_=scr1[0:1, :], func=AF.Exp,
                                scale=-0.5, sem=q_s))
            cnt["s"] += 1
            s_rstd = cnt["s"]
            W("vector", "s", s_rstd)
            E("vector", partial(_tt, out=scr2[0:1, :], a=scr0[0:1, :], b=scr1[0:1, :],
                                op=ALU.mult, sem=q_v))
            cnt["v"] += 1
            v_mr = cnt["v"]
            W("tensor", "v", v_mr)
            W("tensor", "s", cnt["s"])
            for tb in range(2):
                E("tensor", partial(_mm, out=ps2[:, tb * TB:(tb + 1) * TB], l=ones[0:1, 0:128],
                                    r=scr1[0:1, tb * TB:(tb + 1) * TB], st=True, sp=True))
                E("tensor", partial(_mm, out=ps3[:, tb * TB:(tb + 1) * TB], l=ones[0:1, 0:128],
                                    r=scr2[0:1, tb * TB:(tb + 1) * TB], st=True, sp=True,
                                    sem=(q_t if tb == 1 else None)))
            cnt["t"] += 1
            t_bc = cnt["t"]
            W("vector", "t", t_bc)
            evts = []
            for k in range(KC):
                E("vector", partial(_tt, out=scr2[:, :], a=stream[:, k * 1024:(k + 1) * 1024],
                                    b=ps2[:, :], op=ALU.mult))
                E("vector", partial(_tt, out=lnx[:, k * 1024:(k + 1) * 1024], a=scr2[:, :],
                                    b=ps3[:, :], op=ALU.subtract, sem=q_v))
                cnt["v"] += 1
                evts.append(("v", cnt["v"]))
            return evts

        def emit_ffn(lnx_evts, store_out=False, fuse_stats=False):
            nonlocal fills, wd_fills
            fill_n = len(fills)
            silu_hist = []
            mult_hist = []
            for q in range(FFN_NQ):
                W("tensor", "v", cnt["v"])
                for pr in range(FFN_NPAIR):
                    i0 = 16 * q + 2 * pr
                    for half, obase in ((0, i0), (1, 64 + i0)):
                        slot = len(fills) % 4
                        if len(fills) >= 4:
                            W("sync", "t", fills[len(fills) - 4][1])
                        s_ = (32 if half else 0) + q * 8 + pr
                        W("sync", "cp", grank[(q, pr, half)])
                        sname = "wb%d" % slot
                        E("sync", partial(
                            _dma,
                            out=wb[slot].ap()[:, 0:8] if PERF_NODMA else wb[slot].ap(),
                            in_=d_wgub.ap()[:, s_ * 4096:s_ * 4096 + 8] if PERF_NODMA
                                else d_wgub.ap()[:, s_ * 4096:(s_ + 1) * 4096],
                            sem=sems[sname]))
                        cnt[sname] += 16
                        fills.append([slot, None, (sname, cnt[sname])])
                    gslot = fills[-2][0]
                    uslot = fills[-1][0]
                    for sub in range(2):
                        i = i0 + sub
                        for half in range(2):
                            slot = gslot if half == 0 else uslot
                            fi = len(fills) - 2 + half
                            pgrp = ps[2 * (i % 2) + half]
                            W("tensor", fills[fi][2][0], fills[fi][2][1])
                            perk = (q == 0 and pr == 0)
                            if not perk:
                                W("tensor", lnx_evts[KC - 1][0], lnx_evts[KC - 1][1])
                            if len(silu_hist) >= 2 and half == 0:
                                W("tensor", "s", silu_hist[-2])
                            if len(mult_hist) >= 2 and half == 1:
                                W("tensor", "v", mult_hist[-2])
                            for k in range(KC):
                                if perk:
                                    W("tensor", lnx_evts[k][0], lnx_evts[k][1])
                                for tb in range(2):
                                    E("tensor", partial(
                                        _mm,
                                        out=pgrp[:, tb * TB:(tb + 1) * TB],
                                        l=wb[slot][:, k * 256 + sub * 128:k * 256 + (sub + 1) * 128],
                                        r=lnx[:, k * 1024 + tb * TB:k * 1024 + (tb + 1) * TB],
                                        st=(k == 0), sp=(k == KC - 1),
                                        sem=(q_t if (k == KC - 1 and tb == 1) else None)))
                            cnt["t"] += 1
                            fills[fi][1] = cnt["t"]
                            t_grp = cnt["t"]
                            if half == 0:
                                W("scalar", "t", t_grp)
                                if len(mult_hist) >= 2:
                                    W("scalar", "v", mult_hist[-2])
                                E("scalar", partial(_act, out=sgb[i % 2][:, 0:8] if PERF_NOEPI else sgb[i % 2][:, :],
                                                    in_=pgrp[:, 0:8] if PERF_NOEPI else pgrp[:, :],
                                                    func=AF.Silu, sem=q_s))
                                cnt["s"] += 1
                                silu_hist.append(cnt["s"])
                            else:
                                W("vector", "s", silu_hist[-1])
                                W("vector", "t", t_grp)
                                ic_local = i - 16 * q
                                if PERF_NOEPI:
                                    E("vector", partial(_tt, out=acts[:, ic_local * 1024:ic_local * 1024 + 8],
                                                        a=sgb[i % 2][:, 0:8], b=pgrp[:, 0:8],
                                                        op=ALU.mult, sem=q_v))
                                else:
                                    E("vector", partial(_tt, out=acts[:, ic_local * 1024:(ic_local + 1) * 1024],
                                                        a=sgb[i % 2][:, :], b=pgrp[:, :],
                                                        op=ALU.mult, sem=q_v))
                                cnt["v"] += 1
                                mult_hist.append(cnt["v"])
                v_act_done = cnt["v"]
                s_silu_done = cnt["s"]
                if not FFN_M2:
                    continue
                prev_evac = {}
                for dp in range(8):
                    par = dp % 2
                    pAB = (ps[2 * par], ps[2 * par + 1])
                    for iset in range(2):
                        slot = len(wd_fills) % 3
                        if len(wd_fills) >= 3:
                            W("sync", "t", wd_fills[len(wd_fills) - 3][1])
                        sd_ = q * 16 + dp * 2 + iset
                        W("sync", "cp", drank[(q, dp, iset)])
                        sname = "wd%d" % slot
                        E("sync", partial(
                            _dma,
                            out=wd[slot].ap()[:, 0:8] if PERF_NODMA else wd[slot].ap(),
                            in_=d_wdb.ap()[:, sd_ * 2048:sd_ * 2048 + 8] if PERF_NODMA
                                else d_wdb.ap()[:, sd_ * 2048:(sd_ + 1) * 2048],
                            sem=sems[sname]))
                        cnt[sname] += 16
                        wd_fills.append([slot, None, cnt[sname]])
                        W("tensor", sname, cnt[sname])
                        W("tensor", "v", v_act_done)
                        W("tensor", "s", s_silu_done)
                        if par in prev_evac:
                            W("tensor", "v", prev_evac[par])
                        for i4 in range(8):
                            for tb in range(2):
                                for dc in range(2):
                                    first = (iset == 0 and i4 == 0)
                                    last = (iset == 1 and i4 == 7)
                                    ic_local = 8 * iset + i4
                                    E("tensor", partial(
                                        _mm,
                                        out=pAB[dc][:, tb * TB:(tb + 1) * TB],
                                        l=wd[slot][:, i4 * 256 + dc * 128:i4 * 256 + (dc + 1) * 128],
                                        r=acts[:, ic_local * 1024 + tb * TB:ic_local * 1024 + (tb + 1) * TB],
                                        st=first, sp=last,
                                        sem=(q_t if (i4 == 7 and tb == 1 and dc == 1) else None)))
                        cnt["t"] += 1
                        wd_fills[-1][1] = cnt["t"]
                    t_dp = cnt["t"]
                    W("vector", "t", t_dp)
                    for dc in range(2):
                        kk = 2 * dp + dc
                        E("vector", partial(_stt, out=stream[:, kk * 1024:(kk + 1) * 1024],
                                            in0=pAB[dc][:, :], sc=0.5,
                                            in1=stream[:, kk * 1024:(kk + 1) * 1024],
                                            op0=ALU.mult, op1=ALU.add,
                                            sem=(q_v if dc == 1 else None)))
                    cnt["v"] += 1
                    prev_evac[par] = cnt["v"]
                    if fuse_stats and q == FFN_NQ - 1:
                        for kk_ in (2 * dp, 2 * dp + 1):
                            emit_ln_stats_chunk(kk_, ("v", prev_evac[par]))
                    if store_out and STREAM_STORE and q == FFN_NQ - 1:
                        W("scalar", "v", cnt["v"])
                        E("scalar", partial(
                            _dma, out=y_r[:, 2 * dp:2 * dp + 2, :],
                            in_=stream.ap()[:, 2 * dp * 1024:(2 * dp + 2) * 1024]
                                .rearrange("p (k t) -> p k t", t=TPC),
                            sem=q_st))
                        cnt["st"] += 16
            return cnt["v"]

        def emit_ssm(lnx_evts, fuse_stats=False):
            t_entry = cnt["t"]
            W("gpsimd", "t", t_entry)
            E("gpsimd", partial(_dma, out=ctre, in_=d_cre.ap(), sem=q_ws))
            cnt["ws"] += 16
            E("gpsimd", partial(_dma, out=ctim, in_=d_cim.ap(), sem=q_ws))
            cnt["ws"] += 16
            CT_WS = cnt["ws"]
            W("scalar", "ws", CT_WS)
            E("scalar", partial(_act, out=ctim, in_=ctim, func=AF.Copy, scale=-1.0, sem=q_s))
            cnt["s"] += 1
            CT_NEG = cnt["s"]
            drain_mode[0] = True
            W("tensor", "ws", SMALLS)
            W("tensor", "s", cnt["s"])
            for comp, (bt, pg) in enumerate(((btre, ps0), (btim, ps1))):
                for tb in range(2):
                    for k in range(KC):
                        if comp == 0 and tb == 0:
                            W("tensor", lnx_evts[k][0], lnx_evts[k][1])
                        E("tensor", partial(
                            _mm, out=pg[0:NST, tb * TB:(tb + 1) * TB],
                            l=bt[:, k * NST:(k + 1) * NST],
                            r=lnx[:, k * 1024 + tb * TB:k * 1024 + (tb + 1) * TB],
                            st=(k == 0), sp=(k == KC - 1),
                            sem=(q_t if (comp == 1 and tb == 1 and k == KC - 1) else None)))
            cnt["t"] += 1
            t_g = cnt["t"]
            W("vector", "t", t_g)
            W("vector", "v", prepv["V_PREP"])
            E("vector", partial(_ts, out=bure[:, :], a=ps0[0:NST, :], s1=col("fr"), s2=None,
                                op0=ALU.mult, op1=None))
            E("vector", partial(_stt, out=scr0[0:NST, :], in0=ps1[0:NST, :], sc=col("fi"),
                                in1=bure[:, :], op0=ALU.mult, op1=ALU.subtract))
            E("vector", partial(_ts, out=scr0[0:NST, :], a=scr0[0:NST, :], s1=-1.0, s2=None,
                                op0=ALU.mult, op1=None))
            E("vector", partial(_ts, out=buim[:, :], a=ps1[0:NST, :], s1=col("fr"), s2=None,
                                op0=ALU.mult, op1=None))
            E("vector", partial(_stt, out=buim[:, :], in0=ps0[0:NST, :], sc=col("fi"),
                                in1=buim[:, :], op0=ALU.mult, op1=ALU.add))
            E("vector", partial(_copy, out=bure[:, :], in_=scr0[0:NST, :], sem=q_v))
            cnt["v"] += 1

            drain_mode[0] = False

            def DR():
                E("vector", _drain)

            SLS = [slice(j * CH, (j + 1) * CH) for j in range(NCH_S)]
            for sl in SLS:
                E("vector", partial(_tt, out=scr2[0:NST, sl], a=qim[:, :], b=buim[:, sl], op=ALU.mult))
            DR()
            for sl in SLS:
                E("vector", partial(_tt, out=scr0[0:NST, sl], a=qre[:, :], b=bure[:, sl], op=ALU.mult))
            DR()
            for sl in SLS:
                E("vector", partial(_tt, out=scr0[0:NST, sl], a=scr0[0:NST, sl], b=scr2[0:NST, sl], op=ALU.subtract))
            for sl in SLS:
                E("vector", partial(_tt, out=scr2[0:NST, sl], a=qim[:, :], b=bure[:, sl], op=ALU.mult))
            for sl in SLS:
                E("vector", partial(_tt, out=scr1[0:NST, sl], a=qre[:, :], b=buim[:, sl], op=ALU.mult))
            DR()
            for sl in SLS:
                E("vector", partial(_tt, out=scr1[0:NST, sl], a=scr1[0:NST, sl], b=scr2[0:NST, sl], op=ALU.add))
            DR()
            for j, sl in enumerate(SLS):
                E("vector", partial(_scan, out=sre[:, sl], d0=scr0[0:NST, sl], d1=scr0[0:NST, sl],
                                    init=0.0, op0=ALU.add, op1=ALU.bypass))
            for j, sl in enumerate(SLS):
                E("vector", partial(_scan, out=sim[:, sl], d0=scr1[0:NST, sl], d1=scr1[0:NST, sl],
                                    init=0.0, op0=ALU.add, op1=ALU.bypass))
            DR()
            for j in range(NCH_S):
                lc = slice(j * CH + CH - 1, j * CH + CH)
                E("vector", partial(_tt, out=tmpp[:, 4 * j:4 * j + 1], a=pre[:, CH - 1:CH], b=sre[:, lc], op=ALU.mult))
                E("vector", partial(_tt, out=tmpp[:, 4 * j + 1:4 * j + 2], a=pim[:, CH - 1:CH], b=sim[:, lc], op=ALU.mult))
                E("vector", partial(_tt, out=tmpp[:, 4 * j + 2:4 * j + 3], a=pre[:, CH - 1:CH], b=sim[:, lc], op=ALU.mult))
                E("vector", partial(_tt, out=tmpp[:, 4 * j + 3:4 * j + 4], a=pim[:, CH - 1:CH], b=sre[:, lc], op=ALU.mult))
            DR()
            for j in range(NCH_S):
                E("vector", partial(_tt, out=col("clr%d" % j), a=tmpp[:, 4 * j:4 * j + 1],
                                    b=tmpp[:, 4 * j + 1:4 * j + 2], op=ALU.subtract))
                E("vector", partial(_tt, out=col("cli%d" % j), a=tmpp[:, 4 * j + 2:4 * j + 3],
                                    b=tmpp[:, 4 * j + 3:4 * j + 4], op=ALU.add))
            E("vector", partial(_memset, ap=col("glr0"), v=0.0))
            E("vector", partial(_memset, ap=col("gli0"), v=0.0))
            DR()
            for j in range(1, NCH_S):
                pj = j - 1
                E("vector", partial(_tt, out=col("t3"), a=col("zar7"), b=col("clr%d" % pj), op=ALU.mult))
                E("vector", partial(_tt, out=col("t4"), a=col("zai7"), b=col("cli%d" % pj), op=ALU.mult))
                E("vector", partial(_tt, out=col("t5"), a=col("zar7"), b=col("cli%d" % pj), op=ALU.mult))
                E("vector", partial(_tt, out=col("t6"), a=col("zai7"), b=col("clr%d" % pj), op=ALU.mult))
                E("vector", partial(_tt, out=col("g3"), a=col("ar"), b=col("clr%d" % pj), op=ALU.mult))
                E("vector", partial(_tt, out=col("g4"), a=col("ai"), b=col("cli%d" % pj), op=ALU.mult))
                E("vector", partial(_tt, out=col("g5"), a=col("ar"), b=col("cli%d" % pj), op=ALU.mult))
                E("vector", partial(_tt, out=col("g6"), a=col("ai"), b=col("clr%d" % pj), op=ALU.mult))
                DR()
                E("vector", partial(_tt, out=col("t3"), a=col("t3"), b=col("t4"), op=ALU.subtract))
                E("vector", partial(_tt, out=col("t5"), a=col("t5"), b=col("t6"), op=ALU.add))
                E("vector", partial(_tt, out=col("glr%d" % j), a=col("g3"), b=col("g4"), op=ALU.subtract))
                E("vector", partial(_tt, out=col("gli%d" % j), a=col("g5"), b=col("g6"), op=ALU.add))
                DR()
                E("vector", partial(_tt, out=col("clr%d" % j), a=col("clr%d" % j), b=col("t3"), op=ALU.add))
                E("vector", partial(_tt, out=col("cli%d" % j), a=col("cli%d" % j), b=col("t5"), op=ALU.add))
                DR()
            E("vector", partial(_copy, out=col("scr_r"), in_=col("clr%d" % (NCH_S - 1))))
            E("vector", partial(_copy, out=col("scr_i"), in_=col("cli%d" % (NCH_S - 1)), sem=q_v))
            cnt["v"] += 1
            v_state = cnt["v"]
            E("vector", _drain)

            W("gpsimd", "v", v_state)
            E("gpsimd", partial(_dma, out=d_ccin[0:NST, 0:1], in_=col("scr_r"), sem=q_g, slow=True))
            cnt["g"] += 16
            E("gpsimd", partial(_dma, out=d_ccin[NST:2 * NST, 0:1], in_=col("scr_i"), sem=q_g, slow=True))
            cnt["g"] += 16
            W("gpsimd", "g", cnt["g"])
            if PERF_NOCC:
                E("gpsimd", partial(_dma, out=d_ccout.ap()[0:2 * NST, :], in_=d_ccin.ap(), sem=q_cc, n=16))
                cnt["cc"] += 16
            else:
                E("gpsimd", partial(_cc, cin=d_ccin.ap().opt(), cout=d_ccout.ap().opt(), sem=q_cc))
                cnt["cc"] += 1
            W("gpsimd", "cc", cnt["cc"])
            E("gpsimd", partial(_dma, out=sgre[:, :], in_=ccout_r[0:NST, :], sem=q_g, slow=True))
            cnt["g"] += 16
            E("gpsimd", partial(_dma, out=sgim[:, :], in_=ccout_r[NST:2 * NST, :], sem=q_g, slow=True))
            cnt["g"] += 16
            w_gather = cnt["g"]

            drain_mode[0] = True
            W("vector", "g", w_gather)
            W("vector", "in", cnt["in"])
            for m in range(3):
                E("vector", partial(_stt, out=tmpp[:, 64:68], in0=sgre[:, :], sc=1.0,
                                    in1=selsb[:, m * 4:(m + 1) * 4], op0=ALU.mult, op1=ALU.mult,
                                    accum=col("Tr%d" % m)))
                E("vector", partial(_stt, out=tmpp[:, 64:68], in0=sgim[:, :], sc=1.0,
                                    in1=selsb[:, m * 4:(m + 1) * 4], op0=ALU.mult, op1=ALU.mult,
                                    accum=col("Ti%d" % m)))
            E("vector", partial(_copy, out=col("Ir"), in_=col("Tr0")))
            E("vector", partial(_copy, out=col("Ii"), in_=col("Ti0")))
            for m, (wr, wi) in enumerate((("akr", "aki"), ("ak2r", "ak2i")), start=1):
                E("vector", partial(_tt, out=col("t3"), a=col(wr), b=col("Tr%d" % m), op=ALU.mult))
                E("vector", partial(_tt, out=col("t4"), a=col(wi), b=col("Ti%d" % m), op=ALU.mult))
                E("vector", partial(_tt, out=col("t3"), a=col("t3"), b=col("t4"), op=ALU.subtract))
                E("vector", partial(_tt, out=col("Ir"), a=col("Ir"), b=col("t3"), op=ALU.add))
                E("vector", partial(_tt, out=col("t3"), a=col(wr), b=col("Ti%d" % m), op=ALU.mult))
                E("vector", partial(_tt, out=col("t4"), a=col(wi), b=col("Tr%d" % m), op=ALU.mult))
                E("vector", partial(_tt, out=col("t3"), a=col("t3"), b=col("t4"), op=ALU.add))
                E("vector", partial(_tt, out=col("Ii"), a=col("Ii"), b=col("t3"), op=ALU.add))
            drain_mode[0] = False
            E("vector", partial(_tt, out=col("t3"), a=col("ar"), b=col("Ir"), op=ALU.mult))
            E("vector", partial(_tt, out=col("t4"), a=col("ai"), b=col("Ii"), op=ALU.mult))
            E("vector", partial(_tt, out=col("t5"), a=col("ar"), b=col("Ii"), op=ALU.mult))
            E("vector", partial(_tt, out=col("t6"), a=col("ai"), b=col("Ir"), op=ALU.mult))
            DR()
            E("vector", partial(_tt, out=col("hr0"), a=col("t3"), b=col("t4"), op=ALU.subtract))
            E("vector", partial(_tt, out=col("hi0"), a=col("t5"), b=col("t6"), op=ALU.add))
            DR()
            for j in range(1, NCH_S):
                pj = j - 1
                E("vector", partial(_tt, out=col("t3"), a=col("zar7"), b=col("hr%d" % pj), op=ALU.mult))
                E("vector", partial(_tt, out=col("t4"), a=col("zai7"), b=col("hi%d" % pj), op=ALU.mult))
                E("vector", partial(_tt, out=col("t5"), a=col("zar7"), b=col("hi%d" % pj), op=ALU.mult))
                E("vector", partial(_tt, out=col("t6"), a=col("zai7"), b=col("hr%d" % pj), op=ALU.mult))
                DR()
                E("vector", partial(_tt, out=col("hr%d" % j), a=col("t3"), b=col("t4"), op=ALU.subtract))
                E("vector", partial(_tt, out=col("hi%d" % j), a=col("t5"), b=col("t6"), op=ALU.add))
                DR()
            for j in range(NCH_S):
                E("vector", partial(_tt, out=col("wr%d" % j), a=col("glr%d" % j), b=col("hr%d" % j), op=ALU.add))
                E("vector", partial(_tt, out=col("wi%d" % j), a=col("gli%d" % j), b=col("hi%d" % j), op=ALU.add))
            DR()
            for j, sl in enumerate(SLS):
                E("vector", partial(_ts, out=scr0[0:NST, sl], a=sre[:, sl], s1=col("wr%d" % j), s2=None,
                                    op0=ALU.add, op1=None))
            for j, sl in enumerate(SLS):
                E("vector", partial(_ts, out=scr1[0:NST, sl], a=sim[:, sl], s1=col("wi%d" % j), s2=None,
                                    op0=ALU.add, op1=None))
            DR()
            for sl in SLS:
                E("vector", partial(_tt, out=bure[:, sl], a=pre[:, :], b=scr0[0:NST, sl], op=ALU.mult))
            for sl in SLS:
                E("vector", partial(_tt, out=buim[:, sl], a=pre[:, :], b=scr1[0:NST, sl], op=ALU.mult))
            DR()
            for sl in SLS:
                E("vector", partial(_tt, out=scr2[0:NST, sl], a=pim[:, :], b=scr1[0:NST, sl], op=ALU.mult))
            DR()
            for sl in SLS:
                E("vector", partial(_tt, out=bure[:, sl], a=bure[:, sl], b=scr2[0:NST, sl], op=ALU.subtract))
            for sl in SLS:
                E("vector", partial(_tt, out=scr2[0:NST, sl], a=pim[:, :], b=scr0[0:NST, sl], op=ALU.mult))
            DR()
            for sl in SLS:
                E("vector", partial(_tt, out=buim[:, sl], a=buim[:, sl], b=scr2[0:NST, sl], op=ALU.add))
            DR()
            E("vector", partial(_copy, out=xbre, in_=bure[:, :]))
            E("vector", partial(_copy, out=xbim, in_=buim[:, :], sem=q_v))
            cnt["v"] += 1
            v_xs = cnt["v"]

            W("tensor", "ws", CT_WS)
            W("tensor", "s", CT_NEG)
            prev_pe = {}
            for kk in range(KC):
                pg = ps[2 + (kk % 2)]
                W("tensor", "v", v_xs)
                if kk >= 2:
                    W("tensor", "v", prev_pe[kk % 2])
                for tb in range(2):
                    E("tensor", partial(_mm, out=pg[:, tb * TB:(tb + 1) * TB],
                                        l=ctre[:, kk * 128:(kk + 1) * 128],
                                        r=xbre[:, tb * TB:(tb + 1) * TB], st=True, sp=False))
                    E("tensor", partial(_mm, out=pg[:, tb * TB:(tb + 1) * TB],
                                        l=ctim[:, kk * 128:(kk + 1) * 128],
                                        r=xbim[:, tb * TB:(tb + 1) * TB], st=False, sp=True,
                                        sem=(q_t if tb == 1 else None)))
                cnt["t"] += 1
                W("vector", "t", cnt["t"])
                E("vector", partial(_stt, out=pg[:, :], in0=lnx[:, kk * 1024:(kk + 1) * 1024],
                                    sc=dsb[:, kk:kk + 1], in1=pg[:, :], op0=ALU.mult, op1=ALU.add))
                E("vector", partial(_tt, out=stream[:, kk * 1024:(kk + 1) * 1024],
                                    a=stream[:, kk * 1024:(kk + 1) * 1024], b=pg[:, :],
                                    op=ALU.add, sem=q_v))
                cnt["v"] += 1
                prev_pe[kk % 2] = cnt["v"]
                if fuse_stats:
                    drain_mode[0] = False
                    emit_ln_stats_chunk(kk, ("v", cnt["v"]))
            v_ssm = cnt["v"]
            drain_mode[0] = False
            return v_ssm


        if STAGE >= 2:
            for k_ in range(KC):
                emit_ln_stats_chunk(k_, U_EVTS[k_])
            lnx1_evts = emit_ln_tail()
            emit_prep()
        else:
            emit_prep()
        if STAGE >= 3:
            emit_ffn(lnx1_evts, fuse_stats=(STAGE >= 4))
        if STAGE >= 4:
            lnx2_evts = emit_ln_tail()
            emit_ssm(lnx2_evts, fuse_stats=(STAGE >= 5))
        stored = False
        if STAGE >= 5:
            lnx3_evts = emit_ln_tail()
            emit_ffn(lnx3_evts, store_out=(FFN_NQ == 4 and FFN_M2))
            stored = FFN_NQ == 4 and FFN_M2 and STREAM_STORE


        if not stored:
            W("scalar", "v", cnt["v"])
            E("scalar", partial(_dma, out=y_r,
                              in_=stream.ap().rearrange("p (k t) -> p k t", t=TPC),
                              sem=q_st))
            cnt["st"] += 16
        if DEBUG_SSM:
            d_dbg = nc.dram_tensor("dbg", [128, 2048], F32, kind="ExternalOutput")
            dumps = [
                (d_dbg.ap()[0:32, 0:128], cols[:, 0:128]),
                (d_dbg.ap()[0:32, 128:256], pre[:, :]),
                (d_dbg.ap()[0:32, 256:384], pim[:, :]),
                (d_dbg.ap()[0:32, 384:512], qre[:, :]),
                (d_dbg.ap()[0:32, 512:640], qim[:, :]),
                (d_dbg.ap()[0:32, 640:648], sgre[:, :]),
                (d_dbg.ap()[0:32, 648:656], sgim[:, :]),
                (d_dbg.ap()[32:64, 0:1024], sre[:, :]),
                (d_dbg.ap()[32:64, 1024:2048], sim[:, :]),
                (d_dbg.ap()[64:96, 0:1024], bure[:, :]),
                (d_dbg.ap()[64:96, 1024:2048], buim[:, :]),
            ]
            for o_, i_ in dumps:
                E("scalar", partial(_dma, out=o_, in_=i_, sem=q_st))
                cnt["st"] += 16
        W("scalar", "st", cnt["st"])

        def flush(ops):
            def body(e):
                for f in ops:
                    f(e)
            return body

        block.sync(flush(prog["sync"]))
        block.gpsimd(flush(prog["gpsimd"]))
        block.tensor(flush(prog["tensor"]))
        block.vector(flush(prog["vector"]))
        block.scalar(flush(prog["scalar"]))

    return nc




_NC_CACHE = [None]


def kernel(L0_attn_u, L0_attn_Lambda_unconstrained_re, L0_attn_Lambda_im,
           L0_attn_B_re, L0_attn_B_im, L0_attn_C_re, L0_attn_C_im, L0_attn_D,
           L0_attn_log_Delta, L0_ffn_w_gate_up, L0_ffn_w_down):
    u = np.asarray(L0_attn_u, dtype=np.float32)
    wguT = np.ascontiguousarray(np.asarray(L0_ffn_w_gate_up, dtype=np.float32).T)
    wdT = np.ascontiguousarray(np.asarray(L0_ffn_w_down, dtype=np.float32).T)
    breT = np.ascontiguousarray(np.asarray(L0_attn_B_re, dtype=np.float32).T)
    bimT = np.ascontiguousarray(np.asarray(L0_attn_B_im, dtype=np.float32).T)
    creT = np.ascontiguousarray(np.asarray(L0_attn_C_re, dtype=np.float32).T)
    cimT = np.ascontiguousarray(np.asarray(L0_attn_C_im, dtype=np.float32).T)
    dvec = np.ascontiguousarray(np.asarray(L0_attn_D, dtype=np.float32).reshape(KC, 128))
    lre = np.ascontiguousarray(np.asarray(
        L0_attn_Lambda_unconstrained_re, dtype=np.float32).reshape(NST, 1))
    lim = np.ascontiguousarray(np.asarray(L0_attn_Lambda_im, dtype=np.float32).reshape(NST, 1))
    ldt = np.ascontiguousarray(np.asarray(L0_attn_log_Delta, dtype=np.float32)[:NST].reshape(NST, 1))

    if _NC_CACHE[0] is None:
        _NC_CACHE[0] = build_nc()
    nc = _NC_CACHE[0]

    in_maps = []
    for c in range(NCORES):
        b, s = divmod(c, 4)
        u_core = np.ascontiguousarray(u[b, s * TPC:(s + 1) * TPC, :].T)
        selm = np.zeros((NST, 12), np.float32)
        for m in range(3):
            jloc = s - 1 - m
            if jloc >= 0:
                selm[:, m * 4 + jloc] = 1.0
        in_maps.append({
            "u": u_core, "wgu": wguT, "wd": wdT,
            "bre": breT, "bim": bimT, "cre": creT, "cim": cimT,
            "dvec": dvec, "lre": lre, "lim": lim, "ldt": ldt, "selm": selm,
        })
    res = run_bass_kernel_spmd(nc, in_maps, core_ids=list(range(NCORES)))
    _NC_CACHE.append(res)
    out = np.empty((2, 4096, D), np.float32)
    for c in range(NCORES):
        b, s = divmod(c, 4)
        out[b, s * TPC:(s + 1) * TPC, :] = np.asarray(res.results[c]["y"]).T
    return out
